# version 16
# baseline (speedup 1.0000x reference)
import contextlib
import numpy as np
import ml_dtypes
import concourse.bass as bass
import concourse.mybir as mybir
from concourse.bass_utils import run_bass_kernel_spmd

F32 = mybir.dt.float32
BF16 = mybir.dt.bfloat16
AX = mybir.AxisListType.X
ALU = mybir.AluOpType
ACT = mybir.ActivationFunctionType
BF = ml_dtypes.bfloat16

TAPS = False
PHASES = 4
D = 1024
S = 4096
TOK = 2048
NT = 16
NTF = 32
NDP = 8
HF = 4096
NEG = -1.0e30


class Sem:
    def __init__(self, h):
        self.h = h
        self.n = 0

    def inc(self, instr, k=1):
        instr.then_inc(self.h, k)
        self.n += k
        return self.n


def build():
    nc = bass.Bass("TRN2", target_bir_lowering=False, debug=False)

    def inp(name, shape, dt=F32):
        return nc.dram_tensor(name, shape, dt, kind="ExternalInput").ap()

    x_own = inp("x_own", [TOK, D])
    xT_full = inp("xT_full", [D, S])
    xT_own = inp("xT_own", [D, TOK])
    x_t0 = inp("x_t0", [128, D])

    wz = inp("wz", [D, D])
    wdbc = inp("wdbc", [D, 48])
    bz_bc = inp("bz_bc", [128, D])
    bdbc_bc = inp("bdbc_bc", [128, 48])
    alog = inp("alog", [16, 1])
    wout = inp("wout", [D, D])
    bout_bc = inp("bout_bc", [128, D])
    w1T = inp("w1T", [D, HF])
    b1c = inp("b1c", [128, HF // 128])
    w2T = inp("w2T", [HF, D])
    b2_bc = inp("b2_bc", [128, D])
    scale_bc = inp("scale_bc", [128, 1])

    wqkv = inp("wqkv", [D, 1536])
    wo = inp("wo", [D, D])
    gwT = inp("gwT", [128, 48])
    gb_bc = inp("gb_bc", [128, 6])
    ew1T = inp("ew1T", [6, D, D])
    ew2T = inp("ew2T", [6, D, D])
    ew3T = inp("ew3T", [6, D, D])
    fc1T = inp("fc1T", [D, D])
    fc2T = inp("fc2T", [D, D])
    fc3T = inp("fc3T", [D, D])

    ident = inp("ident", [128, 128], BF16)
    identf = inp("identf", [128, 128])
    cos_t = inp("cos_t", [128, NTF * 32], BF16)
    sin_t = inp("sin_t", [128, NTF * 32], BF16)
    mask1 = inp("mask1", [128, 128])
    mask2 = inp("mask2", [128, 512])
    iota6 = inp("iota6", [128, 6])
    own_mask = inp("own_mask", [128, 1])

    y_out = nc.dram_tensor("y_out", [TOK, D], F32, kind="ExternalOutput").ap()
    taps = {}
    if TAPS:
        for t in ("tap_x1", "tap_x2", "tap_x3"):
            taps[t] = nc.dram_tensor(t, [TOK, D], F32, kind="ExternalOutput").ap()
        taps["tap_dbc"] = nc.dram_tensor("tap_dbc", [16, 3 * S], F32, kind="ExternalOutput").ap()
        taps["tap_yc"] = nc.dram_tensor("tap_yc", [128, 34], F32, kind="ExternalOutput").ap()

    sp_x1 = nc.dram_tensor("sp_x1", [TOK + 128, D], F32).ap()
    sp_x2 = nc.dram_tensor("sp_x2", [TOK, D], F32).ap()
    w2T_bf = nc.dram_tensor("w2T_bf", [HF, D], BF16).ap()
    w1bf = nc.dram_tensor("w1bf", [D, HF], BF16).ap()
    ewbf = nc.dram_tensor("ewbf", [21 * 1024, 1024], BF16).ap()
    dbcT_hbm = nc.dram_tensor("dbcT_hbm", [16, 3 * S], F32).ap()
    cc_x2_in = nc.dram_tensor("cc_x2_in", [D, TOK], BF16)
    cc_x2_out = nc.dram_tensor("cc_x2_out", [2 * D, TOK], BF16)
    cc_y_in = nc.dram_tensor("cc_y_in", [S, 512], BF16)
    cc_y_out = nc.dram_tensor("cc_y_out", [2 * S, 512], BF16)
    RG = [[0, 1], [2, 3], [4, 5], [6, 7]]

    ctx = contextlib.ExitStack()
    E = ctx.enter_context
    with ctx:
        XT = E(nc.sbuf_tensor("XT", [128, NDP * TOK], BF16))
        stg = E(nc.sbuf_tensor("stg", [128, 4096], F32))
        wf0 = [E(nc.sbuf_tensor(f"wf0{i}", [128, D], F32)) for i in range(2)]
        wf1 = [E(nc.sbuf_tensor(f"wf1{i}", [128, D], F32)) for i in range(2)]
        xbft = E(nc.sbuf_tensor("xbft", [128, D], BF16))
        gTt = E(nc.sbuf_tensor("gTt", [128, D], BF16))
        stat = E(nc.sbuf_tensor("stat", [128, 16], F32))
        ycols = E(nc.sbuf_tensor("ycols", [128, 34], F32))
        idn = E(nc.sbuf_tensor("idn", [128, 128], BF16))
        idnf = E(nc.sbuf_tensor("idnf", [128, 128], F32))
        omsk = E(nc.sbuf_tensor("omsk", [128, 1], F32))
        x1t0 = E(nc.sbuf_tensor("x1t0", [128, D], F32))
        dmy = E(nc.sbuf_tensor("dmy", [128, D], BF16))

        pb = [E(nc.psum_tensor(f"pb{i}", [128, 512], F32)) for i in range(6)]
        ptb = E(nc.psum_tensor("ptb", [128, 1024], BF16))
        psm = E(nc.psum_tensor("psm", [128, 512], F32))

        sT = Sem(E(nc.semaphore("sT")))
        sV = Sem(E(nc.semaphore("sV")))
        sS = Sem(E(nc.semaphore("sS")))
        sC = Sem(E(nc.semaphore("sC")))
        sO = Sem(E(nc.semaphore("sO")))
        sK = Sem(E(nc.semaphore("sK")))
        sX = Sem(E(nc.semaphore("sX")))
        sGW = Sem(E(nc.semaphore("sGW")))
        sR = [Sem(E(nc.semaphore(f"sR{i}"))) for i in range(2)]
        sP = [Sem(E(nc.semaphore(f"sP{i}"))) for i in range(2)]
        sDB = [Sem(E(nc.semaphore(f"sDB{i}"))) for i in range(2)]
        sDB2 = Sem(E(nc.semaphore("sDB2")))
        sWS = [Sem(E(nc.semaphore(f"sWS{i}"))) for i in range(2)]
        sL = [Sem(E(nc.semaphore(f"sL{i}"))) for i in range(2)]
        sXS = [Sem(E(nc.semaphore(f"sXS{i}"))) for i in range(2)]
        sKB = Sem(E(nc.semaphore("sKB")))
        sY = Sem(E(nc.semaphore("sY")))
        sLT = Sem(E(nc.semaphore("sLT")))
        sCVl = [Sem(E(nc.semaphore(f"sCVl{i}"))) for i in range(2)]
        sCVs = [Sem(E(nc.semaphore(f"sCVs{i}"))) for i in range(2)]
        sCVv = Sem(E(nc.semaphore("sCVv")))
        sWL = [Sem(E(nc.semaphore(f"sWL{i}"))) for i in range(2)]
        sWC = Sem(E(nc.semaphore("sWC")))
        sWSt = [Sem(E(nc.semaphore(f"sWSt{i}"))) for i in range(2)]
        sW1L = Sem(E(nc.semaphore("sW1L")))
        sGAB = Sem(E(nc.semaphore("sGAB")))
        sGC = Sem(E(nc.semaphore("sGC")))
        sLN = Sem(E(nc.semaphore("sLN")))
        sX2 = [Sem(E(nc.semaphore(f"sX2{i}"))) for i in range(2)]
        block = E(nc.Block())

        paCtx = contextlib.ExitStack()
        with paCtx:
            EA = paCtx.enter_context
            regCw = EA(nc.sbuf_tensor("regCw", [128, 16768], BF16))
            bzs = EA(nc.sbuf_tensor("bzs", [128, D], F32))
            bouts = EA(nc.sbuf_tensor("bouts", [128, D], F32))
            bdbcs = EA(nc.sbuf_tensor("bdbcs", [128, 48], F32))
            dbc_ev = [EA(nc.sbuf_tensor(f"dbc_ev{i}", [128, 48], F32)) for i in range(2)]
            acol = EA(nc.sbuf_tensor("acol", [16, 1], F32))
            dbcT_ev = [EA(nc.sbuf_tensor(f"dbcT_ev{i}", [16, 384], F32)) for i in range(2)]
            x0t0b = EA(nc.sbuf_tensor("x0t0b", [128, NDP * 128], BF16))
            ones16 = EA(nc.sbuf_tensor("ones16", [16, 1], F32))

            WOFF_Z, WOFF_DBC, WOFF_OUT = 0, 8192, 8576

            def wzb(p):
                return regCw[:, WOFF_Z + 1024 * p:WOFF_Z + 1024 * (p + 1)]

            def wdbcb(p):
                return regCw[:, WOFF_DBC + 48 * p:WOFF_DBC + 48 * (p + 1)]

            def woutb(p):
                return regCw[:, WOFF_OUT + 1024 * p:WOFF_OUT + 1024 * (p + 1)]

            def p0_sync(sync):
                for dst, src in ((bzs, bz_bc), (bouts, bout_bc), (bdbcs, bdbc_bc),
                                 (idn, ident), (idnf, identf), (omsk, own_mask)):
                    sK.inc(sync.dma_start(out=dst[:], in_=src), 16)
                sK.inc(sync.dma_start(out=stat[0:16, 8:9], in_=alog), 16)
            block.sync(p0_sync)
            P0_CONST = sK.n

            xtbCtx = contextlib.ExitStack()
            with xtbCtx:
                xTbBuf = xtbCtx.enter_context(
                    nc.sbuf_tensor("xTbBuf", [128, NDP * 4096], BF16))

                def xTb(p):
                    return xTbBuf[:, 4096 * p:4096 * (p + 1)]

                S0 = sS.n

                def p0_sync_xt(sync):
                    for u in range(2 * NDP):
                        if u >= 2:
                            sync.wait_ge(sS.h, S0 + u - 1)
                        p, hf = u // 2, u % 2
                        sX2[u % 2].inc(sync.dma_start(
                            out=stg[:, 2048 * (u % 2):2048 * (u % 2 + 1)],
                            in_=xT_full[128 * p:128 * (p + 1),
                                        2048 * hf:2048 * (hf + 1)]), 16)
                block.sync(p0_sync_xt)

                def p0_scalar_xt(scalar):
                    for u in range(2 * NDP):
                        p, hf = u // 2, u % 2
                        scalar.wait_ge(sX2[u % 2].h, 16 * (u // 2 + 1))
                        sS.inc(scalar.copy(
                            out=xTb(p)[:, 2048 * hf:2048 * (hf + 1)],
                            in_=stg[:, 2048 * (u % 2):2048 * (u % 2 + 1)]))
                block.scalar(p0_scalar_xt)

                S1 = sS.n

                def p0_sync_w(sync):
                    for p in range(NDP):
                        sync.wait_ge(sS.h, S1 + 3 * p)
                        sX.inc(sync.dma_start(out=stg[:, 0:1024],
                                              in_=wz[128 * p:128 * (p + 1), :]), 16)
                        sX.inc(sync.dma_start(out=stg[:, 1024:1072],
                                              in_=wdbc[128 * p:128 * (p + 1), :]), 16)
                        sX.inc(sync.dma_start(out=stg[:, 1072:2096],
                                              in_=wout[128 * p:128 * (p + 1), :]), 16)
                block.sync(p0_sync_w)
                W_W = sX.n

                def p0_scalar_w(scalar):
                    for p in range(NDP):
                        scalar.wait_ge(sX.h, W_W - 48 * (NDP - 1 - p))
                        sS.inc(scalar.copy(out=wzb(p), in_=stg[:, 0:1024]))
                        sS.inc(scalar.copy(out=wdbcb(p), in_=stg[:, 1024:1072]))
                        sS.inc(scalar.copy(out=woutb(p), in_=stg[:, 1072:2096]))
                block.scalar(p0_scalar_w)
                P0_WCAST = sS.n

                S2 = sS.n

                def p0_sync_xto(sync):
                    for p in range(NDP):
                        sync.wait_ge(sS.h, S2 + p)
                        sX.inc(sync.dma_start(out=stg[:, 0:2048],
                                              in_=xT_own[128 * p:128 * (p + 1), :]), 16)
                block.sync(p0_sync_xto)
                W_XTO = sX.n

                def p0_scalar_xto(scalar):
                    for p in range(NDP):
                        scalar.wait_ge(sX.h, W_XTO - 16 * (NDP - 1 - p))
                        sS.inc(scalar.copy(out=XT[:, 2048 * p:2048 * (p + 1)],
                                           in_=stg[:, 0:2048]))
                block.scalar(p0_scalar_xto)
                P0_S = sS.n

                def p0_vec_t0(vector):
                    vector.wait_ge(sS.h, S0 + 2 * NDP)
                    xv = xTbBuf.rearrange("q (p c) -> q p c", c=4096)
                    ov = x0t0b.rearrange("q (p c) -> q p c", c=128)
                    sV.inc(vector.tensor_copy(out=ov[:, :, :], in_=xv[:, :, 0:128]))
                block.vector(p0_vec_t0)
                P0_V = sV.n

                TA0, VA0, SA0 = sT.n, sV.n, sS.n

                def pa1_tensor(tensor):
                    tensor.wait_ge(sS.h, P0_WCAST)
                    tensor.wait_ge(sK.h, P0_CONST)
                    for t in range(NTF):
                        bank = pb[4 + (t % 2)]
                        if t >= 2:
                            tensor.wait_ge(sV.h, VA0 + (t - 2) + 1)
                        for p in range(NDP):
                            ins = tensor.matmul(out=bank[:, 0:48],
                                                lhsT=xTb(p)[:, 128 * t:128 * (t + 1)],
                                                rhs=wdbcb(p), start=(p == 0),
                                                stop=(p == NDP - 1))
                        sT.inc(ins)
                        if t >= 1:
                            tensor.wait_ge(sV.h, VA0 + t)
                            if t >= 2:
                                tensor.wait_ge(sS.h, SA0 + (t - 1))
                            for j in range(3):
                                ins = tensor.transpose(
                                    out=pb[3][0:16, 128 * j:128 * (j + 1)],
                                    in_=dbc_ev[(t - 1) % 2][:, 16 * j:16 * (j + 1)],
                                    identity=idnf[:])
                            sT.inc(ins)
                    tensor.wait_ge(sV.h, VA0 + NTF)
                    tensor.wait_ge(sS.h, SA0 + (NTF - 1))
                    for j in range(3):
                        ins = tensor.transpose(out=pb[3][0:16, 128 * j:128 * (j + 1)],
                                               in_=dbc_ev[(NTF - 1) % 2][:, 16 * j:16 * (j + 1)],
                                               identity=idnf[:])
                    sT.inc(ins)
                block.tensor(pa1_tensor)

                def pa1_mm_n(t):
                    return TA0 + (1 if t == 0 else 2 * t)

                def pa1_tr_n(t):
                    return TA0 + (2 * (t + 1) + 1 if t < NTF - 1 else 2 * NTF)

                def pa1_vector(vector):
                    for t in range(NTF):
                        vector.wait_ge(sT.h, pa1_mm_n(t))
                        if t >= 2:
                            vector.wait_ge(sS.h, SA0 + (t - 2) + 1)
                        bank = pb[4 + (t % 2)]
                        sV.inc(vector.scalar_tensor_tensor(
                            out=dbc_ev[t % 2][:, :], in0=bank[:, 0:48], scalar=1.0,
                            in1=bdbcs[:, :], op0=ALU.mult, op1=ALU.add))
                block.vector(pa1_vector)

                def pa1_scalar(scalar):
                    for t in range(NTF):
                        scalar.wait_ge(sT.h, pa1_tr_n(t))
                        if t >= 2:
                            scalar.wait_ge(sDB[t % 2].h, 16 * (t // 2))
                        sS.inc(scalar.copy(out=dbcT_ev[t % 2][:, :],
                                           in_=pb[3][0:16, 0:384]))
                block.scalar(pa1_scalar)
                SA1 = sS.n

                def pa1_gpsimd(gpsimd):
                    ov = dbcT_hbm.rearrange("p (k c) -> p k c", k=3)
                    for t in range(NTF):
                        gpsimd.wait_ge(sS.h, SA0 + t + 1)
                        iv = dbcT_ev[t % 2].rearrange("p (k c) -> p k c", k=3)
                        sDB[t % 2].inc(gpsimd.dma_start(
                            out=ov[:, :, 128 * t:128 * (t + 1)], in_=iv[:, :, :]), 16)
                block.gpsimd(pa1_gpsimd)
            scanCtx = contextlib.ExitStack()
            with scanCtx:
                scanBuf = scanCtx.enter_context(
                    nc.sbuf_tensor("scanBuf", [16, 4 * 4096], F32))
                dB = scanBuf[:, 0:4096]
                dC = scanBuf[:, 4096:8192]
                dDp = scanBuf[:, 8192:12288]
                sc1 = scanBuf[:, 12288:16384]
                sc2 = stg[0:16, 0:4096]

                def pa2_gpsimd(gpsimd):
                    gpsimd.wait_ge(sDB[0].h, sDB[0].n)
                    gpsimd.wait_ge(sDB[1].h, sDB[1].n)
                    sDB2.inc(gpsimd.dma_start(out=scanBuf[:, 0:12288],
                                              in_=dbcT_hbm[:, :]), 16)
                block.gpsimd(pa2_gpsimd)

                def pa2_scalar(scalar):
                    scalar.wait_ge(sS.h, SA1)
                    scalar.wait_ge(sDB2.h, 16)
                    sS.inc(scalar.activation(out=acol[:, 0:1], in_=stat[0:16, 8:9],
                                             func=ACT.Exp))
                    sS.inc(scalar.activation(out=sc1[:, :], in_=dDp[:, :], func=ACT.Relu))
                    sS.inc(scalar.activation(out=sc2[:, :], in_=dDp[:, :], func=ACT.Abs))
                    scalar.wait_ge(sS.h, sS.n)
                    sS.inc(scalar.activation(out=sc2[:, :], in_=sc2[:, :], func=ACT.Exp, scale=-1.0))
                    scalar.wait_ge(sS.h, sS.n)
                    sS.inc(scalar.activation(out=sc2[:, :], in_=sc2[:, :], func=ACT.Ln, bias=1.0))
                block.scalar(pa2_scalar)
                SA2 = sS.n

                def pa2_vector(vector):
                    vector.wait_ge(sS.h, SA2)
                    sV.inc(vector.tensor_tensor(out=sc1[:, :], in0=sc1[:, :], in1=sc2[:, :],
                                                op=ALU.add))
                    sV.inc(vector.tensor_scalar(out=acol[:, 0:1], in0=acol[:, 0:1],
                                                scalar1=-1.0, scalar2=None, op0=ALU.mult))
                    vector.wait_ge(sV.h, sV.n)
                    sV.inc(vector.tensor_scalar(out=sc1[:, :], in0=sc1[:, :],
                                                scalar1=acol[0:16, 0:1], scalar2=None,
                                                op0=ALU.mult))
                block.vector(pa2_vector)
                VA2 = sV.n

                def pa2_scalar2(scalar):
                    scalar.wait_ge(sV.h, VA2)
                    sS.inc(scalar.activation(out=sc1[:, :], in_=sc1[:, :], func=ACT.Exp))
                block.scalar(pa2_scalar2)
                SA3 = sS.n

                def pa2_vector2(vector):
                    vector.wait_ge(sS.h, SA3)
                    sV.inc(vector.tensor_tensor_scan(out=sc2[:, :], data0=sc1[:, :],
                                                     data1=dB[:, :], initial=0.0,
                                                     op0=ALU.mult, op1=ALU.add))
                    vector.wait_ge(sV.h, sV.n)
                    sV.inc(vector.tensor_tensor(out=sc2[:, :], in0=sc2[:, :], in1=dC[:, :],
                                                op=ALU.mult))
                    sV.inc(vector.memset(ones16[:, 0:1], 1.0))
                    sV.inc(vector.memset(stat[:, 5:6], 1e-5))
                    sV.inc(vector.memset(ycols[:, :], 0.0))
                block.vector(pa2_vector2)
                VA3 = sV.n

                def pa2_tensor(tensor):
                    tensor.wait_ge(sV.h, VA3)
                    for t in range(NTF):
                        ins = tensor.matmul(out=psm[:, t:t + 1],
                                            lhsT=sc2[:, 128 * t:128 * (t + 1)],
                                            rhs=ones16[:, 0:1], start=True, stop=True,
                                            skip_group_check=True)
                    sT.inc(ins)
                block.tensor(pa2_tensor)
                TA2 = sT.n

                def pa2_scalar3(scalar):
                    scalar.wait_ge(sT.h, TA2)
                    sS.inc(scalar.copy(out=ycols[:, 0:NTF], in_=psm[:, 0:NTF]))
                    sS.inc(scalar.copy(out=ycols[:, 32:33], in_=psm[:, 0:1]))
                block.scalar(pa2_scalar3)
                SA4 = sS.n

                def pa2_vector3(vector):
                    vector.wait_ge(sS.h, SA4)
                    vector.wait_ge(sK.h, P0_CONST)
                    sV.inc(vector.tensor_sub(out=ycols[:, 0:NT], in0=ycols[:, 0:NT],
                                              in1=ycols[:, NT:2 * NT]))
                    vector.wait_ge(sV.h, sV.n)
                    sV.inc(vector.scalar_tensor_tensor(
                        out=ycols[:, 0:NT], in0=ycols[:, 0:NT], scalar=omsk[:, 0:1],
                        in1=ycols[:, NT:2 * NT], op0=ALU.mult, op1=ALU.add))
                block.vector(pa2_vector3)
                VA4 = sV.n

            TB0, VB0, SB0 = sT.n, sV.n, sS.n
            TPS, VPS, SPS = 4, 8, 8

            def t_n(s, k):
                return TB0 + TPS * s + k

            def v_n(s, k):
                return VB0 + VPS * s + k

            def s_n(s, k):
                return SB0 + SPS * s + k

            def lhsrc(s, p):
                if s == 0:
                    return x0t0b[:, 128 * p:128 * (p + 1)]
                return XT[:, 2048 * p + 128 * (s - 1):2048 * p + 128 * s]

            wcb = EA(nc.sbuf_tensor("wcb", [128, 4096], BF16))
            wcbv = wcb.rearrange("p (a d) -> p a d", d=1024)
            stgv3 = stg.rearrange("p (a d) -> p a d", d=1024)
            w2v_pa = w2T.rearrange("(a p) d -> p a d", p=128)
            w2bv_pa = w2T_bf.rearrange("(a p) d -> p a d", p=128)
            NWC = 32

            def pa3_gpsimd_conv(gpsimd):
                def _wstore(c):
                    if c < 16:
                        p_, h_ = c // 2, c % 2
                        ins = gpsimd.dma_start(
                            out=w1bf[128 * p_:128 * (p_ + 1), 2048 * h_:2048 * (h_ + 1)],
                            in_=wcb[:, 2048 * (c % 2):2048 * (c % 2 + 1)])
                    else:
                        r = c - 16
                        ins = gpsimd.dma_start(
                            out=w2bv_pa[:, 2 * r:2 * r + 2, :],
                            in_=wcbv[:, 2 * (c % 2):2 * (c % 2) + 2, :])
                    sWSt[c % 2].inc(ins, 16)
                for c in range(NWC):
                    if c >= 2:
                        gpsimd.wait_ge(sWC.h, c - 1)
                        _wstore(c - 2)
                    else:
                        gpsimd.wait_ge(sT.h, TA2)
                    if c < 16:
                        p_, h_ = c // 2, c % 2
                        ins = gpsimd.dma_start(
                            out=stg[:, 2048 * (c % 2):2048 * (c % 2 + 1)],
                            in_=w1T[128 * p_:128 * (p_ + 1), 2048 * h_:2048 * (h_ + 1)])
                    else:
                        r = c - 16
                        ins = gpsimd.dma_start(
                            out=stgv3[:, 2 * (c % 2):2 * (c % 2) + 2, :],
                            in_=w2v_pa[:, 2 * r:2 * r + 2, :])
                    sWL[c % 2].inc(ins, 16)
                for c in (NWC - 2, NWC - 1):
                    gpsimd.wait_ge(sWC.h, c + 1)
                    _wstore(c)
            block.gpsimd(pa3_gpsimd_conv)
            WST_A = [sWSt[0].n, sWSt[1].n]

            _wc_next = [0]

            def _wc_cast(scalar, budget):
                for _ in range(budget):
                    c = _wc_next[0]
                    if c >= NWC:
                        return
                    _wc_next[0] = c + 1
                    scalar.wait_ge(sWL[c % 2].h, 16 * (c // 2 + 1))
                    if c >= 2:
                        scalar.wait_ge(sWSt[c % 2].h, 16 * (c // 2))
                    ins = scalar.copy(
                        out=wcb[:, 2048 * (c % 2):2048 * (c % 2 + 1)],
                        in_=stg[:, 2048 * (c % 2):2048 * (c % 2 + 1)])
                    sWC.inc(ins)

            def pa3_sync(sync):
                sync.wait_ge(sV.h, VA4)
                for s in range(NT + 1):
                    if s >= 2:
                        sync.wait_ge(sV.h, v_n(s - 2, 5))
                    src = x_t0 if s == 0 else x_own[128 * (s - 1):128 * s, :]
                    sR[s % 2].inc(sync.dma_start(out=wf1[s % 2][:, :], in_=src), 16)
                    if s >= 1:
                        sync.wait_ge(sV.h, v_n(s - 1, 8))
                        sP[(s - 1) % 2].inc(sync.dma_start(
                            out=sp_x1[128 * (NT if s - 1 == 0 else s - 2):
                                      128 * (NT + 1 if s - 1 == 0 else s - 1), :],
                            in_=wf0[(s - 1) % 2][:, :]), 16)
                sync.wait_ge(sV.h, v_n(NT, 8))
                sP[NT % 2].inc(sync.dma_start(out=sp_x1[128 * (NT - 1):128 * NT, :],
                                              in_=wf0[NT % 2][:, :]), 16)
            block.sync(pa3_sync)
            PA3_P0, PA3_P1 = sP[0].n, sP[1].n
            PA3_R0, PA3_R1 = sR[0].n, sR[1].n

            def pa3_tensor(tensor):
                tensor.wait_ge(sV.h, VA4)
                for s in range(NT + 1):
                    A = (s % 3) * 2
                    if s >= 3:
                        tensor.wait_ge(sV.h, v_n(s - 3, 5))
                    for p in range(NDP):
                        for nch in range(2):
                            ins = tensor.matmul(
                                out=pb[A + nch][:, :],
                                lhsT=lhsrc(s, p),
                                rhs=wzb(p)[:, 512 * nch:512 * (nch + 1)],
                                start=(p == 0), stop=(p == NDP - 1))
                    sT.inc(ins)
                    tensor.wait_ge(sV.h, v_n(s, 3))
                    if s >= 1:
                        tensor.wait_ge(sS.h, s_n(s - 1, 8))
                    for j in range(NDP):
                        ins = tensor.transpose(out=ptb[:, 128 * j:128 * (j + 1)],
                                               in_=xbft[:, 128 * j:128 * (j + 1)],
                                               identity=idn[:])
                    sT.inc(ins)
                    tensor.wait_ge(sS.h, s_n(s, 2))
                    for p in range(NDP):
                        for nch in range(2):
                            ins = tensor.matmul(
                                out=pb[A + nch][:, :],
                                lhsT=gTt[:, 128 * p:128 * (p + 1)],
                                rhs=woutb(p)[:, 512 * nch:512 * (nch + 1)],
                                start=(p == 0), stop=(p == NDP - 1))
                    sT.inc(ins)
                    tensor.wait_ge(sS.h, s_n(s, 7))
                    for j in range(NDP):
                        ins = tensor.transpose(out=ptb[:, 128 * j:128 * (j + 1)],
                                               in_=xbft[:, 128 * j:128 * (j + 1)],
                                               identity=idn[:])
                    sT.inc(ins)
            block.tensor(pa3_tensor)

            def pa3_vector(vector):
                for s in range(NT + 1):
                    A = (s % 3) * 2
                    vector.wait_ge(sT.h, t_n(s, 1))
                    if s >= 2:
                        vector.wait_ge(sS.h, s_n(s - 2, 8))
                        vector.wait_ge(sP[s % 2].h, 16 * (s // 2))
                    sV.inc(vector.scalar_tensor_tensor(out=wf0[s % 2][:, 0:512],
                                                       in0=pb[A][:, :],
                                                       scalar=1.0, in1=bzs[:, 0:512],
                                                       op0=ALU.mult, op1=ALU.add))
                    sV.inc(vector.scalar_tensor_tensor(out=wf0[s % 2][:, 512:1024],
                                                       in0=pb[A + 1][:, :], scalar=1.0,
                                                       in1=bzs[:, 512:1024],
                                                       op0=ALU.mult, op1=ALU.add))
                    vector.wait_ge(sS.h, s_n(s, 1))
                    if s >= 1:
                        vector.wait_ge(sT.h, t_n(s - 1, 4))
                    ycol = ycols[:, 32:33] if s == 0 else ycols[:, s - 1:s]
                    sV.inc(vector.tensor_scalar(out=xbft[:, :], in0=wf0[s % 2][:, :],
                                                scalar1=ycol, scalar2=None, op0=ALU.mult))
                    vector.wait_ge(sT.h, t_n(s, 3))
                    vector.wait_ge(sR[s % 2].h, 16 * (s // 2 + 1))
                    vector.scalar_tensor_tensor(out=wf0[s % 2][:, 0:512], in0=pb[A][:, :],
                                                scalar=1.0, in1=wf1[s % 2][:, 0:512],
                                                op0=ALU.mult, op1=ALU.add)
                    sV.inc(vector.scalar_tensor_tensor(out=wf0[s % 2][:, 512:1024],
                                                       in0=pb[A + 1][:, :], scalar=1.0,
                                                       in1=wf1[s % 2][:, 512:1024],
                                                       op0=ALU.mult, op1=ALU.add))
                    vector.wait_ge(sV.h, sV.n)
                    vector.tensor_tensor(out=wf0[s % 2][:, 0:512], in0=wf0[s % 2][:, 0:512],
                                         in1=bouts[:, 0:512], op=ALU.add)
                    sV.inc(vector.tensor_tensor(out=wf0[s % 2][:, 512:1024],
                                                in0=wf0[s % 2][:, 512:1024],
                                                in1=bouts[:, 512:1024], op=ALU.add))
                    vector.wait_ge(sV.h, sV.n)
                    sV.inc(vector.tensor_reduce(out=stat[:, 0:1], in_=wf0[s % 2][:, :],
                                                axis=AX, op=ALU.add))
                    vector.wait_ge(sS.h, s_n(s, 6))
                    sV.inc(vector.reciprocal(out=stat[:, 4:5], in_=stat[:, 3:4]))
                    vector.wait_ge(sV.h, sV.n)
                    sV.inc(vector.tensor_scalar(out=wf0[s % 2][:, :], in0=wf0[s % 2][:, :],
                                                scalar1=stat[:, 4:5], scalar2=None,
                                                op0=ALU.mult))
            block.vector(pa3_vector)

            def pa3_scalar(scalar):
                for s in range(NT + 1):
                    _wc_cast(scalar, 2)
                    scalar.wait_ge(sV.h, v_n(s, 2))
                    sS.inc(scalar.activation(out=wf0[s % 2][:, :], in_=wf0[s % 2][:, :],
                                             func=ACT.Gelu))
                    scalar.wait_ge(sT.h, t_n(s, 2))
                    if s >= 1:
                        scalar.wait_ge(sT.h, t_n(s - 1, 3))
                    sS.inc(scalar.copy(out=gTt[:, :], in_=ptb[:, :]))
                    scalar.wait_ge(sV.h, v_n(s, 6))
                    sS.inc(scalar.mul(out=stat[:, 1:2], in_=stat[:, 0:1], mul=-1.0 / D))
                    scalar.wait_ge(sS.h, sS.n)
                    sS.inc(scalar.activation(out=wf0[s % 2][:, :], in_=wf0[s % 2][:, :],
                                             func=ACT.Identity, bias=stat[:, 1:2], scale=1.0))
                    scalar.wait_ge(sS.h, sS.n)
                    sS.inc(scalar.activation(out=dmy[:, :], in_=wf0[s % 2][:, :],
                                             func=ACT.Square, accum_out=stat[:, 2:3]))
                    scalar.wait_ge(sS.h, sS.n)
                    sS.inc(scalar.activation(out=stat[:, 3:4], in_=stat[:, 2:3],
                                             func=ACT.Sqrt, scale=1.0 / D,
                                             bias=stat[:, 5:6]))
                    scalar.wait_ge(sV.h, v_n(s, 8))
                    sS.inc(scalar.copy(out=xbft[:, :], in_=wf0[s % 2][:, :]))
                    scalar.wait_ge(sT.h, t_n(s, 4))
                    if s == 0:
                        sS.inc(scalar.copy(out=x1t0[:, :], in_=wf0[0][:, :]))
                    else:
                        ov = XT.rearrange("q (p c) -> q p c", c=TOK)
                        sS.inc(scalar.copy(
                            out=ov[:, :, 128 * (s - 1):128 * s],
                            in_=ptb.rearrange("q (p c) -> q p c", c=128)[:, :, :]))
                _wc_cast(scalar, NWC)
            block.scalar(pa3_scalar)
            PA3_V, PA3_S, PA3_T = sV.n, sS.n, sT.n


        if PHASES >= 2:
            pbCtx = contextlib.ExitStack()
            with pbCtx:
                EB = pbCtx.enter_context
                w1TbBuf = EB(nc.sbuf_tensor("w1TbBuf", [128, NDP * 4096], BF16))
                hT = EB(nc.sbuf_tensor("hT", [128, 32 * 256], BF16))
                w2c = [EB(nc.sbuf_tensor(f"w2c{i}", [128, 1024], BF16)) for i in range(2)]
                b2s = EB(nc.sbuf_tensor("b2s", [128, D], F32))
                evodf = EB(nc.sbuf_tensor("evodf", [4, 4096], F32))
                ev1b = EB(nc.sbuf_tensor("ev1b", [4, D], BF16))
                ev4b = EB(nc.sbuf_tensor("ev4b", [4, D], BF16))
                od1b = EB(nc.sbuf_tensor("od1b", [4, D], BF16))
                od4b = EB(nc.sbuf_tensor("od4b", [4, D], BF16))
                ev1M = EB(nc.sbuf_tensor("ev1M", [4, HF], BF16))
                ErB = EB(nc.sbuf_tensor("ErB", [4, 4096], BF16))
                odT = EB(nc.sbuf_tensor("odT", [128, 64], BF16))
                u1b = EB(nc.sbuf_tensor("u1b", [4, 256], BF16))
                vtb = EB(nc.sbuf_tensor("vtb", [4, 1024], BF16))
                scls = EB(nc.sbuf_tensor("scls", [128, 1], F32))
                b1cs = EB(nc.sbuf_tensor("b1cs", [128, 32], F32))

                def w1s(ds, hs):
                    return w1TbBuf[:, 4096 * ds + 128 * hs:4096 * ds + 128 * (hs + 1)]

                SB_PA3 = PA3_S

                def pb_gpsimd_w(gpsimd):
                    gpsimd.wait_ge(sV.h, VA4)
                    sKB.inc(gpsimd.dma_start(out=b2s[:], in_=b2_bc), 16)
                    sKB.inc(gpsimd.dma_start(out=scls[:], in_=scale_bc), 16)
                    sKB.inc(gpsimd.dma_start(out=b1cs[:, :], in_=b1c), 16)
                    gpsimd.wait_ge(sT.h, PA3_T)
                    gpsimd.wait_ge(sWSt[0].h, WST_A[0])
                    gpsimd.wait_ge(sWSt[1].h, WST_A[1])
                    for p in range(NDP):
                        sW1L.inc(gpsimd.dma_start(
                            out=w1TbBuf[:, 4096 * p:4096 * (p + 1)],
                            in_=w1bf[128 * p:128 * (p + 1), :]), 16)
                block.gpsimd(pb_gpsimd_w)
                W1L = sW1L.n

                def pb_gpsimd_lora(gpsimd):
                    gpsimd.wait_ge(sS.h, SA4 + 8)
                    for j in range(4):
                        sLT.inc(gpsimd.dma_start(out=evodf[j:j + 1, 0:1024],
                                                 in_=x1t0[1 + 2 * j:2 + 2 * j, :]), 16)
                        sLT.inc(gpsimd.dma_start(out=evodf[j:j + 1, 1024:2048],
                                                 in_=x1t0[9 + 2 * j:10 + 2 * j, :]), 16)
                        sLT.inc(gpsimd.dma_start(out=evodf[j:j + 1, 2048:3072],
                                                 in_=x1t0[2 + 2 * j:3 + 2 * j, :]), 16)
                        sLT.inc(gpsimd.dma_start(out=evodf[j:j + 1, 3072:4096],
                                                 in_=x1t0[10 + 2 * j:11 + 2 * j, :]), 16)
                block.gpsimd(pb_gpsimd_lora)
                GB4 = sLT.n

                def pb_scalar_lora(scalar):
                    scalar.wait_ge(sLT.h, GB4)
                    sS.inc(scalar.memzero(ev1M[:, :]))
                    sS.inc(scalar.memzero(ErB[:, :]))
                    sS.inc(scalar.copy(out=ev1b[:, :], in_=evodf[:, 0:1024]))
                    sS.inc(scalar.copy(out=ev4b[:, :], in_=evodf[:, 1024:2048]))
                    sS.inc(scalar.copy(out=od1b[:, :], in_=evodf[:, 2048:3072]))
                    sS.inc(scalar.copy(out=od4b[:, :], in_=evodf[:, 3072:4096]))
                block.scalar(pb_scalar_lora)
                SB_L1 = sS.n

                def pb_gpsimd_lora2(gpsimd):
                    gpsimd.wait_ge(sS.h, SB_L1)
                    for j in range(4):
                        sLT.inc(gpsimd.dma_start(out=ev1M[j:j + 1, 1024 * j:1024 * (j + 1)],
                                                 in_=ev1b[j:j + 1, :]), 16)
                    for r in range(4):
                        for j in range(4):
                            sLT.inc(gpsimd.dma_start(
                                out=ErB[j:j + 1, 1024 * r + 256 * j:1024 * r + 256 * (j + 1)],
                                in_=ev4b[j:j + 1, r:1024:4],
                                allow_slow_non_contiguous=True), 16)
                block.gpsimd(pb_gpsimd_lora2)
                GB5 = sLT.n

                def pb_tensor_lora(tensor):
                    tensor.wait_ge(sS.h, SB_L1)
                    tensor.wait_ge(sS.h, SB_PA3)
                    for ds in range(NDP):
                        if ds > 0:
                            tensor.wait_ge(sS.h, SB_L1 + 2 * ds)
                        ins = tensor.transpose(out=ptb[:, 0:4], in_=od1b[:, 128 * ds:128 * (ds + 1)],
                                               identity=idn[0:4, 0:4])
                        sT.inc(ins)
                        ins = tensor.transpose(out=ptb[:, 4:8], in_=od4b[:, 128 * ds:128 * (ds + 1)],
                                               identity=idn[0:4, 0:4])
                        sT.inc(ins)
                block.tensor(pb_tensor_lora)
                TB_L = sT.n

                def pb_scalar_lora2(scalar):
                    for ds in range(NDP):
                        scalar.wait_ge(sT.h, TB_L - (2 * (NDP - 1 - ds) + 1))
                        sS.inc(scalar.copy(out=odT[:, 4 * ds:4 * (ds + 1)], in_=ptb[:, 0:4]))
                        scalar.wait_ge(sT.h, TB_L - 2 * (NDP - 1 - ds))
                        sS.inc(scalar.copy(out=odT[:, 32 + 4 * ds:32 + 4 * (ds + 1)],
                                           in_=ptb[:, 4:8]))
                block.scalar(pb_scalar_lora2)
                SB_L2 = sS.n

                TB0, VB0, SB0 = sT.n, sV.n, sS.n
                TPB, VPB, SPB = 70, 10, 47

                def bt_n(g, k):
                    return TB0 + TPB * g + k

                def bv_n(g, k):
                    return VB0 + VPB * g + k

                def bs_n(g, k):
                    return SB0 + SPB * g + k

                def bl_slab(g, hs):
                    return 16 * (16 * g + hs // 2 + 1)

                def pb_sync(sync):
                    sync.wait_ge(sP[0].h, PA3_P0)
                    sync.wait_ge(sP[1].h, PA3_P1)
                    for g in range(8):
                        for tt in range(2):
                            i = 2 * g + tt
                            if g >= 1:
                                sync.wait_ge(sV.h, bv_n(g - 1, 5 * tt + 5))
                            sR[tt].inc(sync.dma_start(out=wf1[tt][:, :],
                                                      in_=sp_x1[128 * i:128 * (i + 1), :]), 16)
                        for tt in range(2):
                            sync.wait_ge(sV.h, bv_n(g, 5 * tt + 5))
                            sP[tt].inc(sync.dma_start(out=sp_x2[256 * g + 128 * tt:
                                                                256 * g + 128 * (tt + 1), :],
                                                      in_=wf0[tt][:, :]), 16)
                block.sync(pb_sync)
                PB_R0, PB_P0, PB_P1 = sR[0].n, sP[0].n, sP[1].n

                def pb_tensor(tensor):
                    tensor.wait_ge(sW1L.h, W1L)
                    tensor.wait_ge(sLT.h, GB5)
                    tensor.wait_ge(sS.h, SB_L2)
                    for g in range(8):
                        if g >= 1:
                            tensor.wait_ge(sS.h, bs_n(g - 1, 35))
                        for ds in range(NDP):
                            ins = tensor.matmul(out=psm[0:4, 0:256], lhsT=odT[:, 4 * ds:4 * (ds + 1)],
                                                rhs=XT[:, 2048 * ds + 256 * g:2048 * ds + 256 * (g + 1)],
                                                start=(ds == 0), stop=(ds == NDP - 1),
                                                skip_group_check=True)
                        sT.inc(ins)
                        tensor.wait_ge(sS.h, bs_n(g, 1))
                        if g >= 1:
                            tensor.wait_ge(sT.h, bt_n(g - 1, 68))
                        for hs in range(32):
                            bank = pb[hs % 2]
                            if hs >= 2:
                                tensor.wait_ge(sS.h, bs_n(g, hs))
                            for ds in range(NDP):
                                tensor.matmul(out=bank[:, 0:256], lhsT=w1s(ds, hs),
                                              rhs=XT[:, 2048 * ds + 256 * g:2048 * ds + 256 * (g + 1)],
                                              start=(ds == 0), stop=False)
                            ins = tensor.matmul(out=bank[:, 0:256],
                                                lhsT=ev1M[0:4, 128 * hs:128 * (hs + 1)],
                                                rhs=u1b[0:4, :], start=False, stop=True)
                            sT.inc(ins)
                        tensor.wait_ge(sS.h, bs_n(g, 33))
                        for tt in range(2):
                            if g >= 1 or tt == 1:
                                tensor.wait_ge(sS.h, bs_n(g - 1 + tt, 35 - tt))
                            for r in range(4):
                                for ds in range(NDP):
                                    ins = tensor.matmul(
                                        out=psm[0:4, 128 * r:128 * (r + 1)],
                                        lhsT=odT[:, 32 + 4 * ds:32 + 4 * (ds + 1)],
                                        rhs=hT[:, 256 * (8 * r + ds) + 128 * tt:
                                               256 * (8 * r + ds) + 128 * (tt + 1)],
                                        start=(ds == 0), stop=(ds == NDP - 1),
                                        skip_group_check=True)
                            sT.inc(ins)
                        if g >= 1:
                            tensor.wait_ge(sV.h, bv_n(g - 1, 10))
                        for hs in range(32):
                            tensor.wait_ge(sL[hs % 2].h, bl_slab(g, hs))
                            for tt in range(2):
                                for nc2 in range(2):
                                    ins = tensor.matmul(
                                        out=pb[2 + 2 * tt + nc2][:, :],
                                        lhsT=hT[:, 256 * hs + 128 * tt:256 * hs + 128 * (tt + 1)],
                                        rhs=w2c[hs % 2][:, 512 * nc2:512 * (nc2 + 1)],
                                        start=(hs == 0), stop=False)
                            sT.inc(ins)
                        tensor.wait_ge(sS.h, bs_n(g, 35))
                        for tt in range(2):
                            for nc2 in range(2):
                                for r in range(4):
                                    ins = tensor.matmul(
                                        out=pb[2 + 2 * tt + nc2][:, :],
                                        lhsT=vtb[0:4, 512 * tt + 128 * r:512 * tt + 128 * (r + 1)],
                                        rhs=ErB[0:4, 1024 * r + 512 * nc2:1024 * r + 512 * (nc2 + 1)],
                                        start=False, stop=(r == 3))
                        sT.inc(ins)
                        for tt in range(2):
                            tensor.wait_ge(sS.h, bs_n(g, 40 + 6 * tt))
                            if g >= 1 or tt == 1:
                                tensor.wait_ge(sS.h, bs_n(g - 1 + tt, 47 - 6 * tt))
                            for j in range(NDP):
                                ins = tensor.transpose(out=ptb[:, 128 * j:128 * (j + 1)],
                                                       in_=xbft[:, 128 * j:128 * (j + 1)],
                                                       identity=idn[:])
                            sT.inc(ins)
                block.tensor(pb_tensor)

                def pb_gpsimd_main(gpsimd):
                    gpsimd.wait_ge(sLT.h, GB5)
                    gpsimd.wait_ge(sWSt[0].h, WST_A[0])
                    gpsimd.wait_ge(sWSt[1].h, WST_A[1])
                    for g in range(8):
                        for hs in range(32):
                            if hs >= 2:
                                gpsimd.wait_ge(sT.h, bt_n(g, 36 + hs - 2))
                            elif g >= 1:
                                gpsimd.wait_ge(sT.h, bt_n(g - 1, 68))
                            sL[hs % 2].inc(gpsimd.dma_start(
                                out=w2c[hs % 2][:, :],
                                in_=w2bv_pa[:, hs, :]), 16)
                        for tt in range(2):
                            i = 2 * g + tt
                            gpsimd.wait_ge(sS.h, bs_n(g, 41 + 6 * tt))
                            ov = cc_x2_in.ap().rearrange("(j r) t -> r j t", r=128)
                            iv = xbft.rearrange("r (j c) -> r j c", c=128)
                            sXS[tt].inc(gpsimd.dma_start(out=ov[:, :, 128 * i:128 * (i + 1)],
                                                         in_=iv[:, :, :]), 16)
                block.gpsimd(pb_gpsimd_main)

                def pb_vector(vector):
                    vector.wait_ge(sKB.h, 48)
                    for g in range(8):
                        for tt in range(2):
                            vector.wait_ge(sT.h, bt_n(g, 68))
                            vector.wait_ge(sR[tt].h, (PA3_R0 if tt == 0 else PA3_R1)
                                           + 16 * (g + 1))
                            if g >= 1:
                                vector.wait_ge(sP[tt].h, (PA3_P0 if tt == 0 else PA3_P1)
                                               + 16 * g)
                            b0, b1 = pb[2 + 2 * tt], pb[3 + 2 * tt]
                            vector.scalar_tensor_tensor(out=wf0[tt][:, 0:512], in0=b0[:, :],
                                                        scalar=1.0, in1=wf1[tt][:, 0:512],
                                                        op0=ALU.mult, op1=ALU.add)
                            sV.inc(vector.scalar_tensor_tensor(out=wf0[tt][:, 512:1024],
                                                               in0=b1[:, :],
                                                               scalar=1.0, in1=wf1[tt][:, 512:1024],
                                                               op0=ALU.mult, op1=ALU.add))
                            vector.wait_ge(sV.h, sV.n)
                            vector.tensor_tensor(out=wf0[tt][:, 0:512], in0=wf0[tt][:, 0:512],
                                                 in1=b2s[:, 0:512], op=ALU.add)
                            sV.inc(vector.tensor_tensor(out=wf0[tt][:, 512:1024],
                                                        in0=wf0[tt][:, 512:1024],
                                                        in1=b2s[:, 512:1024], op=ALU.add))
                            vector.wait_ge(sV.h, sV.n)
                            sV.inc(vector.tensor_reduce(out=stat[:, 0:1], in_=wf0[tt][:, :],
                                                        axis=AX, op=ALU.add))
                            vector.wait_ge(sS.h, bs_n(g, 39 + 6 * tt))
                            sV.inc(vector.reciprocal(out=stat[:, 4:5], in_=stat[:, 3:4]))
                            vector.wait_ge(sV.h, sV.n)
                            sV.inc(vector.tensor_scalar(out=wf0[tt][:, :], in0=wf0[tt][:, :],
                                                        scalar1=stat[:, 4:5], scalar2=None,
                                                        op0=ALU.mult))
                block.vector(pb_vector)

                def pb_scalar_main(scalar):
                    scalar.wait_ge(sKB.h, 48)
                    for g in range(8):
                        scalar.wait_ge(sT.h, bt_n(g, 1))
                        sS.inc(scalar.activation(out=u1b[:, :], in_=psm[0:4, 0:256],
                                                 func=ACT.Identity, scale=scls[0:4, 0:1]))
                        for hs in range(32):
                            scalar.wait_ge(sT.h, bt_n(g, 1 + hs + 1))
                            sS.inc(scalar.activation(out=hT[:, 256 * hs:256 * (hs + 1)],
                                                     in_=pb[hs % 2][:, 0:256], func=ACT.Gelu,
                                                     bias=b1cs[:, hs:hs + 1], scale=1.0))
                        for tt in range(2):
                            scalar.wait_ge(sT.h, bt_n(g, 34 + tt))
                            sS.inc(scalar.activation(out=vtb[:, 512 * tt:512 * (tt + 1)],
                                                     in_=psm[0:4, 0:512],
                                                     func=ACT.Identity, scale=scls[0:4, 0:1]))
                        for tt in range(2):
                            scalar.wait_ge(sV.h, bv_n(g, 5 * tt + 3))
                            sS.inc(scalar.mul(out=stat[:, 1:2], in_=stat[:, 0:1], mul=-1.0 / D))
                            scalar.wait_ge(sS.h, sS.n)
                            sS.inc(scalar.activation(out=wf0[tt][:, :], in_=wf0[tt][:, :],
                                                     func=ACT.Identity, bias=stat[:, 1:2],
                                                     scale=1.0))
                            scalar.wait_ge(sS.h, sS.n)
                            sS.inc(scalar.activation(out=dmy[:, :], in_=wf0[tt][:, :],
                                                     func=ACT.Square, accum_out=stat[:, 2:3]))
                            scalar.wait_ge(sS.h, sS.n)
                            sS.inc(scalar.activation(out=stat[:, 3:4], in_=stat[:, 2:3],
                                                     func=ACT.Sqrt, scale=1.0 / D,
                                                     bias=stat[:, 5:6]))
                            scalar.wait_ge(sV.h, bv_n(g, 5 * tt + 5))
                            if g > 0 or tt == 1:
                                scalar.wait_ge(sXS[1 - tt].h, 16 * (g + tt))
                            sS.inc(scalar.copy(out=xbft[:, :], in_=wf0[tt][:, :]))
                            scalar.wait_ge(sT.h, bt_n(g, 69 + tt))
                            sS.inc(scalar.copy(out=xbft[:, :], in_=ptb[:, :]))
                block.scalar(pb_scalar_main)
                PB_V, PB_S, PB_T = sV.n, sS.n, sT.n



        if PHASES >= 3:
          pdC = contextlib.ExitStack()
          with pdC:
            pcC = contextlib.ExitStack()
            EC = pcC.enter_context
            qTs = EC(nc.sbuf_tensor("qTs", [128, 4 * S], BF16))
            kTs = EC(nc.sbuf_tensor("kTs", [128, 4 * S], BF16))
            vAll = EC(nc.sbuf_tensor("vAll", [128, NTF * 520], BF16))
            mparts = EC(nc.sbuf_tensor("mparts", [128, 16], F32))
            mcol = EC(nc.sbuf_tensor("mcol", [128, 2], F32))
            cvb = EC(nc.sbuf_tensor("cvb", [128, 2048], BF16))
            sKC = Sem(E(nc.semaphore("sKC")))
            sQG = [Sem(E(nc.semaphore(f"sQG{i}"))) for i in range(2)]
            sHQ = Sem(E(nc.semaphore("sHQ")))
            sYG = [Sem(E(nc.semaphore(f"sYG{i}"))) for i in range(2)]

            def pc_gpsimd_cc(gpsimd):
                gpsimd.wait_ge(sXS[0].h, sXS[0].n)
                gpsimd.wait_ge(sXS[1].h, sXS[1].n)
                sC.inc(gpsimd.collective_compute(
                    "AllGather", ALU.bypass, replica_groups=RG,
                    ins=[cc_x2_in.ap()], outs=[cc_x2_out.ap()]))
            block.gpsimd(pc_gpsimd_cc)
            CC1 = sC.n

            KC1 = sKC.n

            qkvC = contextlib.ExitStack()
            with qkvC:
                EQ = qkvC.enter_context
                wqkb = EQ(nc.sbuf_tensor("wqkb", [128, NDP * 1024], BF16))
                x2cb = EQ(nc.sbuf_tensor("x2cb", [128, NDP * 512], BF16))
                x2c = [x2cb, x2cb]
                costt = EQ(nc.sbuf_tensor("costt", [128, NTF * 32], BF16))
                sintt = EQ(nc.sbuf_tensor("sintt", [128, NTF * 32], BF16))
                rsc = [EQ(nc.sbuf_tensor(f"rsc{i}", [128, 256], F32)) for i in range(2)]
                qrop = EQ(nc.sbuf_tensor("qrop", [128, 512], BF16))

                def pc_sync_qkw(sync):
                    sync.wait_ge(sKC.h, KC1)
                    sKC.inc(sync.dma_start(out=costt[:], in_=cos_t), 16)
                    sKC.inc(sync.dma_start(out=sintt[:], in_=sin_t), 16)
                    SQ0 = sS.n
                    for p in range(NDP):
                        sync.wait_ge(sS.h, SQ0 + p)
                        sX.inc(sync.dma_start(out=stg[:, 0:1024],
                                              in_=wqkv[128 * p:128 * (p + 1), 0:1024]), 16)
                block.sync(pc_sync_qkw)
                XQ1 = sX.n
                SQC0 = sS.n

                def pc_scalar_qkw(scalar):
                    scalar.wait_ge(sS.h, SQC0)
                    for p in range(NDP):
                        scalar.wait_ge(sX.h, XQ1 - 16 * (NDP - 1 - p))
                        scalar.mul(out=wqkb[:, 1024 * p:1024 * p + 512],
                                   in_=stg[:, 0:512], mul=0.125)
                        sS.inc(scalar.copy(out=wqkb[:, 1024 * p + 512:1024 * (p + 1)],
                                           in_=stg[:, 512:1024]))
                block.scalar(pc_scalar_qkw)
                SQC1 = sS.n

                TQ0, VQ0, SQ1 = sT.n, sV.n, sS.n
                TPQ, VPQ, SPQ = 4 * 4, 4 * 14, 4 * 3

                def tq_n(tc, k):
                    return TQ0 + TPQ * tc + k

                def vq_n(tc, k):
                    return VQ0 + VPQ * tc + k

                def sq_n(tc, k):
                    return SQ1 + SPQ * tc + k

                def x2src(tc, p):
                    slab = tc // 4
                    return cc_x2_out.ap()[1024 * slab + 128 * p:1024 * slab + 128 * (p + 1),
                                          512 * (tc % 4):512 * (tc % 4 + 1)]

                def pc_gpsimd_x2(gpsimd):
                    gpsimd.wait_ge(sC.h, CC1)
                    for tc in range(8):
                        if tc >= 1:
                            gpsimd.wait_ge(sT.h, tq_n(tc - 1, 14))
                        for p in range(NDP):
                            sQG[tc % 2].inc(gpsimd.dma_start(
                                out=x2c[tc % 2][:, 512 * p:512 * (p + 1)],
                                in_=x2src(tc, p)), 16)
                block.gpsimd(pc_gpsimd_x2)

                def pc_tensor_qk(tensor):
                    tensor.wait_ge(sS.h, SQC1)
                    tensor.wait_ge(sV.h, PB_V)
                    for tc in range(8):
                        tensor.wait_ge(sQG[tc % 2].h, 128 * (tc // 2 + 1))
                        for ts_ in range(4):
                            kt = 4 * tc + ts_
                            if tc > 0 or ts_ > 0:
                                tensor.wait_ge(sV.h, vq_n(tc, 14 * ts_) if ts_ > 0
                                               else vq_n(tc - 1, 14 * 4))
                            for p in range(NDP):
                                tensor.matmul(out=pb[0][:, :],
                                              lhsT=x2c[tc % 2][:, 512 * p + 128 * ts_:
                                                               512 * p + 128 * (ts_ + 1)],
                                              rhs=wqkb[:, 1024 * p:1024 * p + 512],
                                              start=(p == 0), stop=(p == NDP - 1))
                            for p in range(NDP):
                                ins = tensor.matmul(out=pb[1][:, :],
                                                    lhsT=x2c[tc % 2][:, 512 * p + 128 * ts_:
                                                                     512 * p + 128 * (ts_ + 1)],
                                                    rhs=wqkb[:, 1024 * p + 512:1024 * (p + 1)],
                                                    start=(p == 0), stop=(p == NDP - 1))
                            sT.inc(ins, 2)
                            tensor.wait_ge(sV.h, vq_n(tc, 14 * ts_ + 7))
                            if tc > 0 or ts_ > 0:
                                tensor.wait_ge(sS.h, sq_n(tc, 3 * ts_) if ts_ > 0
                                               else sq_n(tc - 1, 12))
                            for j in range(4):
                                ins = tensor.transpose(out=ptb[:, 128 * j:128 * (j + 1)],
                                                       in_=qrop[:, 128 * j:128 * (j + 1)],
                                                       identity=idn[:])
                            sT.inc(ins)
                            tensor.wait_ge(sV.h, vq_n(tc, 14 * ts_ + 14))
                            tensor.wait_ge(sS.h, sq_n(tc, 3 * ts_ + 1))
                            for j in range(4):
                                ins = tensor.transpose(out=ptb[:, 512 + 128 * j:512 + 128 * (j + 1)],
                                                       in_=qrop[:, 128 * j:128 * (j + 1)],
                                                       identity=idn[:])
                            sT.inc(ins)
                block.tensor(pc_tensor_qk)

                def rope_ops(vector, src_bank, kt, out_buf, n0):
                    pv = src_bank.rearrange("t (h c) -> t h c", c=64)
                    ov = out_buf.rearrange("t (h c) -> t h c", c=64)
                    cosb = costt[:, 32 * kt:32 * (kt + 1)].rearrange(
                        "t (h c) -> t h c", h=1).broadcast_to([128, 8, 32])
                    sinb = sintt[:, 32 * kt:32 * (kt + 1)].rearrange(
                        "t (h c) -> t h c", h=1).broadcast_to([128, 8, 32])
                    r0 = rsc[0].rearrange("t (h c) -> t h c", c=32)
                    r1 = rsc[1].rearrange("t (h c) -> t h c", c=32)
                    sV.inc(vector.tensor_tensor(out=r0[:, :, :], in0=pv[:, :, 0:32],
                                                in1=cosb, op=ALU.mult))
                    sV.inc(vector.tensor_tensor(out=r1[:, :, :], in0=pv[:, :, 32:64],
                                                in1=sinb, op=ALU.mult))
                    vector.wait_ge(sV.h, sV.n)
                    sV.inc(vector.tensor_tensor(out=ov[:, :, 0:32], in0=r0[:, :, :],
                                                in1=r1[:, :, :], op=ALU.subtract))
                    vector.wait_ge(sV.h, sV.n)
                    sV.inc(vector.tensor_tensor(out=r0[:, :, :], in0=pv[:, :, 0:32],
                                                in1=sinb, op=ALU.mult))
                    sV.inc(vector.tensor_tensor(out=r1[:, :, :], in0=pv[:, :, 32:64],
                                                in1=cosb, op=ALU.mult))
                    vector.wait_ge(sV.h, sV.n)
                    sV.inc(vector.tensor_tensor(out=ov[:, :, 32:64], in0=r0[:, :, :],
                                                in1=r1[:, :, :], op=ALU.add))
                    vector.wait_ge(sV.h, sV.n)
                    sV.inc(vector.engine_nop())
                    return

                def pc_vector_rope(vector):
                    vector.wait_ge(sKC.h, KC1 + 32)
                    for tc in range(8):
                        for ts_ in range(4):
                            kt = 4 * tc + ts_
                            vector.wait_ge(sT.h, tq_n(tc, 4 * ts_ + 2))
                            rope_ops(vector, pb[0], kt, qrop, 0)
                            vector.wait_ge(sT.h, tq_n(tc, 4 * ts_ + 3))
                            rope_ops(vector, pb[1], kt, qrop, 7)
                block.vector(pc_vector_rope)

                def pc_scalar_qev(scalar):
                    for tc in range(8):
                        for ts_ in range(4):
                            kt = 4 * tc + ts_
                            scalar.wait_ge(sT.h, tq_n(tc, 4 * ts_ + 3))
                            ovq = qTs.rearrange("r (j c) -> r j c", c=4096)
                            sS.inc(scalar.copy(
                                out=ovq[:, :, 128 * kt:128 * (kt + 1)],
                                in_=ptb.rearrange("r (j c) -> r j c", c=128)[:, 0:4, :]))
                            scalar.wait_ge(sT.h, tq_n(tc, 4 * ts_ + 4))
                            ovk = kTs.rearrange("r (j c) -> r j c", c=4096)
                            sS.inc(scalar.copy(
                                out=ovk[:, :, 128 * kt:128 * (kt + 1)],
                                in_=ptb.rearrange("r (j c) -> r j c", c=128)[:, 4:8, :]))
                            sS.inc(scalar.copy(out=dmy[:, 0:8], in_=dmy[:, 0:8]))
                block.scalar(pc_scalar_qev)
                TQ_END, VQ_END, SQ_END = sT.n, sV.n, sS.n

                def pc_sync_vw(sync):
                    sync.wait_ge(sS.h, SQ_END)
                    SQ2 = sS.n
                    for p in range(NDP):
                        sync.wait_ge(sS.h, SQ_END + p)
                        sX.inc(sync.dma_start(out=stg[:, 0:512],
                                              in_=wqkv[128 * p:128 * (p + 1), 1024:1536]), 16)
                block.sync(pc_sync_vw)
                XV1 = sX.n

                def pc_scalar_vw(scalar):
                    for p in range(NDP):
                        scalar.wait_ge(sX.h, XV1 - 16 * (NDP - 1 - p))
                        sS.inc(scalar.copy(out=wqkb[:, 512 * p:512 * (p + 1)],
                                           in_=stg[:, 0:512]))
                block.scalar(pc_scalar_vw)
                SV1 = sS.n

                def pc_vector_vones(vector):
                    vv = vAll.rearrange("t (k h c) -> t k h c", h=8, c=65)
                    sV.inc(vector.memset(vv[:, :, :, 64:65], 1.0))
                block.vector(pc_vector_vones)
                VV1 = sV.n

                TV0, SV2 = sT.n, sS.n

                def pc_gpsimd_x2v(gpsimd):
                    gpsimd.wait_ge(sT.h, TQ_END)
                    for tc in range(8):
                        if tc >= 1:
                            gpsimd.wait_ge(sT.h, TV0 + 4 * tc)
                        for p in range(NDP):
                            sQG[tc % 2].inc(gpsimd.dma_start(
                                out=x2c[tc % 2][:, 512 * p:512 * (p + 1)],
                                in_=x2src(tc, p)), 16)
                block.gpsimd(pc_gpsimd_x2v)

                def pc_tensor_v(tensor):
                    tensor.wait_ge(sS.h, SV1)
                    tensor.wait_ge(sV.h, PB_V)
                    for tc in range(8):
                        tensor.wait_ge(sQG[tc % 2].h, 128 * 4 + 128 * (tc // 2 + 1))
                        for ts_ in range(4):
                            if tc > 0 or ts_ > 0:
                                tensor.wait_ge(sS.h, SV2 + 4 * tc + ts_)
                            for p in range(NDP):
                                ins = tensor.matmul(out=pb[2][:, :],
                                                    lhsT=x2c[tc % 2][:, 512 * p + 128 * ts_:
                                                                     512 * p + 128 * (ts_ + 1)],
                                                    rhs=wqkb[:, 512 * p:512 * (p + 1)],
                                                    start=(p == 0), stop=(p == NDP - 1))
                            sT.inc(ins)
                block.tensor(pc_tensor_v)

                def pc_scalar_vev(scalar):
                    for tc in range(8):
                        for ts_ in range(4):
                            kt = 4 * tc + ts_
                            scalar.wait_ge(sT.h, TV0 + 4 * tc + ts_ + 1)
                            vv = vAll.rearrange("t (k h c) -> t k h c", h=8, c=65)
                            sS.inc(scalar.copy(
                                out=vv[:, kt, :, 0:64],
                                in_=pb[2].rearrange("t (h c) -> t h c", c=64)[:, :, :]))
                block.scalar(pc_scalar_vev)
                SV3 = sS.n
            hdC = pcC.enter_context(contextlib.ExitStack())
            _EH = hdC.enter_context
            qhat = _EH(nc.sbuf_tensor("qhat", [65, S], BF16))
            khat = _EH(nc.sbuf_tensor("khat", [65, S], BF16))
            pS = [_EH(nc.sbuf_tensor(f"pS{i}", [128, 512], BF16)) for i in range(2)]
            accev = _EH(nc.sbuf_tensor("accev", [65, 512], F32))
            rlc = _EH(nc.sbuf_tensor("rlc", [128, 1], F32))
            ysel = _EH(nc.sbuf_tensor("ysel", [128, 256], BF16))
            msk1s = _EH(nc.sbuf_tensor("msk1s", [128, 128], F32))
            msk2s = _EH(nc.sbuf_tensor("msk2s", [128, 512], F32))
            zb512 = _EH(nc.sbuf_tensor("zb512", [128, 512], BF16))

            def pc_sync_k(sync):
                sync.wait_ge(sS.h, SV3)
                sKC.inc(sync.dma_start(out=msk1s[:], in_=mask1), 16)
                sKC.inc(sync.dma_start(out=msk2s[:], in_=mask2), 16)
            block.sync(pc_sync_k)
            KC2 = sKC.n

            NCV = 168

            def _cv_src(m):
                e, which = m // 3, m % 3
                if e == 6:
                    t = (fc1T, fc2T, fc3T)[which]
                    return t.rearrange("(a p) d -> p a d", p=128)
                t = (ew1T, ew3T, ew2T)[which]
                return t[e].rearrange("(a p) d -> p a d", p=128)

            ewbv = ewbf.rearrange("(m a p) d -> m p a d", p=128, a=8)
            cvfv = stg.rearrange("p (a d) -> p a d", d=1024)
            cvbv = cvb.rearrange("p (a d) -> p a d", d=1024)

            def pc_sync_cv(sync):
                sync.wait_ge(sWC.h, NWC)
                def _store(st):
                    m, k = st // 8, st % 8
                    sCVs[st % 2].inc(sync.dma_start(
                        out=ewbv[m][:, k:k + 1, :],
                        in_=cvbv[:, st % 2:st % 2 + 1, :]), 16)
                for s in range(NCV):
                    if s >= 2:
                        sync.wait_ge(sCVv.h, s - 1)
                        _store(s - 2)
                    m, k = s // 8, s % 8
                    sCVl[s % 2].inc(sync.dma_start(
                        out=cvfv[:, s % 2:s % 2 + 1, :],
                        in_=_cv_src(m)[:, k:k + 1, :]), 16)
                for s in (NCV - 2, NCV - 1):
                    sync.wait_ge(sCVv.h, s + 1)
                    _store(s)
            block.sync(pc_sync_cv)
            CV_F = [sCVs[0].n, sCVs[1].n]

            _cv_next = [0]

            def _cv_cast(vector, budget):
                for _ in range(budget):
                    c = _cv_next[0]
                    if c >= NCV:
                        return
                    _cv_next[0] = c + 1
                    vector.wait_ge(sCVl[c % 2].h, 16 * (c // 2 + 1))
                    if c >= 2:
                        vector.wait_ge(sCVs[c % 2].h, 16 * (c // 2))
                    sCVv.inc(vector.tensor_scalar(
                        out=cvbv[:, c % 2:c % 2 + 1, :],
                        in0=cvfv[:, c % 2:c % 2 + 1, :],
                        scalar1=0.0, scalar2=None, op0=ALU.add))

            T1_per = [2] * NTF
            V1_per = [2] * NTF
            T1_head = sum(T1_per)
            V1_head = sum(V1_per)
            S1_head = 8
            P2P = []
            for G in range(8):
                ps_ = []
                for j in range(4):
                    if G == 0:
                        ps_.append(("d", j, 4 * G + j))
                        if j < 3:
                            ps_.append(("s", j, 4 * G + j))
                    else:
                        ps_.append(("m", j, 4 * G + j))
                for kt in range(4 * G):
                    ps_.append(("f", -1, kt))
                P2P.append(ps_)
            T2G = [1 + 2 * len(P2P[G]) + 4 for G in range(8)]
            S2G = [len(P2P[G]) + 1 + 5 for G in range(8)]
            V2G = [4 + 4 for G in range(8)]

            def T1QK(qt):
                return qt + 1 if qt < 2 else 2 * qt

            def T1TR(qt):
                return 3 + 2 * qt if qt < NTF - 1 else 2 * NTF

            def T2ST(pp):
                return pp + 2 if pp < 2 else 2 * pp + 1

            def T2AV(pp, np2):
                return 4 + 2 * pp if pp < np2 - 1 else 2 * np2 + 1
            T2_head = sum(T2G)
            S2_head = sum(S2G)
            V2_head = sum(V2G)
            TPH = T1_head + T2_head
            SPH = S1_head + S2_head + sum(S2G) * 0 + 0
            VPH = V1_head + V2_head
            SPH = S1_head + S2_head

            TH0, VH0, SH0 = sT.n, sV.n, sS.n

            def th_n(hh, k):
                return TH0 + TPH * hh + k

            def vh_n(hh, k):
                return VH0 + VPH * hh + k

            def sh_n(hh, k):
                return SH0 + SPH * hh + k

            import itertools
            T1_cum = [0] + list(itertools.accumulate(T1_per))
            V1_cum = [0] + list(itertools.accumulate(V1_per))
            T2_cum = [T1_head] + [T1_head + c for c in itertools.accumulate(T2G)]
            S2_cum = [S1_head] + [S1_head + c for c in itertools.accumulate(S2G)]
            V2_cum = [V1_head] + [V1_head + c for c in itertools.accumulate(V2G)]


            def pc_vector_kones(vector):
                vector.wait_ge(sS.h, SV3)
                sV.inc(vector.memset(khat[64:65, :], 1.0))
                sV.inc(vector.memset(zb512[:, :], 0.0))
            block.vector(pc_vector_kones)
            VK1 = sV.n
            VH0b = sV.n
            TH0b, SH0b = sT.n, sS.n
            TH0, VH0, SH0 = sT.n, sV.n, sS.n

            def pc_tensor_heads(tensor):
                tensor.wait_ge(sV.h, VK1)
                tensor.wait_ge(sS.h, SV3)

                def _qk(hh, qt):
                    if qt == 0 and hh >= 1:
                        tensor.wait_ge(sV.h, vh_n(hh - 1, VPH))
                    ins = tensor.matmul(out=pb[qt % 2][:, 0:128],
                                        lhsT=qhat[0:64, 128 * qt:128 * (qt + 1)],
                                        rhs=khat[0:64, 128 * qt:128 * (qt + 1)],
                                        start=True, stop=True,
                                        skip_group_check=True)
                    sT.inc(ins)

                def _tr1(hh, qt):
                    tensor.wait_ge(sV.h, vh_n(hh, 2 * qt + 2))
                    if qt >= 4:
                        tensor.wait_ge(sS.h, sh_n(hh, qt // 4))
                    ins = tensor.transpose(out=psm[0:1, 128 * (qt % 4):128 * (qt % 4) + 128],
                                           in_=mcol[:, qt % 2:qt % 2 + 1], identity=idnf[:])
                    sT.inc(ins)

                for hh in range(8):
                    tensor.wait_ge(sHQ.h, 32 * (hh + 1))
                    _qk(hh, 0)
                    _qk(hh, 1)
                    for qt in range(NTF):
                        _tr1(hh, qt)
                        if qt + 2 < NTF:
                            _qk(hh, qt + 2)
                    for G in range(8):
                        abank = pb[4]
                        np2 = len(P2P[G])

                        def _st(pp):
                            kind, i, kt = P2P[G][pp]
                            sbank = pb[2 + pp % 2]
                            if kind == "f":
                                ins = tensor.matmul(out=sbank[:, 0:512],
                                                    lhsT=khat[:, 128 * kt:128 * (kt + 1)],
                                                    rhs=qhat[:, 512 * G:512 * (G + 1)],
                                                    start=True, stop=True,
                                                    skip_group_check=True)
                            elif kind == "d":
                                ins = tensor.matmul(out=sbank[:, 0:128],
                                                    lhsT=khat[:, 128 * kt:128 * (kt + 1)],
                                                    rhs=qhat[:, 512 * G + 128 * i:512 * G + 128 * (i + 1)],
                                                    start=True, stop=True,
                                                    skip_group_check=True)
                            elif kind == "m":
                                w = 512 - 128 * i
                                ins = tensor.matmul(out=sbank[:, 0:w],
                                                    lhsT=khat[:, 128 * kt:128 * (kt + 1)],
                                                    rhs=qhat[:, 512 * G + 128 * i:512 * (G + 1)],
                                                    start=True, stop=True,
                                                    skip_group_check=True)
                            else:
                                w = 128 * (3 - i)
                                ins = tensor.matmul(out=sbank[:, 0:w],
                                                    lhsT=khat[:, 128 * kt:128 * (kt + 1)],
                                                    rhs=qhat[:, 512 * G + 128 * (i + 1):512 * (G + 1)],
                                                    start=True, stop=True,
                                                    skip_group_check=True)
                            sT.inc(ins)

                        def _av(pp):
                            kind, i, kt = P2P[G][pp]
                            tensor.wait_ge(sS.h, sh_n(hh, S2_cum[G] + pp + 1))
                            srt = False
                            stp = (pp == np2 - 1) or (G == 0 and kind == "d")
                            vsl = vAll[:, 520 * kt + 65 * hh:520 * kt + 65 * (hh + 1)]
                            if kind == "f":
                                ins = tensor.matmul(out=abank[0:65, 0:512],
                                                    lhsT=vsl, rhs=pS[pp % 2][:, 0:512],
                                                    start=False, stop=stp,
                                                    skip_group_check=True)
                            elif kind == "d":
                                ins = tensor.matmul(out=abank[0:65, 128 * i:128 * (i + 1)],
                                                    lhsT=vsl, rhs=pS[pp % 2][:, 0:128],
                                                    start=srt, stop=stp,
                                                    skip_group_check=True)
                            elif kind == "m":
                                w = 512 - 128 * i
                                ins = tensor.matmul(out=abank[0:65, 128 * i:512],
                                                    lhsT=vsl, rhs=pS[pp % 2][:, 0:w],
                                                    start=srt, stop=stp,
                                                    skip_group_check=True)
                            else:
                                w = 128 * (3 - i)
                                ins = tensor.matmul(out=abank[0:65, 128 * (i + 1):512],
                                                    lhsT=vsl, rhs=pS[pp % 2][:, 0:w],
                                                    start=srt, stop=stp,
                                                    skip_group_check=True)
                            sT.inc(ins)

                        tensor.wait_ge(sS.h, sh_n(hh, (G + 1)))
                        tensor.wait_ge(sS.h, sh_n(hh, S2_cum[G]))
                        ins = tensor.matmul(out=abank[0:65, 0:512],
                                            lhsT=vAll[:, 0:65],
                                            rhs=zb512[:, :], start=True, stop=False,
                                            skip_group_check=True)
                        sT.inc(ins)
                        _st(0)
                        _st(1)
                        for pp in range(np2):
                            _av(pp)
                            if pp + 2 < np2:
                                _st(pp + 2)
                        tensor.wait_ge(sS.h, sh_n(hh, S2_cum[G] + np2 + 1))
                        for i in range(4):
                            ins = tensor.transpose(out=psm[0:128, 128 * i:128 * i + 65],
                                                   in_=accev[:, 128 * i:128 * (i + 1)],
                                                   identity=idnf[0:65, 0:65])
                        sT.inc(ins, 4)
            block.tensor(pc_tensor_heads)

            def pc_vector_heads2(vector):
                vector.wait_ge(sKC.h, KC2)
                for hh in range(8):
                    for qt in range(NTF):
                        bank = pb[qt % 2]
                        vector.wait_ge(sT.h, th_n(hh, T1QK(qt)))
                        sV.inc(vector.tensor_tensor(out=bank[:, 0:128], in0=bank[:, 0:128],
                                                    in1=msk1s[:, :], op=ALU.add))
                        vector.wait_ge(sV.h, sV.n)
                        sV.inc(vector.tensor_reduce(out=mcol[:, qt % 2:qt % 2 + 1],
                                                    in_=bank[:, 0:128],
                                                    axis=AX, op=ALU.max))
                    for G in range(8):
                        np2 = len(P2P[G])
                        for pp, (kind, i, kt) in enumerate(P2P[G]):
                            if kind not in ("d", "m"):
                                continue
                            w = 128 if kind == "d" else 512 - 128 * i
                            vector.wait_ge(sT.h, th_n(hh, T2_cum[G] + T2ST(pp)))
                            sV.inc(vector.tensor_tensor(out=pb[2 + pp % 2][:, 0:w],
                                                        in0=pb[2 + pp % 2][:, 0:w],
                                                        in1=msk2s[:, 0:w], op=ALU.add))
                        _cv_cast(vector, 3)
                        vector.wait_ge(sT.h, th_n(hh, T2_cum[G + 1]))
                        vector.wait_ge(sS.h, sh_n(hh, S2_cum[G + 1]))
                        if G >= 1:
                            vector.wait_ge(sY.h, 16 * (8 * hh + G))
                        elif hh >= 1:
                            vector.wait_ge(sY.h, 16 * 8 * hh)
                        for i in range(4):
                            sV.inc(vector.tensor_scalar(
                                out=ysel[:, 64 * i:64 * (i + 1)],
                                in0=psm[0:128, 128 * i:128 * i + 64],
                                scalar1=stat[:, 10 + i:11 + i],
                                scalar2=None, op0=ALU.mult))
            block.vector(pc_vector_heads2)
            ATTV = sV.n

            def pc_scalar_heads(scalar):
                for hh in range(8):
                    for G in range(8):
                        scalar.wait_ge(sT.h, th_n(hh, T1TR(4 * G + 3)))
                        sS.inc(scalar.activation(out=qhat[64:65, 512 * G:512 * (G + 1)],
                                                 in_=psm[0:1, 0:512], func=ACT.Identity,
                                                 scale=-1.0))
                    for G in range(8):
                        np2 = len(P2P[G])
                        nmask = 0
                        for pp, (kind, i, kt) in enumerate(P2P[G]):
                            sbank = pb[2 + pp % 2]
                            scalar.wait_ge(sT.h, th_n(hh, T2_cum[G] + T2ST(pp)))
                            if kind in ("d", "m"):
                                w = 128 if kind == "d" else 512 - 128 * i
                                scalar.wait_ge(sV.h, vh_n(hh, V2_cum[G] + nmask + 1))
                                nmask += 1
                                sS.inc(scalar.activation(out=pS[pp % 2][:, 0:w],
                                                         in_=sbank[:, 0:w], func=ACT.Exp))
                            elif kind == "s":
                                w = 128 * (3 - i)
                                sS.inc(scalar.activation(out=pS[pp % 2][:, 0:w],
                                                         in_=sbank[:, 0:w], func=ACT.Exp))
                            else:
                                sS.inc(scalar.activation(out=pS[pp % 2][:, 0:512],
                                                         in_=sbank[:, 0:512], func=ACT.Exp))
                        scalar.wait_ge(sT.h, th_n(hh, T2_cum[G] + 1 + 2 * np2))
                        sS.inc(scalar.copy(out=accev[:, :], in_=pb[4][0:65, 0:512]))
                        scalar.wait_ge(sT.h, th_n(hh, T2_cum[G + 1]))
                        for i in range(4):
                            sS.inc(scalar.activation(out=stat[:, 6 + i:7 + i],
                                                     in_=psm[0:128, 128 * i + 64:128 * i + 65],
                                                     func=ACT.Ln))
                        scalar.wait_ge(sS.h, sS.n)
                        sS.inc(scalar.activation(out=stat[:, 10:14],
                                                 in_=stat[:, 6:10],
                                                 func=ACT.Exp, scale=-1.0))
            block.scalar(pc_scalar_heads)

            def pc_gpsimd_heads(gpsimd):
                gpsimd.wait_ge(sS.h, SV3)
                for hh in range(8):
                    pt_ = hh // 2
                    base = 64 * (hh % 2)
                    if hh >= 1:
                        gpsimd.wait_ge(sT.h, th_n(hh - 1, TPH))
                    sHQ.inc(gpsimd.dma_start(
                        out=qhat[0:64, :],
                        in_=qTs[base:base + 64, 4096 * pt_:4096 * (pt_ + 1)]), 16)
                    sHQ.inc(gpsimd.dma_start(
                        out=khat[0:64, :],
                        in_=kTs[base:base + 64, 4096 * pt_:4096 * (pt_ + 1)]), 16)
                    for G in range(8):
                        gpsimd.wait_ge(sV.h, vh_n(hh, V2_cum[G + 1]))
                        ov = cc_y_in.ap().rearrange("(g i t) c -> g t i c", i=4, t=128)
                        iv = ysel.rearrange("t (i c) -> t i c", c=64)
                        sY.inc(gpsimd.dma_start(out=ov[G][:, :, 64 * hh:64 * (hh + 1)],
                                                in_=iv[:, :, :]), 16)
            block.gpsimd(pc_gpsimd_heads)
            Y_T = sY.n

            def pc_gpsimd_cc2(gpsimd):
                gpsimd.wait_ge(sY.h, Y_T)
                sC.inc(gpsimd.collective_compute(
                    "AllGather", ALU.bypass, replica_groups=RG,
                    ins=[cc_y_in.ap()], outs=[cc_y_out.ap()]))
            block.gpsimd(pc_gpsimd_cc2)
            CC2 = sC.n

            pcC.close()
            accF = pdC.enter_context(nc.sbuf_tensor("accF", [128, NT * D], F32))

            opC = contextlib.ExitStack()
            with opC:
                EO = opC.enter_context
                wob = EO(nc.sbuf_tensor("wob", [128, NDP * 1024], BF16))
                ygt = [EO(nc.sbuf_tensor(f"ygt{i}", [128, 2048], BF16)) for i in range(2)]
                yfull = EO(nc.sbuf_tensor("yfull", [128, D], BF16))
                yTt = EO(nc.sbuf_tensor("yTt", [128, D], BF16))

                XO0 = sX.n
                SO0 = sS.n

                def pc_sync_wo(sync):
                    for p in range(NDP):
                        sync.wait_ge(sS.h, SO0 + p)
                        sX.inc(sync.dma_start(out=stg[:, 0:1024],
                                              in_=wo[128 * p:128 * (p + 1), :]), 16)
                block.sync(pc_sync_wo)
                XO1 = sX.n

                def pc_scalar_wo(scalar):
                    scalar.wait_ge(sS.h, SO0)
                    for p in range(NDP):
                        scalar.wait_ge(sX.h, XO1 - 16 * (NDP - 1 - p))
                        sS.inc(scalar.copy(out=wob[:, 1024 * p:1024 * (p + 1)],
                                           in_=stg[:, 0:1024]))
                block.scalar(pc_scalar_wo)
                SO1 = sS.n

                TO0, VO0, SO2 = sT.n, sV.n, sS.n
                TPO, VPO, SPO = 4, 9, 7
                R_B = [sR[0].n, sR[1].n]

                def to_n(i, k):
                    return TO0 + TPO * i + k

                def vo_n(i, k):
                    return VO0 + VPO * i + k

                def so_n(i, k):
                    return SO2 + SPO * i + k

                def pc_gpsimd_yg(gpsimd):
                    gpsimd.wait_ge(sC.h, CC2)
                    for i in range(NT):
                        if i >= 2:
                            gpsimd.wait_ge(sV.h, vo_n(i - 2, 4))
                        for sl in range(2):
                            for ab in range(2):
                                sYG[i % 2].inc(gpsimd.dma_start(
                                    out=ygt[i % 2][:, 1024 * sl + 512 * ab:
                                                   1024 * sl + 512 * (ab + 1)],
                                    in_=cc_y_out.ap()[4096 * sl + 2048 * ab + 128 * i:
                                                      4096 * sl + 2048 * ab + 128 * (i + 1), :]), 16)
                block.gpsimd(pc_gpsimd_yg)

                def pc_vector_op(vector):
                    vector.wait_ge(sS.h, SO1)
                    vector.wait_ge(sP[0].h, PB_P0)
                    vector.wait_ge(sP[1].h, PB_P1)
                    for i in range(NT):
                        vector.wait_ge(sYG[i % 2].h, 64 * (i // 2 + 1))
                        if i >= 1:
                            vector.wait_ge(sT.h, to_n(i - 1, 1))
                        for sl in range(2):
                            a = ygt[i % 2][:, 1024 * sl:1024 * sl + 512]
                            b = ygt[i % 2][:, 1024 * sl + 512:1024 * sl + 1024]
                            sV.inc(vector.tensor_sub(out=a, in0=a, in1=b))
                            vector.wait_ge(sV.h, sV.n)
                            sV.inc(vector.scalar_tensor_tensor(
                                out=yfull[:, 512 * sl:512 * (sl + 1)], in0=a,
                                scalar=omsk[:, 0:1], in1=b, op0=ALU.mult, op1=ALU.add))
                        vector.wait_ge(sT.h, to_n(i, 2))
                        vector.wait_ge(sR[i % 2].h, R_B[i % 2] + 16 * (i // 2 + 1))
                        if i >= 2:
                            vector.wait_ge(sS.h, so_n(i - 2, 4))
                        A = (i % 3) * 2
                        vector.scalar_tensor_tensor(out=wf0[i % 2][:, 0:512],
                                                    in0=pb[A][:, :], scalar=1.0,
                                                    in1=wf1[i % 2][:, 0:512],
                                                    op0=ALU.mult, op1=ALU.add)
                        sV.inc(vector.scalar_tensor_tensor(out=wf0[i % 2][:, 512:1024],
                                                           in0=pb[A + 1][:, :], scalar=1.0,
                                                           in1=wf1[i % 2][:, 512:1024],
                                                           op0=ALU.mult, op1=ALU.add), 2)
                        vector.wait_ge(sV.h, sV.n)
                        sV.inc(vector.tensor_reduce(out=stat[:, 0:1], in_=wf0[i % 2][:, :],
                                                    axis=AX, op=ALU.add))
                        vector.wait_ge(sS.h, so_n(i, 5))
                        sV.inc(vector.reciprocal(out=stat[:, 4:5], in_=stat[:, 3:4]))
                        vector.wait_ge(sV.h, sV.n)
                        sV.inc(vector.tensor_scalar(out=accF[:, 1024 * i:1024 * (i + 1)],
                                                    in0=wf0[i % 2][:, :],
                                                    scalar1=stat[:, 4:5], scalar2=None,
                                                    op0=ALU.mult))
                block.vector(pc_vector_op)

                def pc_sync_res(sync):
                    for i in range(NT):
                        if i >= 2:
                            sync.wait_ge(sV.h, vo_n(i - 2, 6))
                        sR[i % 2].inc(sync.dma_start(
                            out=wf1[i % 2][:, :],
                            in_=sp_x2[128 * i:128 * (i + 1), :]), 16)
                block.sync(pc_sync_res)

                def pc_tensor_op(tensor):
                    tensor.wait_ge(sS.h, SO1)
                    tensor.wait_ge(sV.h, ATTV)
                    for i in range(NT):
                        tensor.wait_ge(sV.h, vo_n(i, 4))
                        if i >= 1:
                            tensor.wait_ge(sS.h, so_n(i - 1, 7))
                        for j in range(NDP):
                            ins = tensor.transpose(out=ptb[:, 128 * j:128 * (j + 1)],
                                                   in_=yfull[:, 128 * j:128 * (j + 1)],
                                                   identity=idn[:])
                        sT.inc(ins)
                        tensor.wait_ge(sS.h, so_n(i, 1))
                        if i >= 3:
                            tensor.wait_ge(sV.h, vo_n(i - 3, 6))
                        A = (i % 3) * 2
                        for p in range(NDP):
                            for nc2 in range(2):
                                ins = tensor.matmul(out=pb[A + nc2][:, :],
                                                    lhsT=yTt[:, 128 * p:128 * (p + 1)],
                                                    rhs=wob[:, 1024 * p + 512 * nc2:
                                                            1024 * p + 512 * (nc2 + 1)],
                                                    start=(p == 0), stop=(p == NDP - 1))
                        sT.inc(ins)
                        tensor.wait_ge(sS.h, so_n(i, 6))
                        for j in range(4):
                            ins = tensor.transpose(out=ptb[:, 128 * j:128 * (j + 1)],
                                                   in_=xbft[:, 128 * j:128 * (j + 1)],
                                                   identity=idn[:])
                        sT.inc(ins)
                        for j in range(4, NDP):
                            ins = tensor.transpose(out=ptb[:, 128 * j:128 * (j + 1)],
                                                   in_=xbft[:, 128 * j:128 * (j + 1)],
                                                   identity=idn[:])
                        sT.inc(ins)
                block.tensor(pc_tensor_op)

                def pc_scalar_op(scalar):
                    for i in range(NT):
                        scalar.wait_ge(sT.h, to_n(i, 1))
                        sS.inc(scalar.copy(out=yTt[:, :], in_=ptb[:, :]))
                        scalar.wait_ge(sV.h, vo_n(i, 7))
                        sS.inc(scalar.mul(out=stat[:, 1:2], in_=stat[:, 0:1], mul=-1.0 / D))
                        scalar.wait_ge(sS.h, sS.n)
                        sS.inc(scalar.activation(out=wf0[i % 2][:, :], in_=wf0[i % 2][:, :],
                                                 func=ACT.Identity, bias=stat[:, 1:2],
                                                 scale=1.0))
                        scalar.wait_ge(sS.h, sS.n)
                        sS.inc(scalar.activation(out=dmy[:, :], in_=wf0[i % 2][:, :],
                                                 func=ACT.Square, accum_out=stat[:, 2:3]))
                        scalar.wait_ge(sS.h, sS.n)
                        sS.inc(scalar.activation(out=stat[:, 3:4], in_=stat[:, 2:3],
                                                 func=ACT.Sqrt, scale=1.0 / D,
                                                 bias=stat[:, 5:6]))
                        scalar.wait_ge(sV.h, vo_n(i, 9))
                        sS.inc(scalar.copy(out=xbft[:, :],
                                           in_=accF[:, 1024 * i:1024 * (i + 1)]))
                        scalar.wait_ge(sT.h, to_n(i, 4))
                        ov = XT.rearrange("q (p c) -> q p c", c=TOK)
                        sS.inc(scalar.copy(
                            out=ov[:, :, 128 * i:128 * (i + 1)],
                            in_=ptb.rearrange("q (p c) -> q p c", c=128)[:, :, :]))
                block.scalar(pc_scalar_op)
                PC_T, PC_V, PC_S = sT.n, sV.n, sS.n

            TAP3 = sO.n
            if TAPS:
                def pc_tap3(sync):
                    sync.wait_ge(sV.h, PC_V)
                    for i in range(NT):
                        sO.inc(sync.dma_start(
                            out=taps["tap_x3"][128 * i:128 * (i + 1), :],
                            in_=accF[:, 1024 * i:1024 * (i + 1)]), 16)
                block.sync(pc_tap3)
            TAP3E = sO.n

            if PHASES >= 4:
                pdE = pdC.enter_context
                ewA = pdE(nc.sbuf_tensor("ewA", [128, NDP * 1024], BF16))
                ewB = pdE(nc.sbuf_tensor("ewB", [128, NDP * 1024], BF16))
                ewC = pdE(nc.sbuf_tensor("ewC", [128, NDP * 1024], BF16))
                uT = pdE(nc.sbuf_tensor("uT", [128, NDP * 1024], BF16))
                silb = pdE(nc.sbuf_tensor("silb", [128, 512], BF16))
                lnsum = pdE(nc.sbuf_tensor("lnsum", [128, NT], F32))
                gwTb = pdE(nc.sbuf_tensor("gwTb", [128, 48], BF16))
                gbs = pdE(nc.sbuf_tensor("gbs", [128, 6], F32))
                iot6 = pdE(nc.sbuf_tensor("iot6", [128, 6], F32))
                geb = pdE(nc.sbuf_tensor("geb", [128, NT * 6], F32))
                gwf = pdE(nc.sbuf_tensor("gwf", [128, 48], F32))

                KD0 = sKC.n

                def pd_sync_k(sync):
                    sync.wait_ge(sS.h, PC_S)
                    sKC.inc(sync.dma_start(out=gbs[:], in_=gb_bc), 16)
                    sKC.inc(sync.dma_start(out=iot6[:], in_=iota6), 16)
                    sKC.inc(sync.dma_start(out=gwf[:], in_=gwT), 16)
                block.sync(pd_sync_k)
                KD1 = sKC.n

                def pd_scalar_k(scalar):
                    scalar.wait_ge(sKC.h, KD1)
                    sS.inc(scalar.copy(out=gwTb[:, :], in_=gwf[:, :]))
                block.scalar(pd_scalar_k)
                SD0 = sS.n

                TD0, VD0, SD1 = sT.n, sV.n, sS.n
                TPD, VPD, SPD = 1, 11, 1

                def td_n(i, k):
                    return TD0 + TPD * i + k

                def vd_n(i, k):
                    return VD0 + VPD * i + k

                def sd_n(i, k):
                    return SD1 + SPD * i + k

                def pd_tensor_g(tensor):
                    tensor.wait_ge(sS.h, SD0)
                    tensor.wait_ge(sV.h, PC_V)
                    for i in range(NT):
                        if i >= 1:
                            tensor.wait_ge(sV.h, vd_n(i - 1, 1))
                        for p in range(NDP):
                            ins = tensor.matmul(
                                out=psm[0:128, 0:6],
                                lhsT=XT[:, 2048 * p + 128 * i:2048 * p + 128 * (i + 1)],
                                rhs=gwTb[:, 6 * p:6 * (p + 1)],
                                start=(p == 0), stop=(p == NDP - 1),
                                skip_group_check=True)
                        sT.inc(ins)
                block.tensor(pd_tensor_g)

                def pd_vector_g(vector):
                    vector.wait_ge(sS.h, SD0)
                    for i in range(NT):
                        vector.wait_ge(sT.h, td_n(i, 1))
                        sV.inc(vector.scalar_tensor_tensor(
                            out=mparts[:, 0:6], in0=psm[0:128, 0:6], scalar=1.0,
                            in1=gbs[:, :], op0=ALU.mult, op1=ALU.add))
                        vector.wait_ge(sS.h, sd_n(i, 1))
                        sV.inc(vector.tensor_reduce(out=stat[:, 6:7], in_=mparts[:, 8:14],
                                                    axis=AX, op=ALU.max))
                        vector.wait_ge(sV.h, sV.n)
                        sV.inc(vector.tensor_scalar(out=mparts[:, 0:6], in0=mparts[:, 8:14],
                                                    scalar1=stat[:, 6:7], scalar2=None,
                                                    op0=ALU.is_ge))
                        vector.wait_ge(sV.h, sV.n)
                        sV.inc(vector.scalar_tensor_tensor(
                            out=mparts[:, 0:6], in0=mparts[:, 0:6], scalar=-1e30,
                            in1=mparts[:, 8:14], op0=ALU.mult, op1=ALU.add))
                        vector.wait_ge(sV.h, sV.n)
                        sV.inc(vector.tensor_reduce(out=stat[:, 7:8], in_=mparts[:, 0:6],
                                                    axis=AX, op=ALU.max))
                        vector.wait_ge(sV.h, sV.n)
                        sV.inc(vector.tensor_tensor(out=stat[:, 8:9], in0=stat[:, 6:7],
                                                    in1=stat[:, 7:8], op=ALU.add))
                        vector.wait_ge(sV.h, sV.n)
                        sV.inc(vector.tensor_scalar(out=stat[:, 8:9], in0=stat[:, 8:9],
                                                    scalar1=1e-8, scalar2=None, op0=ALU.add))
                        vector.wait_ge(sV.h, sV.n)
                        sV.inc(vector.reciprocal(out=stat[:, 9:10], in_=stat[:, 8:9]))
                        vector.wait_ge(sV.h, sV.n)
                        sV.inc(vector.tensor_scalar(out=mparts[:, 0:6], in0=mparts[:, 8:14],
                                                    scalar1=stat[:, 7:8], scalar2=None,
                                                    op0=ALU.is_ge))
                        vector.wait_ge(sV.h, sV.n)
                        sV.inc(vector.tensor_tensor(out=mparts[:, 0:6], in0=mparts[:, 0:6],
                                                    in1=mparts[:, 8:14], op=ALU.mult))
                        vector.wait_ge(sV.h, sV.n)
                        sV.inc(vector.tensor_scalar(out=geb[:, 6 * i:6 * (i + 1)],
                                                    in0=mparts[:, 0:6],
                                                    scalar1=stat[:, 9:10], scalar2=None,
                                                    op0=ALU.mult))
                        vector.wait_ge(sV.h, sV.n)
                block.vector(pd_vector_g)

                def pd_scalar_g(scalar):
                    for i in range(NT):
                        scalar.wait_ge(sV.h, vd_n(i, 1))
                        sS.inc(scalar.activation(out=mparts[:, 8:14], in_=mparts[:, 0:6],
                                                 func=ACT.Sigmoid))
                block.scalar(pd_scalar_g)
                GATE_V, GATE_S, GATE_T = sV.n, sS.n, sT.n

                def ew_src(e, which):
                    if e == 6:
                        t = (fc1T, fc2T, fc3T)[which]
                        return t.rearrange("(a p) d -> p a d", p=128)
                    t = (ew1T, ew3T, ew2T)[which]
                    return t[e].rearrange("(a p) d -> p a d", p=128)

                TPE, SPE, VPE = 80, 32, 64
                TE0, SE0, VE0 = sT.n, sS.n, sV.n

                def te_n(e, k):
                    return TE0 + TPE * e + k

                def se_n(e, k):
                    return SE0 + SPE * e + k

                def ve_n(e, k):
                    return VE0 + VPE * e + k

                ewbv2 = ewbf.rearrange("(m a p) d -> m p a d", p=128, a=8)

                def pd_gpsimd_w(gpsimd):
                    gpsimd.wait_ge(sS.h, SE0)
                    gpsimd.wait_ge(sV.h, GATE_V)
                    gpsimd.wait_ge(sCVs[0].h, CV_F[0])
                    gpsimd.wait_ge(sCVs[1].h, CV_F[1])
                    for e in range(7):
                        for j in range(6):
                            which, hf = j // 2, j % 2
                            if e >= 1:
                                gpsimd.wait_ge(sT.h,
                                               te_n(e - 1, 72 if j < 4 else 80))
                            dst = (ewA, ewA, ewB, ewB, ewC, ewC)[j]
                            sem = sGC if j >= 4 else sGAB
                            sem.inc(gpsimd.dma_start(
                                out=dst.rearrange("p (a d) -> p a d",
                                                  d=1024)[:, 4 * hf:4 * (hf + 1), :],
                                in_=ewbv2[3 * e + which][:, 4 * hf:4 * (hf + 1), :]),
                                16)
                block.gpsimd(pd_gpsimd_w)

                def pd_scalar_e(scalar):
                    for e in range(7):
                        for ip in range(16):
                            scalar.wait_ge(sT.h, te_n(e, 2 * ip + 1))
                            if ip >= 1:
                                scalar.wait_ge(sV.h, ve_n(e, ip))
                            sS.inc(scalar.activation(out=silb[:, :],
                                                     in_=pb[(2 * ip) % 4][:, :],
                                                     func=ACT.Silu))
                        for ip in range(16):
                            scalar.wait_ge(sT.h, te_n(e, 40 + 2 * ip + 1))
                            scalar.wait_ge(sV.h, ve_n(e, 32 + ip))
                            sS.inc(scalar.activation(out=silb[:, :],
                                                     in_=pb[(2 * ip) % 4][:, :],
                                                     func=ACT.Silu))
                block.scalar(pd_scalar_e)

                LV_R = {}

                def pd_vector_e(vector):
                    vector.wait_ge(sV.h, GATE_V)
                    if TAPS:
                        vector.wait_ge(sO.h, TAP3E)
                    for e in range(7):
                        for h2 in range(2):
                            for ip in range(16):
                                hs, tc2 = ip // 2, ip % 2
                                vector.wait_ge(sS.h, se_n(e, (1 if h2 == 0 else 17) + ip))
                                vector.wait_ge(sT.h, te_n(e, 40 * h2 + 2 * ip + 2))
                                sV.inc(vector.tensor_tensor(
                                    out=uT[:, 1024 * hs + 512 * tc2:1024 * hs + 512 * (tc2 + 1)],
                                    in0=silb[:, :], in1=pb[(2 * ip) % 4 + 1][:, :],
                                    op=ALU.mult))
                            for ti in range(8):
                                i = 8 * h2 + ti
                                vector.wait_ge(sT.h, te_n(e, 40 * h2 + 33 + ti))
                                for nc2 in range(2):
                                    a = accF[:, 1024 * i + 512 * nc2:1024 * i + 512 * (nc2 + 1)]
                                    if e == 6:
                                        sV.inc(vector.scalar_tensor_tensor(
                                            out=a, in0=pb[4 + nc2][:, :], scalar=1.0,
                                            in1=a, op0=ALU.mult, op1=ALU.add))
                                    else:
                                        sV.inc(vector.scalar_tensor_tensor(
                                            out=a, in0=pb[4 + nc2][:, :],
                                            scalar=geb[:, 6 * i + e:6 * i + e + 1],
                                            in1=a, op0=ALU.mult, op1=ALU.add))
                                if e == 6:
                                    vector.wait_ge(sV.h, sV.n)
                                    sLN.inc(vector.tensor_reduce(
                                        out=lnsum[:, i:i + 1],
                                        in_=accF[:, 1024 * i:1024 * (i + 1)],
                                        axis=AX, op=ALU.add))
                                    LV_R[i] = sLN.n
                block.vector(pd_vector_e)

                def pd_tensor_e(tensor):
                    tensor.wait_ge(sV.h, GATE_V)
                    tensor.wait_ge(sV.h, PC_V)
                    for e in range(7):
                        for h2 in range(2):
                            tb = 40 * h2
                            for ip in range(16):
                                hs, tc2 = ip // 2, ip % 2
                                bA = pb[(2 * ip) % 4]
                                bB = pb[(2 * ip) % 4 + 1]
                                if ip == 0:
                                    tensor.wait_ge(sGAB.h, 64 * (e + 1))
                                if ip >= 2:
                                    tensor.wait_ge(sV.h, ve_n(e, 32 * h2 + ip - 1))
                                elif h2 == 1:
                                    tensor.wait_ge(sV.h, ve_n(e, 16))
                                elif e >= 1:
                                    tensor.wait_ge(sV.h, ve_n(e - 1, VPE))
                                for p in range(NDP):
                                    ins = tensor.matmul(
                                        out=bA[:, :],
                                        lhsT=ewA[:, 1024 * p + 128 * hs:1024 * p + 128 * (hs + 1)],
                                        rhs=XT[:, 2048 * p + 1024 * h2 + 512 * tc2:
                                               2048 * p + 1024 * h2 + 512 * (tc2 + 1)],
                                        start=(p == 0), stop=(p == NDP - 1))
                                sT.inc(ins)
                                for p in range(NDP):
                                    ins = tensor.matmul(
                                        out=bB[:, :],
                                        lhsT=ewB[:, 1024 * p + 128 * hs:1024 * p + 128 * (hs + 1)],
                                        rhs=XT[:, 2048 * p + 1024 * h2 + 512 * tc2:
                                               2048 * p + 1024 * h2 + 512 * (tc2 + 1)],
                                        start=(p == 0), stop=(p == NDP - 1))
                                sT.inc(ins)
                            tensor.wait_ge(sGC.h, 32 * (e + 1))
                            tensor.wait_ge(sV.h, ve_n(e, 32 * h2 + 16))
                            for ti in range(8):
                                if ti >= 1:
                                    tensor.wait_ge(sV.h, ve_n(e, 32 * h2 + 16 + 2 * ti))
                                elif h2 == 1:
                                    tensor.wait_ge(sV.h, ve_n(e, 48))
                                elif e >= 1:
                                    tensor.wait_ge(sV.h, ve_n(e - 1, VPE))
                                for hs in range(8):
                                    for nc2 in range(2):
                                        ins = tensor.matmul(
                                            out=pb[4 + nc2][:, :],
                                            lhsT=uT[:, 1024 * hs + 128 * ti:1024 * hs + 128 * (ti + 1)],
                                            rhs=ewC[:, 1024 * hs + 512 * nc2:1024 * hs + 512 * (nc2 + 1)],
                                            start=(hs == 0), stop=(hs == NDP - 1))
                                sT.inc(ins)
                block.tensor(pd_tensor_e)
                E_V, E_S, E_T = sV.n, sS.n, sT.n

                LT0, LV0, LS0 = sT.n, sV.n, sS.n
                VPL, SPL = 3, 4
                P_B = [sP[0].n, sP[1].n]

                def lv_n(i, k):
                    return LV0 + VPL * i + k

                def ls_n(i, k):
                    return LS0 + SPL * i + k

                LV_M = {}

                def _ln_cols(i):
                    o = 6 * (i % 2)
                    return o, o + 1, o + 2, o + 3, o + 4

                def pd_vector_ln(vector):
                    for i in range(NT):
                        a = accF[:, 1024 * i:1024 * (i + 1)]
                        _, _, _, c3, c4 = _ln_cols(i)
                        if i >= 2:
                            vector.wait_ge(sP[i % 2].h, P_B[i % 2] + 16 * (i // 2))
                        vector.wait_ge(sS.h, ls_n(i, 4))
                        sV.inc(vector.reciprocal(out=stat[:, c4:c4 + 1],
                                                 in_=stat[:, c3:c3 + 1]))
                        vector.wait_ge(sV.h, sV.n)
                        sV.inc(vector.tensor_scalar(out=wf0[i % 2][:, :], in0=a,
                                                    scalar1=stat[:, c4:c4 + 1],
                                                    scalar2=None, op0=ALU.mult))
                        LV_M[i] = sV.n
                block.vector(pd_vector_ln)

                def pd_scalar_ln(scalar):
                    for i in range(NT):
                        a = accF[:, 1024 * i:1024 * (i + 1)]
                        _, c1, c2, c3, _ = _ln_cols(i)
                        scalar.wait_ge(sLN.h, LV_R[i])
                        if i >= 2:
                            scalar.wait_ge(sV.h, LV_M[i - 2])
                        sS.inc(scalar.mul(out=stat[:, c1:c1 + 1],
                                          in_=lnsum[:, i:i + 1], mul=-1.0 / D))
                        scalar.wait_ge(sS.h, sS.n)
                        sS.inc(scalar.activation(out=a, in_=a, func=ACT.Identity,
                                                 bias=stat[:, c1:c1 + 1], scale=1.0))
                        scalar.wait_ge(sS.h, sS.n)
                        sS.inc(scalar.activation(out=dmy[:, :], in_=a, func=ACT.Square,
                                                 accum_out=stat[:, c2:c2 + 1]))
                        scalar.wait_ge(sS.h, sS.n)
                        sS.inc(scalar.activation(out=stat[:, c3:c3 + 1],
                                                 in_=stat[:, c2:c2 + 1],
                                                 func=ACT.Sqrt, scale=1.0 / D,
                                                 bias=stat[:, 5:6]))
                block.scalar(pd_scalar_ln)

                def pd_sync_out(sync):
                    for i in range(NT):
                        sync.wait_ge(sV.h, LV_M[i])
                        sP[i % 2].inc(sync.dma_start(
                            out=y_out[128 * i:128 * (i + 1), :],
                            in_=wf0[i % 2][:, :]), 16)
                    sync.wait_ge(sP[0].h, sP[0].n)
                    sync.wait_ge(sP[1].h, sP[1].n)
                block.sync(pd_sync_out)



        FIN_V, FIN_S, FIN_T = sV.n, sS.n, sT.n

        src_sp = sp_x2 if PHASES >= 2 else sp_x1

        def tmp_out(sync):
            if PHASES >= 4:
                return
            sync.wait_ge(sP[0].h, sP[0].n)
            sync.wait_ge(sP[1].h, sP[1].n)
            sync.wait_ge(sS.h, FIN_S)
            for t in range(NT):
                sO.inc(sync.dma_start(out=y_out[128 * t:128 * (t + 1), :],
                                      in_=src_sp[128 * t:128 * (t + 1), :]), 16)
            sync.wait_ge(sO.h, sO.n)
        block.sync(tmp_out)

        if TAPS:
            def tap_sync(sync):
                sync.wait_ge(sP[0].h, PB_P0)
                sync.wait_ge(sP[1].h, PB_P1)
                for t in range(NT):
                    sO.inc(sync.dma_start(out=taps["tap_x1"][128 * t:128 * (t + 1), :],
                                          in_=sp_x1[128 * t:128 * (t + 1), :]), 16)
                sO.inc(sync.dma_start(out=taps["tap_dbc"], in_=dbcT_hbm[:, :]), 16)
                sO.inc(sync.dma_start(out=taps["tap_yc"], in_=ycols[:, :]), 16)
                sync.wait_ge(sO.h, sO.n)
            block.sync(tap_sync)

    return nc


def _prep(inputs):
    x = np.ascontiguousarray(np.asarray(inputs["L0_attn_x"], dtype=np.float32))
    W_in = np.asarray(inputs["L0_attn_W_in"], dtype=np.float32)
    b_in = np.asarray(inputs["L0_attn_b_in"], dtype=np.float32)

    base = {
        "wz": np.ascontiguousarray(W_in[:, D:2 * D]),
        "wdbc": np.ascontiguousarray(W_in[:, 2 * D:2 * D + 48]),
        "bz_bc": _bcast(b_in[D:2 * D]),
        "bdbc_bc": _bcast(b_in[2 * D:2 * D + 48]),
        "alog": np.ascontiguousarray(np.asarray(inputs["L0_attn_A_log"],
                                                dtype=np.float32).reshape(16, 1)),
        "wout": np.ascontiguousarray(np.asarray(inputs["L0_attn_W_out"], dtype=np.float32)),
        "bout_bc": _bcast(np.asarray(inputs["L0_attn_b_out"], dtype=np.float32)),
        "w1T": np.ascontiguousarray(np.asarray(inputs["L0_ffn_W1"], dtype=np.float32).T),
        "b1c": np.ascontiguousarray(np.asarray(inputs["L0_ffn_b1"],
                                               dtype=np.float32).reshape(HF // 128, 128).T),
        "w2T": np.ascontiguousarray(np.asarray(inputs["L0_ffn_W2"], dtype=np.float32).T),
        "b2_bc": _bcast(np.asarray(inputs["L0_ffn_b2"], dtype=np.float32)),
        "scale_bc": _bcast(np.asarray(inputs["L0_ffn_scale"],
                                      dtype=np.float32).reshape(1)).reshape(128, 1).copy(),
        "wo": np.ascontiguousarray(np.asarray(inputs["L1_attn_W_o"], dtype=np.float32)),
        "gwT": np.ascontiguousarray(np.asarray(inputs["L1_ffn_gate_w"],
                                               dtype=np.float32).T.reshape(8, 128, 6)
                                    .transpose(1, 0, 2).reshape(128, 48)),
        "gb_bc": _bcast(np.asarray(inputs["L1_ffn_gate_b"], dtype=np.float32)),
        "ew1T": np.ascontiguousarray(np.asarray(inputs["L1_ffn_ew1"],
                                                dtype=np.float32).transpose(0, 2, 1)),
        "ew2T": np.ascontiguousarray(np.asarray(inputs["L1_ffn_ew2"],
                                                dtype=np.float32).transpose(0, 2, 1)),
        "ew3T": np.ascontiguousarray(np.asarray(inputs["L1_ffn_ew3"],
                                                dtype=np.float32).transpose(0, 2, 1)),
        "fc1T": np.ascontiguousarray(np.asarray(inputs["L1_ffn_fc1"], dtype=np.float32).T),
        "fc2T": np.ascontiguousarray(np.asarray(inputs["L1_ffn_fc2"], dtype=np.float32).T),
        "fc3T": np.ascontiguousarray(np.asarray(inputs["L1_ffn_fc3"], dtype=np.float32).T),
        "ident": np.eye(128, dtype=BF),
        "identf": np.eye(128, dtype=np.float32),
        "mask1": np.where(np.arange(128)[None, :] <= np.arange(128)[:, None],
                          0.0, NEG).astype(np.float32),
        "mask2": np.concatenate([
            np.where(np.arange(128)[:, None] <= np.arange(128)[None, :],
                     0.0, NEG).astype(np.float32),
            np.zeros((128, 384), dtype=np.float32)], axis=1),
        "iota6": np.ascontiguousarray(np.broadcast_to(
            np.arange(6, dtype=np.float32), (128, 6))),
    }
    half = 32
    theta = 1.0 / (10000.0 ** (np.arange(half, dtype=np.float64) / half))
    tpos = np.arange(S, dtype=np.float64)
    ang = tpos[:, None] * theta[None, :]
    cosv = np.cos(ang).astype(np.float32).astype(BF)
    sinv = np.sin(ang).astype(np.float32).astype(BF)
    base["cos_t"] = np.ascontiguousarray(
        cosv.reshape(NTF, 128, half).transpose(1, 0, 2).reshape(128, NTF * half))
    base["sin_t"] = np.ascontiguousarray(
        sinv.reshape(NTF, 128, half).transpose(1, 0, 2).reshape(128, NTF * half))

    W_qkv = np.asarray(inputs["L1_attn_W_qkv"], dtype=np.float32)
    in_maps = []
    for c in range(8):
        b, h = c // 2, c % 2
        m = dict(base)
        m["x_own"] = np.ascontiguousarray(x[b, h * TOK:(h + 1) * TOK])
        m["xT_full"] = np.ascontiguousarray(x[b].T)
        m["xT_own"] = np.ascontiguousarray(x[b, h * TOK:(h + 1) * TOK].T)
        m["x_t0"] = np.ascontiguousarray(x[b, 0:128])
        m["wqkv"] = np.ascontiguousarray(np.concatenate(
            [W_qkv[:, 512 * h:512 * (h + 1)],
             W_qkv[:, D + 512 * h:D + 512 * (h + 1)],
             W_qkv[:, 2 * D + 512 * h:2 * D + 512 * (h + 1)]], axis=1))
        m["own_mask"] = np.full((128, 1), 1.0 - h, dtype=np.float32)
        in_maps.append(m)
    return in_maps


def _bcast(v):
    v = np.asarray(v, dtype=np.float32).reshape(-1)
    return np.ascontiguousarray(np.broadcast_to(v[None, :], (128, v.size)))


def kernel(**inputs):
    in_maps = _prep(inputs)
    nc = build()
    res = run_bass_kernel_spmd(nc, in_maps, core_ids=list(range(8)))
    out = np.empty((4, S, D), dtype=np.float32)
    for c in range(8):
        b, h = c // 2, c % 2
        out[b, h * TOK:(h + 1) * TOK] = res.results[c]["y_out"]
    kernel.last_results = res
    return out


# version 18
# speedup vs baseline: 1.0214x; 1.0214x over previous
import contextlib
import numpy as np
import ml_dtypes
import concourse.bass as bass
import concourse.mybir as mybir
from concourse.bass_utils import run_bass_kernel_spmd

F32 = mybir.dt.float32
BF16 = mybir.dt.bfloat16
AX = mybir.AxisListType.X
ALU = mybir.AluOpType
ACT = mybir.ActivationFunctionType
BF = ml_dtypes.bfloat16

TAPS = False
PHASES = 4
D = 1024
S = 4096
TOK = 2048
NT = 16
NTF = 32
NDP = 8
HF = 4096
NEG = -1.0e30


class Sem:
    def __init__(self, h):
        self.h = h
        self.n = 0

    def inc(self, instr, k=1):
        instr.then_inc(self.h, k)
        self.n += k
        return self.n


def build():
    nc = bass.Bass("TRN2", target_bir_lowering=False, debug=False)

    def inp(name, shape, dt=F32):
        return nc.dram_tensor(name, shape, dt, kind="ExternalInput").ap()

    x_own = inp("x_own", [TOK, D])
    xT_full = inp("xT_full", [D, S])
    xT_own = inp("xT_own", [D, TOK])
    x_t0 = inp("x_t0", [128, D])

    wz = inp("wz", [D, D])
    wdbc = inp("wdbc", [D, 48])
    bz_bc = inp("bz_bc", [128, D])
    bdbc_bc = inp("bdbc_bc", [128, 48])
    alog = inp("alog", [16, 1])
    wout = inp("wout", [D, D])
    bout_bc = inp("bout_bc", [128, D])
    w1T = inp("w1T", [D, HF])
    b1c = inp("b1c", [128, HF // 128])
    w2T = inp("w2T", [HF, D])
    b2_bc = inp("b2_bc", [128, D])
    scale_bc = inp("scale_bc", [128, 1])

    wqkv = inp("wqkv", [D, 1536])
    wo = inp("wo", [D, D])
    gwT = inp("gwT", [128, 48])
    gb_bc = inp("gb_bc", [128, 6])
    ew1T = inp("ew1T", [6, D, D])
    ew2T = inp("ew2T", [6, D, D])
    ew3T = inp("ew3T", [6, D, D])
    fc1T = inp("fc1T", [D, D])
    fc2T = inp("fc2T", [D, D])
    fc3T = inp("fc3T", [D, D])

    ident = inp("ident", [128, 128], BF16)
    identf = inp("identf", [128, 128])
    cos_t = inp("cos_t", [128, NTF * 32], BF16)
    sin_t = inp("sin_t", [128, NTF * 32], BF16)
    mask1 = inp("mask1", [128, 128])
    mask2 = inp("mask2", [128, 512])
    iota6 = inp("iota6", [128, 6])
    own_mask = inp("own_mask", [128, 1])

    y_out = nc.dram_tensor("y_out", [TOK, D], F32, kind="ExternalOutput").ap()
    taps = {}
    if TAPS:
        for t in ("tap_x1", "tap_x2", "tap_x3"):
            taps[t] = nc.dram_tensor(t, [TOK, D], F32, kind="ExternalOutput").ap()
        taps["tap_dbc"] = nc.dram_tensor("tap_dbc", [16, 3 * S], F32, kind="ExternalOutput").ap()
        taps["tap_yc"] = nc.dram_tensor("tap_yc", [128, 34], F32, kind="ExternalOutput").ap()

    sp_x1 = nc.dram_tensor("sp_x1", [TOK + 128, D], F32).ap()
    sp_x2 = nc.dram_tensor("sp_x2", [TOK, D], F32).ap()
    w2T_bf = nc.dram_tensor("w2T_bf", [HF, D], BF16).ap()
    w1bf = nc.dram_tensor("w1bf", [D, HF], BF16).ap()
    ewbf = nc.dram_tensor("ewbf", [21 * 1024, 1024], BF16).ap()
    dbcT_hbm = nc.dram_tensor("dbcT_hbm", [16, 3 * S], F32).ap()
    cc_x2_in_a = nc.dram_tensor("cc_x2_in_a", [D, TOK // 2], BF16)
    cc_x2_out_a = nc.dram_tensor("cc_x2_out_a", [2 * D, TOK // 2], BF16)
    cc_x2_in_b = nc.dram_tensor("cc_x2_in_b", [D, TOK // 2], BF16)
    cc_x2_out_b = nc.dram_tensor("cc_x2_out_b", [2 * D, TOK // 2], BF16)
    cc_y_in = nc.dram_tensor("cc_y_in", [S, 512], BF16)
    cc_y_out = nc.dram_tensor("cc_y_out", [2 * S, 512], BF16)
    RG = [[0, 1], [2, 3], [4, 5], [6, 7]]

    ctx = contextlib.ExitStack()
    E = ctx.enter_context
    with ctx:
        XT = E(nc.sbuf_tensor("XT", [128, NDP * TOK], BF16))
        stg = E(nc.sbuf_tensor("stg", [128, 4096], F32))
        wf0 = [E(nc.sbuf_tensor(f"wf0{i}", [128, D], F32)) for i in range(2)]
        wf1 = [E(nc.sbuf_tensor(f"wf1{i}", [128, D], F32)) for i in range(2)]
        xbft = E(nc.sbuf_tensor("xbft", [128, D], BF16))
        gTt = E(nc.sbuf_tensor("gTt", [128, D], BF16))
        stat = E(nc.sbuf_tensor("stat", [128, 16], F32))
        ycols = E(nc.sbuf_tensor("ycols", [128, 34], F32))
        idn = E(nc.sbuf_tensor("idn", [128, 128], BF16))
        idnf = E(nc.sbuf_tensor("idnf", [128, 128], F32))
        omsk = E(nc.sbuf_tensor("omsk", [128, 1], F32))
        x1t0 = E(nc.sbuf_tensor("x1t0", [128, D], F32))
        dmy = E(nc.sbuf_tensor("dmy", [128, D], BF16))

        pb = [E(nc.psum_tensor(f"pb{i}", [128, 512], F32)) for i in range(6)]
        ptb = E(nc.psum_tensor("ptb", [128, 1024], BF16))
        psm = E(nc.psum_tensor("psm", [128, 512], F32))

        sT = Sem(E(nc.semaphore("sT")))
        sV = Sem(E(nc.semaphore("sV")))
        sS = Sem(E(nc.semaphore("sS")))
        sC = Sem(E(nc.semaphore("sC")))
        sO = Sem(E(nc.semaphore("sO")))
        sK = Sem(E(nc.semaphore("sK")))
        sX = Sem(E(nc.semaphore("sX")))
        sGW = Sem(E(nc.semaphore("sGW")))
        sR = [Sem(E(nc.semaphore(f"sR{i}"))) for i in range(2)]
        sP = [Sem(E(nc.semaphore(f"sP{i}"))) for i in range(2)]
        sDB = [Sem(E(nc.semaphore(f"sDB{i}"))) for i in range(2)]
        sDB2 = Sem(E(nc.semaphore("sDB2")))
        sWS = [Sem(E(nc.semaphore(f"sWS{i}"))) for i in range(2)]
        sL = [Sem(E(nc.semaphore(f"sL{i}"))) for i in range(2)]
        sXS = [Sem(E(nc.semaphore(f"sXS{i}"))) for i in range(2)]
        sKB = Sem(E(nc.semaphore("sKB")))
        sY = Sem(E(nc.semaphore("sY")))
        sLT = Sem(E(nc.semaphore("sLT")))
        sCVl = [Sem(E(nc.semaphore(f"sCVl{i}"))) for i in range(2)]
        sCVs = [Sem(E(nc.semaphore(f"sCVs{i}"))) for i in range(2)]
        sCVv = Sem(E(nc.semaphore("sCVv")))
        sWL = [Sem(E(nc.semaphore(f"sWL{i}"))) for i in range(2)]
        sWC = Sem(E(nc.semaphore("sWC")))
        sWSt = [Sem(E(nc.semaphore(f"sWSt{i}"))) for i in range(2)]
        sW1L = Sem(E(nc.semaphore("sW1L")))
        sGAB = Sem(E(nc.semaphore("sGAB")))
        sGC = Sem(E(nc.semaphore("sGC")))
        sLN = Sem(E(nc.semaphore("sLN")))
        sX2 = [Sem(E(nc.semaphore(f"sX2{i}"))) for i in range(2)]
        sXSa = [Sem(E(nc.semaphore(f"sXSa{i}"))) for i in range(2)]
        block = E(nc.Block())

        paCtx = contextlib.ExitStack()
        with paCtx:
            EA = paCtx.enter_context
            regCw = EA(nc.sbuf_tensor("regCw", [128, 16768], BF16))
            bzs = EA(nc.sbuf_tensor("bzs", [128, D], F32))
            bouts = EA(nc.sbuf_tensor("bouts", [128, D], F32))
            bdbcs = EA(nc.sbuf_tensor("bdbcs", [128, 48], F32))
            dbc_ev = [EA(nc.sbuf_tensor(f"dbc_ev{i}", [128, 48], F32)) for i in range(2)]
            acol = EA(nc.sbuf_tensor("acol", [16, 1], F32))
            dbcT_ev = [EA(nc.sbuf_tensor(f"dbcT_ev{i}", [16, 384], F32)) for i in range(2)]
            x0t0b = EA(nc.sbuf_tensor("x0t0b", [128, NDP * 128], BF16))
            ones16 = EA(nc.sbuf_tensor("ones16", [16, 1], F32))

            WOFF_Z, WOFF_DBC, WOFF_OUT = 0, 8192, 8576

            def wzb(p):
                return regCw[:, WOFF_Z + 1024 * p:WOFF_Z + 1024 * (p + 1)]

            def wdbcb(p):
                return regCw[:, WOFF_DBC + 48 * p:WOFF_DBC + 48 * (p + 1)]

            def woutb(p):
                return regCw[:, WOFF_OUT + 1024 * p:WOFF_OUT + 1024 * (p + 1)]

            def p0_sync(sync):
                for dst, src in ((bzs, bz_bc), (bouts, bout_bc), (bdbcs, bdbc_bc),
                                 (idn, ident), (idnf, identf), (omsk, own_mask)):
                    sK.inc(sync.dma_start(out=dst[:], in_=src), 16)
                sK.inc(sync.dma_start(out=stat[0:16, 8:9], in_=alog), 16)
            block.sync(p0_sync)
            P0_CONST = sK.n

            xtbCtx = contextlib.ExitStack()
            with xtbCtx:
                xTbBuf = xtbCtx.enter_context(
                    nc.sbuf_tensor("xTbBuf", [128, NDP * 4096], BF16))

                def xTb(p):
                    return xTbBuf[:, 4096 * p:4096 * (p + 1)]

                S0 = sS.n

                def p0_sync_xt(sync):
                    for u in range(2 * NDP):
                        if u >= 2:
                            sync.wait_ge(sS.h, S0 + u - 1)
                        p, hf = u // 2, u % 2
                        sX2[u % 2].inc(sync.dma_start(
                            out=stg[:, 2048 * (u % 2):2048 * (u % 2 + 1)],
                            in_=xT_full[128 * p:128 * (p + 1),
                                        2048 * hf:2048 * (hf + 1)]), 16)
                block.sync(p0_sync_xt)

                def p0_scalar_xt(scalar):
                    for u in range(2 * NDP):
                        p, hf = u // 2, u % 2
                        scalar.wait_ge(sX2[u % 2].h, 16 * (u // 2 + 1))
                        sS.inc(scalar.copy(
                            out=xTb(p)[:, 2048 * hf:2048 * (hf + 1)],
                            in_=stg[:, 2048 * (u % 2):2048 * (u % 2 + 1)]))
                block.scalar(p0_scalar_xt)

                S1 = sS.n

                def p0_sync_w(sync):
                    for p in range(NDP):
                        sync.wait_ge(sS.h, S1 + 3 * p)
                        sX.inc(sync.dma_start(out=stg[:, 0:1024],
                                              in_=wz[128 * p:128 * (p + 1), :]), 16)
                        sX.inc(sync.dma_start(out=stg[:, 1024:1072],
                                              in_=wdbc[128 * p:128 * (p + 1), :]), 16)
                        sX.inc(sync.dma_start(out=stg[:, 1072:2096],
                                              in_=wout[128 * p:128 * (p + 1), :]), 16)
                block.sync(p0_sync_w)
                W_W = sX.n

                def p0_scalar_w(scalar):
                    for p in range(NDP):
                        scalar.wait_ge(sX.h, W_W - 48 * (NDP - 1 - p))
                        sS.inc(scalar.copy(out=wzb(p), in_=stg[:, 0:1024]))
                        sS.inc(scalar.copy(out=wdbcb(p), in_=stg[:, 1024:1072]))
                        sS.inc(scalar.copy(out=woutb(p), in_=stg[:, 1072:2096]))
                block.scalar(p0_scalar_w)
                P0_WCAST = sS.n

                S2 = sS.n

                def p0_sync_xto(sync):
                    for p in range(NDP):
                        sync.wait_ge(sS.h, S2 + p)
                        sX.inc(sync.dma_start(out=stg[:, 0:2048],
                                              in_=xT_own[128 * p:128 * (p + 1), :]), 16)
                block.sync(p0_sync_xto)
                W_XTO = sX.n

                def p0_scalar_xto(scalar):
                    for p in range(NDP):
                        scalar.wait_ge(sX.h, W_XTO - 16 * (NDP - 1 - p))
                        sS.inc(scalar.copy(out=XT[:, 2048 * p:2048 * (p + 1)],
                                           in_=stg[:, 0:2048]))
                block.scalar(p0_scalar_xto)
                P0_S = sS.n

                def p0_vec_t0(vector):
                    vector.wait_ge(sS.h, S0 + 2 * NDP)
                    xv = xTbBuf.rearrange("q (p c) -> q p c", c=4096)
                    ov = x0t0b.rearrange("q (p c) -> q p c", c=128)
                    sV.inc(vector.tensor_copy(out=ov[:, :, :], in_=xv[:, :, 0:128]))
                block.vector(p0_vec_t0)
                P0_V = sV.n

                TA0, VA0, SA0 = sT.n, sV.n, sS.n

                def pa1_tensor(tensor):
                    tensor.wait_ge(sS.h, P0_WCAST)
                    tensor.wait_ge(sK.h, P0_CONST)
                    for t in range(NTF):
                        bank = pb[4 + (t % 2)]
                        if t >= 2:
                            tensor.wait_ge(sV.h, VA0 + (t - 2) + 1)
                        for p in range(NDP):
                            ins = tensor.matmul(out=bank[:, 0:48],
                                                lhsT=xTb(p)[:, 128 * t:128 * (t + 1)],
                                                rhs=wdbcb(p), start=(p == 0),
                                                stop=(p == NDP - 1))
                        sT.inc(ins)
                        if t >= 1:
                            tensor.wait_ge(sV.h, VA0 + t)
                            if t >= 2:
                                tensor.wait_ge(sS.h, SA0 + (t - 1))
                            for j in range(3):
                                ins = tensor.transpose(
                                    out=pb[3][0:16, 128 * j:128 * (j + 1)],
                                    in_=dbc_ev[(t - 1) % 2][:, 16 * j:16 * (j + 1)],
                                    identity=idnf[:])
                            sT.inc(ins)
                    tensor.wait_ge(sV.h, VA0 + NTF)
                    tensor.wait_ge(sS.h, SA0 + (NTF - 1))
                    for j in range(3):
                        ins = tensor.transpose(out=pb[3][0:16, 128 * j:128 * (j + 1)],
                                               in_=dbc_ev[(NTF - 1) % 2][:, 16 * j:16 * (j + 1)],
                                               identity=idnf[:])
                    sT.inc(ins)
                block.tensor(pa1_tensor)

                def pa1_mm_n(t):
                    return TA0 + (1 if t == 0 else 2 * t)

                def pa1_tr_n(t):
                    return TA0 + (2 * (t + 1) + 1 if t < NTF - 1 else 2 * NTF)

                def pa1_vector(vector):
                    for t in range(NTF):
                        vector.wait_ge(sT.h, pa1_mm_n(t))
                        if t >= 2:
                            vector.wait_ge(sS.h, SA0 + (t - 2) + 1)
                        bank = pb[4 + (t % 2)]
                        sV.inc(vector.scalar_tensor_tensor(
                            out=dbc_ev[t % 2][:, :], in0=bank[:, 0:48], scalar=1.0,
                            in1=bdbcs[:, :], op0=ALU.mult, op1=ALU.add))
                block.vector(pa1_vector)

                def pa1_scalar(scalar):
                    for t in range(NTF):
                        scalar.wait_ge(sT.h, pa1_tr_n(t))
                        if t >= 2:
                            scalar.wait_ge(sDB[t % 2].h, 16 * (t // 2))
                        sS.inc(scalar.copy(out=dbcT_ev[t % 2][:, :],
                                           in_=pb[3][0:16, 0:384]))
                block.scalar(pa1_scalar)
                SA1 = sS.n

                def pa1_gpsimd(gpsimd):
                    ov = dbcT_hbm.rearrange("p (k c) -> p k c", k=3)
                    for t in range(NTF):
                        gpsimd.wait_ge(sS.h, SA0 + t + 1)
                        iv = dbcT_ev[t % 2].rearrange("p (k c) -> p k c", k=3)
                        sDB[t % 2].inc(gpsimd.dma_start(
                            out=ov[:, :, 128 * t:128 * (t + 1)], in_=iv[:, :, :]), 16)
                block.gpsimd(pa1_gpsimd)
            scanCtx = contextlib.ExitStack()
            with scanCtx:
                scanBuf = scanCtx.enter_context(
                    nc.sbuf_tensor("scanBuf", [16, 4 * 4096], F32))
                dB = scanBuf[:, 0:4096]
                dC = scanBuf[:, 4096:8192]
                dDp = scanBuf[:, 8192:12288]
                sc1 = scanBuf[:, 12288:16384]
                sc2 = stg[0:16, 0:4096]

                def pa2_gpsimd(gpsimd):
                    gpsimd.wait_ge(sDB[0].h, sDB[0].n)
                    gpsimd.wait_ge(sDB[1].h, sDB[1].n)
                    sDB2.inc(gpsimd.dma_start(out=scanBuf[:, 0:12288],
                                              in_=dbcT_hbm[:, :]), 16)
                block.gpsimd(pa2_gpsimd)

                def pa2_scalar(scalar):
                    scalar.wait_ge(sS.h, SA1)
                    scalar.wait_ge(sDB2.h, 16)
                    sS.inc(scalar.activation(out=acol[:, 0:1], in_=stat[0:16, 8:9],
                                             func=ACT.Exp))
                    sS.inc(scalar.activation(out=sc1[:, :], in_=dDp[:, :], func=ACT.Relu))
                    sS.inc(scalar.activation(out=sc2[:, :], in_=dDp[:, :], func=ACT.Abs))
                    scalar.wait_ge(sS.h, sS.n)
                    sS.inc(scalar.activation(out=sc2[:, :], in_=sc2[:, :], func=ACT.Exp, scale=-1.0))
                    scalar.wait_ge(sS.h, sS.n)
                    sS.inc(scalar.activation(out=sc2[:, :], in_=sc2[:, :], func=ACT.Ln, bias=1.0))
                block.scalar(pa2_scalar)
                SA2 = sS.n

                def pa2_vector(vector):
                    vector.wait_ge(sS.h, SA2)
                    sV.inc(vector.tensor_tensor(out=sc1[:, :], in0=sc1[:, :], in1=sc2[:, :],
                                                op=ALU.add))
                    sV.inc(vector.tensor_scalar(out=acol[:, 0:1], in0=acol[:, 0:1],
                                                scalar1=-1.0, scalar2=None, op0=ALU.mult))
                    vector.wait_ge(sV.h, sV.n)
                    sV.inc(vector.tensor_scalar(out=sc1[:, :], in0=sc1[:, :],
                                                scalar1=acol[0:16, 0:1], scalar2=None,
                                                op0=ALU.mult))
                block.vector(pa2_vector)
                VA2 = sV.n

                def pa2_scalar2(scalar):
                    scalar.wait_ge(sV.h, VA2)
                    sS.inc(scalar.activation(out=sc1[:, :], in_=sc1[:, :], func=ACT.Exp))
                block.scalar(pa2_scalar2)
                SA3 = sS.n

                def pa2_vector2(vector):
                    vector.wait_ge(sS.h, SA3)
                    sV.inc(vector.tensor_tensor_scan(out=sc2[:, :], data0=sc1[:, :],
                                                     data1=dB[:, :], initial=0.0,
                                                     op0=ALU.mult, op1=ALU.add))
                    vector.wait_ge(sV.h, sV.n)
                    sV.inc(vector.tensor_tensor(out=sc2[:, :], in0=sc2[:, :], in1=dC[:, :],
                                                op=ALU.mult))
                    sV.inc(vector.memset(ones16[:, 0:1], 1.0))
                    sV.inc(vector.memset(stat[:, 5:6], 1e-5))
                    sV.inc(vector.memset(ycols[:, :], 0.0))
                block.vector(pa2_vector2)
                VA3 = sV.n

                def pa2_tensor(tensor):
                    tensor.wait_ge(sV.h, VA3)
                    for t in range(NTF):
                        ins = tensor.matmul(out=psm[:, t:t + 1],
                                            lhsT=sc2[:, 128 * t:128 * (t + 1)],
                                            rhs=ones16[:, 0:1], start=True, stop=True,
                                            skip_group_check=True)
                    sT.inc(ins)
                block.tensor(pa2_tensor)
                TA2 = sT.n

                def pa2_scalar3(scalar):
                    scalar.wait_ge(sT.h, TA2)
                    sS.inc(scalar.copy(out=ycols[:, 0:NTF], in_=psm[:, 0:NTF]))
                    sS.inc(scalar.copy(out=ycols[:, 32:33], in_=psm[:, 0:1]))
                block.scalar(pa2_scalar3)
                SA4 = sS.n

                def pa2_vector3(vector):
                    vector.wait_ge(sS.h, SA4)
                    vector.wait_ge(sK.h, P0_CONST)
                    sV.inc(vector.tensor_sub(out=ycols[:, 0:NT], in0=ycols[:, 0:NT],
                                              in1=ycols[:, NT:2 * NT]))
                    vector.wait_ge(sV.h, sV.n)
                    sV.inc(vector.scalar_tensor_tensor(
                        out=ycols[:, 0:NT], in0=ycols[:, 0:NT], scalar=omsk[:, 0:1],
                        in1=ycols[:, NT:2 * NT], op0=ALU.mult, op1=ALU.add))
                block.vector(pa2_vector3)
                VA4 = sV.n

            TB0, VB0, SB0 = sT.n, sV.n, sS.n
            TPS, VPS, SPS = 4, 8, 8

            def t_n(s, k):
                return TB0 + TPS * s + k

            def v_n(s, k):
                return VB0 + VPS * s + k

            def s_n(s, k):
                return SB0 + SPS * s + k

            def lhsrc(s, p):
                if s == 0:
                    return x0t0b[:, 128 * p:128 * (p + 1)]
                return XT[:, 2048 * p + 128 * (s - 1):2048 * p + 128 * s]

            wcb = EA(nc.sbuf_tensor("wcb", [128, 4096], BF16))
            wcbv = wcb.rearrange("p (a d) -> p a d", d=1024)
            stgv3 = stg.rearrange("p (a d) -> p a d", d=1024)
            w2v_pa = w2T.rearrange("(a p) d -> p a d", p=128)
            w2bv_pa = w2T_bf.rearrange("(a p) d -> p a d", p=128)
            NWC = 32

            def pa3_gpsimd_conv(gpsimd):
                def _wstore(c):
                    if c < 16:
                        p_, h_ = c // 2, c % 2
                        ins = gpsimd.dma_start(
                            out=w1bf[128 * p_:128 * (p_ + 1), 2048 * h_:2048 * (h_ + 1)],
                            in_=wcb[:, 2048 * (c % 2):2048 * (c % 2 + 1)])
                    else:
                        r = c - 16
                        ins = gpsimd.dma_start(
                            out=w2bv_pa[:, 2 * r:2 * r + 2, :],
                            in_=wcbv[:, 2 * (c % 2):2 * (c % 2) + 2, :])
                    sWSt[c % 2].inc(ins, 16)
                for c in range(NWC):
                    if c >= 2:
                        gpsimd.wait_ge(sWC.h, c - 1)
                        _wstore(c - 2)
                    else:
                        gpsimd.wait_ge(sT.h, TA2)
                    if c < 16:
                        p_, h_ = c // 2, c % 2
                        ins = gpsimd.dma_start(
                            out=stg[:, 2048 * (c % 2):2048 * (c % 2 + 1)],
                            in_=w1T[128 * p_:128 * (p_ + 1), 2048 * h_:2048 * (h_ + 1)])
                    else:
                        r = c - 16
                        ins = gpsimd.dma_start(
                            out=stgv3[:, 2 * (c % 2):2 * (c % 2) + 2, :],
                            in_=w2v_pa[:, 2 * r:2 * r + 2, :])
                    sWL[c % 2].inc(ins, 16)
                for c in (NWC - 2, NWC - 1):
                    gpsimd.wait_ge(sWC.h, c + 1)
                    _wstore(c)
            block.gpsimd(pa3_gpsimd_conv)
            WST_A = [sWSt[0].n, sWSt[1].n]

            _wc_next = [0]

            def _wc_cast(scalar, budget):
                for _ in range(budget):
                    c = _wc_next[0]
                    if c >= NWC:
                        return
                    _wc_next[0] = c + 1
                    scalar.wait_ge(sWL[c % 2].h, 16 * (c // 2 + 1))
                    if c >= 2:
                        scalar.wait_ge(sWSt[c % 2].h, 16 * (c // 2))
                    ins = scalar.copy(
                        out=wcb[:, 2048 * (c % 2):2048 * (c % 2 + 1)],
                        in_=stg[:, 2048 * (c % 2):2048 * (c % 2 + 1)])
                    sWC.inc(ins)

            def pa3_sync(sync):
                sync.wait_ge(sV.h, VA4)
                for s in range(NT + 1):
                    if s >= 2:
                        sync.wait_ge(sV.h, v_n(s - 2, 5))
                    src = x_t0 if s == 0 else x_own[128 * (s - 1):128 * s, :]
                    sR[s % 2].inc(sync.dma_start(out=wf1[s % 2][:, :], in_=src), 16)
                    if s >= 1:
                        sync.wait_ge(sV.h, v_n(s - 1, 8))
                        sP[(s - 1) % 2].inc(sync.dma_start(
                            out=sp_x1[128 * (NT if s - 1 == 0 else s - 2):
                                      128 * (NT + 1 if s - 1 == 0 else s - 1), :],
                            in_=wf0[(s - 1) % 2][:, :]), 16)
                sync.wait_ge(sV.h, v_n(NT, 8))
                sP[NT % 2].inc(sync.dma_start(out=sp_x1[128 * (NT - 1):128 * NT, :],
                                              in_=wf0[NT % 2][:, :]), 16)
            block.sync(pa3_sync)
            PA3_P0, PA3_P1 = sP[0].n, sP[1].n
            PA3_R0, PA3_R1 = sR[0].n, sR[1].n

            def pa3_tensor(tensor):
                tensor.wait_ge(sV.h, VA4)
                for s in range(NT + 1):
                    A = (s % 3) * 2
                    if s >= 3:
                        tensor.wait_ge(sV.h, v_n(s - 3, 5))
                    for p in range(NDP):
                        for nch in range(2):
                            ins = tensor.matmul(
                                out=pb[A + nch][:, :],
                                lhsT=lhsrc(s, p),
                                rhs=wzb(p)[:, 512 * nch:512 * (nch + 1)],
                                start=(p == 0), stop=(p == NDP - 1))
                    sT.inc(ins)
                    tensor.wait_ge(sV.h, v_n(s, 3))
                    if s >= 1:
                        tensor.wait_ge(sS.h, s_n(s - 1, 8))
                    for j in range(NDP):
                        ins = tensor.transpose(out=ptb[:, 128 * j:128 * (j + 1)],
                                               in_=xbft[:, 128 * j:128 * (j + 1)],
                                               identity=idn[:])
                    sT.inc(ins)
                    tensor.wait_ge(sS.h, s_n(s, 2))
                    for p in range(NDP):
                        for nch in range(2):
                            ins = tensor.matmul(
                                out=pb[A + nch][:, :],
                                lhsT=gTt[:, 128 * p:128 * (p + 1)],
                                rhs=woutb(p)[:, 512 * nch:512 * (nch + 1)],
                                start=(p == 0), stop=(p == NDP - 1))
                    sT.inc(ins)
                    tensor.wait_ge(sS.h, s_n(s, 7))
                    for j in range(NDP):
                        ins = tensor.transpose(out=ptb[:, 128 * j:128 * (j + 1)],
                                               in_=xbft[:, 128 * j:128 * (j + 1)],
                                               identity=idn[:])
                    sT.inc(ins)
            block.tensor(pa3_tensor)

            def pa3_vector(vector):
                for s in range(NT + 1):
                    A = (s % 3) * 2
                    vector.wait_ge(sT.h, t_n(s, 1))
                    if s >= 2:
                        vector.wait_ge(sS.h, s_n(s - 2, 8))
                        vector.wait_ge(sP[s % 2].h, 16 * (s // 2))
                    sV.inc(vector.scalar_tensor_tensor(out=wf0[s % 2][:, 0:512],
                                                       in0=pb[A][:, :],
                                                       scalar=1.0, in1=bzs[:, 0:512],
                                                       op0=ALU.mult, op1=ALU.add))
                    sV.inc(vector.scalar_tensor_tensor(out=wf0[s % 2][:, 512:1024],
                                                       in0=pb[A + 1][:, :], scalar=1.0,
                                                       in1=bzs[:, 512:1024],
                                                       op0=ALU.mult, op1=ALU.add))
                    vector.wait_ge(sS.h, s_n(s, 1))
                    if s >= 1:
                        vector.wait_ge(sT.h, t_n(s - 1, 4))
                    ycol = ycols[:, 32:33] if s == 0 else ycols[:, s - 1:s]
                    sV.inc(vector.tensor_scalar(out=xbft[:, :], in0=wf0[s % 2][:, :],
                                                scalar1=ycol, scalar2=None, op0=ALU.mult))
                    vector.wait_ge(sT.h, t_n(s, 3))
                    vector.wait_ge(sR[s % 2].h, 16 * (s // 2 + 1))
                    vector.scalar_tensor_tensor(out=wf0[s % 2][:, 0:512], in0=pb[A][:, :],
                                                scalar=1.0, in1=wf1[s % 2][:, 0:512],
                                                op0=ALU.mult, op1=ALU.add)
                    sV.inc(vector.scalar_tensor_tensor(out=wf0[s % 2][:, 512:1024],
                                                       in0=pb[A + 1][:, :], scalar=1.0,
                                                       in1=wf1[s % 2][:, 512:1024],
                                                       op0=ALU.mult, op1=ALU.add))
                    vector.wait_ge(sV.h, sV.n)
                    vector.tensor_tensor(out=wf0[s % 2][:, 0:512], in0=wf0[s % 2][:, 0:512],
                                         in1=bouts[:, 0:512], op=ALU.add)
                    sV.inc(vector.tensor_tensor(out=wf0[s % 2][:, 512:1024],
                                                in0=wf0[s % 2][:, 512:1024],
                                                in1=bouts[:, 512:1024], op=ALU.add))
                    vector.wait_ge(sV.h, sV.n)
                    sV.inc(vector.tensor_reduce(out=stat[:, 0:1], in_=wf0[s % 2][:, :],
                                                axis=AX, op=ALU.add))
                    vector.wait_ge(sS.h, s_n(s, 6))
                    sV.inc(vector.reciprocal(out=stat[:, 4:5], in_=stat[:, 3:4]))
                    vector.wait_ge(sV.h, sV.n)
                    sV.inc(vector.tensor_scalar(out=wf0[s % 2][:, :], in0=wf0[s % 2][:, :],
                                                scalar1=stat[:, 4:5], scalar2=None,
                                                op0=ALU.mult))
            block.vector(pa3_vector)

            def pa3_scalar(scalar):
                for s in range(NT + 1):
                    _wc_cast(scalar, 2)
                    scalar.wait_ge(sV.h, v_n(s, 2))
                    sS.inc(scalar.activation(out=wf0[s % 2][:, :], in_=wf0[s % 2][:, :],
                                             func=ACT.Gelu))
                    scalar.wait_ge(sT.h, t_n(s, 2))
                    if s >= 1:
                        scalar.wait_ge(sT.h, t_n(s - 1, 3))
                    sS.inc(scalar.copy(out=gTt[:, :], in_=ptb[:, :]))
                    scalar.wait_ge(sV.h, v_n(s, 6))
                    sS.inc(scalar.mul(out=stat[:, 1:2], in_=stat[:, 0:1], mul=-1.0 / D))
                    scalar.wait_ge(sS.h, sS.n)
                    sS.inc(scalar.activation(out=wf0[s % 2][:, :], in_=wf0[s % 2][:, :],
                                             func=ACT.Identity, bias=stat[:, 1:2], scale=1.0))
                    scalar.wait_ge(sS.h, sS.n)
                    sS.inc(scalar.activation(out=dmy[:, :], in_=wf0[s % 2][:, :],
                                             func=ACT.Square, accum_out=stat[:, 2:3]))
                    scalar.wait_ge(sS.h, sS.n)
                    sS.inc(scalar.activation(out=stat[:, 3:4], in_=stat[:, 2:3],
                                             func=ACT.Sqrt, scale=1.0 / D,
                                             bias=stat[:, 5:6]))
                    scalar.wait_ge(sV.h, v_n(s, 8))
                    sS.inc(scalar.copy(out=xbft[:, :], in_=wf0[s % 2][:, :]))
                    scalar.wait_ge(sT.h, t_n(s, 4))
                    if s == 0:
                        sS.inc(scalar.copy(out=x1t0[:, :], in_=wf0[0][:, :]))
                    else:
                        ov = XT.rearrange("q (p c) -> q p c", c=TOK)
                        sS.inc(scalar.copy(
                            out=ov[:, :, 128 * (s - 1):128 * s],
                            in_=ptb.rearrange("q (p c) -> q p c", c=128)[:, :, :]))
                _wc_cast(scalar, NWC)
            block.scalar(pa3_scalar)
            PA3_V, PA3_S, PA3_T = sV.n, sS.n, sT.n


        if PHASES >= 2:
            pbCtx = contextlib.ExitStack()
            with pbCtx:
                EB = pbCtx.enter_context
                w1TbBuf = EB(nc.sbuf_tensor("w1TbBuf", [128, NDP * 4096], BF16))
                hT = EB(nc.sbuf_tensor("hT", [128, 32 * 256], BF16))
                w2c = [EB(nc.sbuf_tensor(f"w2c{i}", [128, 1024], BF16)) for i in range(2)]
                b2s = EB(nc.sbuf_tensor("b2s", [128, D], F32))
                evodf = EB(nc.sbuf_tensor("evodf", [4, 4096], F32))
                ev1b = EB(nc.sbuf_tensor("ev1b", [4, D], BF16))
                ev4b = EB(nc.sbuf_tensor("ev4b", [4, D], BF16))
                od1b = EB(nc.sbuf_tensor("od1b", [4, D], BF16))
                od4b = EB(nc.sbuf_tensor("od4b", [4, D], BF16))
                ev1M = EB(nc.sbuf_tensor("ev1M", [4, HF], BF16))
                ErB = EB(nc.sbuf_tensor("ErB", [4, 4096], BF16))
                odT = EB(nc.sbuf_tensor("odT", [128, 64], BF16))
                u1b = EB(nc.sbuf_tensor("u1b", [4, 256], BF16))
                vtb = EB(nc.sbuf_tensor("vtb", [4, 1024], BF16))
                scls = EB(nc.sbuf_tensor("scls", [128, 1], F32))
                b1cs = EB(nc.sbuf_tensor("b1cs", [128, 32], F32))

                def w1s(ds, hs):
                    return w1TbBuf[:, 4096 * ds + 128 * hs:4096 * ds + 128 * (hs + 1)]

                SB_PA3 = PA3_S

                def pb_gpsimd_w(gpsimd):
                    gpsimd.wait_ge(sV.h, VA4)
                    sKB.inc(gpsimd.dma_start(out=b2s[:], in_=b2_bc), 16)
                    sKB.inc(gpsimd.dma_start(out=scls[:], in_=scale_bc), 16)
                    sKB.inc(gpsimd.dma_start(out=b1cs[:, :], in_=b1c), 16)
                    gpsimd.wait_ge(sT.h, PA3_T)
                    gpsimd.wait_ge(sWSt[0].h, WST_A[0])
                    gpsimd.wait_ge(sWSt[1].h, WST_A[1])
                    for p in range(NDP):
                        sW1L.inc(gpsimd.dma_start(
                            out=w1TbBuf[:, 4096 * p:4096 * (p + 1)],
                            in_=w1bf[128 * p:128 * (p + 1), :]), 16)
                block.gpsimd(pb_gpsimd_w)
                W1L = sW1L.n

                def pb_gpsimd_lora(gpsimd):
                    gpsimd.wait_ge(sS.h, SA4 + 8)
                    for j in range(4):
                        sLT.inc(gpsimd.dma_start(out=evodf[j:j + 1, 0:1024],
                                                 in_=x1t0[1 + 2 * j:2 + 2 * j, :]), 16)
                        sLT.inc(gpsimd.dma_start(out=evodf[j:j + 1, 1024:2048],
                                                 in_=x1t0[9 + 2 * j:10 + 2 * j, :]), 16)
                        sLT.inc(gpsimd.dma_start(out=evodf[j:j + 1, 2048:3072],
                                                 in_=x1t0[2 + 2 * j:3 + 2 * j, :]), 16)
                        sLT.inc(gpsimd.dma_start(out=evodf[j:j + 1, 3072:4096],
                                                 in_=x1t0[10 + 2 * j:11 + 2 * j, :]), 16)
                block.gpsimd(pb_gpsimd_lora)
                GB4 = sLT.n

                def pb_scalar_lora(scalar):
                    scalar.wait_ge(sLT.h, GB4)
                    sS.inc(scalar.memzero(ev1M[:, :]))
                    sS.inc(scalar.memzero(ErB[:, :]))
                    sS.inc(scalar.copy(out=ev1b[:, :], in_=evodf[:, 0:1024]))
                    sS.inc(scalar.copy(out=ev4b[:, :], in_=evodf[:, 1024:2048]))
                    sS.inc(scalar.copy(out=od1b[:, :], in_=evodf[:, 2048:3072]))
                    sS.inc(scalar.copy(out=od4b[:, :], in_=evodf[:, 3072:4096]))
                block.scalar(pb_scalar_lora)
                SB_L1 = sS.n

                def pb_gpsimd_lora2(gpsimd):
                    gpsimd.wait_ge(sS.h, SB_L1)
                    for j in range(4):
                        sLT.inc(gpsimd.dma_start(out=ev1M[j:j + 1, 1024 * j:1024 * (j + 1)],
                                                 in_=ev1b[j:j + 1, :]), 16)
                    for r in range(4):
                        for j in range(4):
                            sLT.inc(gpsimd.dma_start(
                                out=ErB[j:j + 1, 1024 * r + 256 * j:1024 * r + 256 * (j + 1)],
                                in_=ev4b[j:j + 1, r:1024:4],
                                allow_slow_non_contiguous=True), 16)
                block.gpsimd(pb_gpsimd_lora2)
                GB5 = sLT.n

                def pb_tensor_lora(tensor):
                    tensor.wait_ge(sS.h, SB_L1)
                    tensor.wait_ge(sS.h, SB_PA3)
                    for ds in range(NDP):
                        if ds > 0:
                            tensor.wait_ge(sS.h, SB_L1 + 2 * ds)
                        ins = tensor.transpose(out=ptb[:, 0:4], in_=od1b[:, 128 * ds:128 * (ds + 1)],
                                               identity=idn[0:4, 0:4])
                        sT.inc(ins)
                        ins = tensor.transpose(out=ptb[:, 4:8], in_=od4b[:, 128 * ds:128 * (ds + 1)],
                                               identity=idn[0:4, 0:4])
                        sT.inc(ins)
                block.tensor(pb_tensor_lora)
                TB_L = sT.n

                def pb_scalar_lora2(scalar):
                    for ds in range(NDP):
                        scalar.wait_ge(sT.h, TB_L - (2 * (NDP - 1 - ds) + 1))
                        sS.inc(scalar.copy(out=odT[:, 4 * ds:4 * (ds + 1)], in_=ptb[:, 0:4]))
                        scalar.wait_ge(sT.h, TB_L - 2 * (NDP - 1 - ds))
                        sS.inc(scalar.copy(out=odT[:, 32 + 4 * ds:32 + 4 * (ds + 1)],
                                           in_=ptb[:, 4:8]))
                block.scalar(pb_scalar_lora2)
                SB_L2 = sS.n

                TB0, VB0, SB0 = sT.n, sV.n, sS.n
                TPB, VPB, SPB = 70, 10, 47

                def bt_n(g, k):
                    return TB0 + TPB * g + k

                def bv_n(g, k):
                    return VB0 + VPB * g + k

                def bs_n(g, k):
                    return SB0 + SPB * g + k

                def bl_slab(g, hs):
                    return 16 * (16 * g + hs // 2 + 1)

                def pb_sync(sync):
                    sync.wait_ge(sP[0].h, PA3_P0)
                    sync.wait_ge(sP[1].h, PA3_P1)
                    for g in range(8):
                        for tt in range(2):
                            i = 2 * g + tt
                            if g >= 1:
                                sync.wait_ge(sV.h, bv_n(g - 1, 5 * tt + 5))
                            sR[tt].inc(sync.dma_start(out=wf1[tt][:, :],
                                                      in_=sp_x1[128 * i:128 * (i + 1), :]), 16)
                        for tt in range(2):
                            sync.wait_ge(sV.h, bv_n(g, 5 * tt + 5))
                            sP[tt].inc(sync.dma_start(out=sp_x2[256 * g + 128 * tt:
                                                                256 * g + 128 * (tt + 1), :],
                                                      in_=wf0[tt][:, :]), 16)
                block.sync(pb_sync)
                PB_R0, PB_P0, PB_P1 = sR[0].n, sP[0].n, sP[1].n

                def pb_tensor(tensor):
                    tensor.wait_ge(sW1L.h, W1L)
                    tensor.wait_ge(sLT.h, GB5)
                    tensor.wait_ge(sS.h, SB_L2)
                    for g in range(8):
                        if g >= 1:
                            tensor.wait_ge(sS.h, bs_n(g - 1, 35))
                        for ds in range(NDP):
                            ins = tensor.matmul(out=psm[0:4, 0:256], lhsT=odT[:, 4 * ds:4 * (ds + 1)],
                                                rhs=XT[:, 2048 * ds + 256 * g:2048 * ds + 256 * (g + 1)],
                                                start=(ds == 0), stop=(ds == NDP - 1),
                                                skip_group_check=True)
                        sT.inc(ins)
                        tensor.wait_ge(sS.h, bs_n(g, 1))
                        if g >= 1:
                            tensor.wait_ge(sT.h, bt_n(g - 1, 68))
                        for hs in range(32):
                            bank = pb[hs % 2]
                            if hs >= 2:
                                tensor.wait_ge(sS.h, bs_n(g, hs))
                            for ds in range(NDP):
                                tensor.matmul(out=bank[:, 0:256], lhsT=w1s(ds, hs),
                                              rhs=XT[:, 2048 * ds + 256 * g:2048 * ds + 256 * (g + 1)],
                                              start=(ds == 0), stop=False)
                            ins = tensor.matmul(out=bank[:, 0:256],
                                                lhsT=ev1M[0:4, 128 * hs:128 * (hs + 1)],
                                                rhs=u1b[0:4, :], start=False, stop=True)
                            sT.inc(ins)
                        tensor.wait_ge(sS.h, bs_n(g, 33))
                        for tt in range(2):
                            if g >= 1 or tt == 1:
                                tensor.wait_ge(sS.h, bs_n(g - 1 + tt, 35 - tt))
                            for r in range(4):
                                for ds in range(NDP):
                                    ins = tensor.matmul(
                                        out=psm[0:4, 128 * r:128 * (r + 1)],
                                        lhsT=odT[:, 32 + 4 * ds:32 + 4 * (ds + 1)],
                                        rhs=hT[:, 256 * (8 * r + ds) + 128 * tt:
                                               256 * (8 * r + ds) + 128 * (tt + 1)],
                                        start=(ds == 0), stop=(ds == NDP - 1),
                                        skip_group_check=True)
                            sT.inc(ins)
                        if g >= 1:
                            tensor.wait_ge(sV.h, bv_n(g - 1, 10))
                        for hs in range(32):
                            tensor.wait_ge(sL[hs % 2].h, bl_slab(g, hs))
                            for tt in range(2):
                                for nc2 in range(2):
                                    ins = tensor.matmul(
                                        out=pb[2 + 2 * tt + nc2][:, :],
                                        lhsT=hT[:, 256 * hs + 128 * tt:256 * hs + 128 * (tt + 1)],
                                        rhs=w2c[hs % 2][:, 512 * nc2:512 * (nc2 + 1)],
                                        start=(hs == 0), stop=False)
                            sT.inc(ins)
                        tensor.wait_ge(sS.h, bs_n(g, 35))
                        for tt in range(2):
                            for nc2 in range(2):
                                for r in range(4):
                                    ins = tensor.matmul(
                                        out=pb[2 + 2 * tt + nc2][:, :],
                                        lhsT=vtb[0:4, 512 * tt + 128 * r:512 * tt + 128 * (r + 1)],
                                        rhs=ErB[0:4, 1024 * r + 512 * nc2:1024 * r + 512 * (nc2 + 1)],
                                        start=False, stop=(r == 3))
                        sT.inc(ins)
                        for tt in range(2):
                            tensor.wait_ge(sS.h, bs_n(g, 40 + 6 * tt))
                            if g >= 1 or tt == 1:
                                tensor.wait_ge(sS.h, bs_n(g - 1 + tt, 47 - 6 * tt))
                            for j in range(NDP):
                                ins = tensor.transpose(out=ptb[:, 128 * j:128 * (j + 1)],
                                                       in_=xbft[:, 128 * j:128 * (j + 1)],
                                                       identity=idn[:])
                            sT.inc(ins)
                block.tensor(pb_tensor)

                CC_A = sC.n + 1

                def pb_gpsimd_main(gpsimd):
                    gpsimd.wait_ge(sLT.h, GB5)
                    gpsimd.wait_ge(sWSt[0].h, WST_A[0])
                    gpsimd.wait_ge(sWSt[1].h, WST_A[1])
                    for g in range(8):
                        for hs in range(32):
                            if hs >= 2:
                                gpsimd.wait_ge(sT.h, bt_n(g, 36 + hs - 2))
                            elif g >= 1:
                                gpsimd.wait_ge(sT.h, bt_n(g - 1, 68))
                            sL[hs % 2].inc(gpsimd.dma_start(
                                out=w2c[hs % 2][:, :],
                                in_=w2bv_pa[:, hs, :]), 16)
                        for tt in range(2):
                            i = 2 * g + tt
                            gpsimd.wait_ge(sS.h, bs_n(g, 41 + 6 * tt))
                            dst = cc_x2_in_a if i < 8 else cc_x2_in_b
                            col = 128 * (i % 8)
                            ov = dst.ap().rearrange("(j r) t -> r j t", r=128)
                            iv = xbft.rearrange("r (j c) -> r j c", c=128)
                            sem = sXSa[tt] if i < 8 else sXS[tt]
                            sem.inc(gpsimd.dma_start(out=ov[:, :, col:col + 128],
                                                     in_=iv[:, :, :]), 16)
                        if g == 3:
                            gpsimd.wait_ge(sXSa[0].h, sXSa[0].n)
                            gpsimd.wait_ge(sXSa[1].h, sXSa[1].n)
                            sC.inc(gpsimd.collective_compute(
                                "AllGather", ALU.bypass, replica_groups=RG,
                                ins=[cc_x2_in_a.ap()], outs=[cc_x2_out_a.ap()]))
                block.gpsimd(pb_gpsimd_main)

                def pb_vector(vector):
                    vector.wait_ge(sKB.h, 48)
                    for g in range(8):
                        for tt in range(2):
                            vector.wait_ge(sT.h, bt_n(g, 68))
                            vector.wait_ge(sR[tt].h, (PA3_R0 if tt == 0 else PA3_R1)
                                           + 16 * (g + 1))
                            if g >= 1:
                                vector.wait_ge(sP[tt].h, (PA3_P0 if tt == 0 else PA3_P1)
                                               + 16 * g)
                            b0, b1 = pb[2 + 2 * tt], pb[3 + 2 * tt]
                            vector.scalar_tensor_tensor(out=wf0[tt][:, 0:512], in0=b0[:, :],
                                                        scalar=1.0, in1=wf1[tt][:, 0:512],
                                                        op0=ALU.mult, op1=ALU.add)
                            sV.inc(vector.scalar_tensor_tensor(out=wf0[tt][:, 512:1024],
                                                               in0=b1[:, :],
                                                               scalar=1.0, in1=wf1[tt][:, 512:1024],
                                                               op0=ALU.mult, op1=ALU.add))
                            vector.wait_ge(sV.h, sV.n)
                            vector.tensor_tensor(out=wf0[tt][:, 0:512], in0=wf0[tt][:, 0:512],
                                                 in1=b2s[:, 0:512], op=ALU.add)
                            sV.inc(vector.tensor_tensor(out=wf0[tt][:, 512:1024],
                                                        in0=wf0[tt][:, 512:1024],
                                                        in1=b2s[:, 512:1024], op=ALU.add))
                            vector.wait_ge(sV.h, sV.n)
                            sV.inc(vector.tensor_reduce(out=stat[:, 0:1], in_=wf0[tt][:, :],
                                                        axis=AX, op=ALU.add))
                            vector.wait_ge(sS.h, bs_n(g, 39 + 6 * tt))
                            sV.inc(vector.reciprocal(out=stat[:, 4:5], in_=stat[:, 3:4]))
                            vector.wait_ge(sV.h, sV.n)
                            sV.inc(vector.tensor_scalar(out=wf0[tt][:, :], in0=wf0[tt][:, :],
                                                        scalar1=stat[:, 4:5], scalar2=None,
                                                        op0=ALU.mult))
                block.vector(pb_vector)

                def pb_scalar_main(scalar):
                    scalar.wait_ge(sKB.h, 48)
                    for g in range(8):
                        scalar.wait_ge(sT.h, bt_n(g, 1))
                        sS.inc(scalar.activation(out=u1b[:, :], in_=psm[0:4, 0:256],
                                                 func=ACT.Identity, scale=scls[0:4, 0:1]))
                        for hs in range(32):
                            scalar.wait_ge(sT.h, bt_n(g, 1 + hs + 1))
                            sS.inc(scalar.activation(out=hT[:, 256 * hs:256 * (hs + 1)],
                                                     in_=pb[hs % 2][:, 0:256], func=ACT.Gelu,
                                                     bias=b1cs[:, hs:hs + 1], scale=1.0))
                        for tt in range(2):
                            scalar.wait_ge(sT.h, bt_n(g, 34 + tt))
                            sS.inc(scalar.activation(out=vtb[:, 512 * tt:512 * (tt + 1)],
                                                     in_=psm[0:4, 0:512],
                                                     func=ACT.Identity, scale=scls[0:4, 0:1]))
                        for tt in range(2):
                            scalar.wait_ge(sV.h, bv_n(g, 5 * tt + 3))
                            sS.inc(scalar.mul(out=stat[:, 1:2], in_=stat[:, 0:1], mul=-1.0 / D))
                            scalar.wait_ge(sS.h, sS.n)
                            sS.inc(scalar.activation(out=wf0[tt][:, :], in_=wf0[tt][:, :],
                                                     func=ACT.Identity, bias=stat[:, 1:2],
                                                     scale=1.0))
                            scalar.wait_ge(sS.h, sS.n)
                            sS.inc(scalar.activation(out=dmy[:, :], in_=wf0[tt][:, :],
                                                     func=ACT.Square, accum_out=stat[:, 2:3]))
                            scalar.wait_ge(sS.h, sS.n)
                            sS.inc(scalar.activation(out=stat[:, 3:4], in_=stat[:, 2:3],
                                                     func=ACT.Sqrt, scale=1.0 / D,
                                                     bias=stat[:, 5:6]))
                            scalar.wait_ge(sV.h, bv_n(g, 5 * tt + 5))
                            if g > 0 or tt == 1:
                                if 2 * g + tt <= 8:
                                    scalar.wait_ge(sXSa[1 - tt].h, 16 * (g + tt))
                                else:
                                    scalar.wait_ge(sXS[1 - tt].h, 16 * (g + tt - 4))
                            sS.inc(scalar.copy(out=xbft[:, :], in_=wf0[tt][:, :]))
                            scalar.wait_ge(sT.h, bt_n(g, 69 + tt))
                            sS.inc(scalar.copy(out=xbft[:, :], in_=ptb[:, :]))
                block.scalar(pb_scalar_main)
                PB_V, PB_S, PB_T = sV.n, sS.n, sT.n



        if PHASES >= 3:
          pdC = contextlib.ExitStack()
          with pdC:
            pcC = contextlib.ExitStack()
            EC = pcC.enter_context
            qTs = EC(nc.sbuf_tensor("qTs", [128, 4 * S], BF16))
            kTs = EC(nc.sbuf_tensor("kTs", [128, 4 * S], BF16))
            vAll = EC(nc.sbuf_tensor("vAll", [128, NTF * 520], BF16))
            mparts = EC(nc.sbuf_tensor("mparts", [128, 16], F32))
            mcol = EC(nc.sbuf_tensor("mcol", [128, 2], F32))
            cvb = EC(nc.sbuf_tensor("cvb", [128, 2048], BF16))
            sKC = Sem(E(nc.semaphore("sKC")))
            sQG = [Sem(E(nc.semaphore(f"sQG{i}"))) for i in range(2)]
            sHQ = Sem(E(nc.semaphore("sHQ")))
            sYG = [Sem(E(nc.semaphore(f"sYG{i}"))) for i in range(2)]

            def pc_gpsimd_cc(gpsimd):
                gpsimd.wait_ge(sXS[0].h, sXS[0].n)
                gpsimd.wait_ge(sXS[1].h, sXS[1].n)
                sC.inc(gpsimd.collective_compute(
                    "AllGather", ALU.bypass, replica_groups=RG,
                    ins=[cc_x2_in_b.ap()], outs=[cc_x2_out_b.ap()]))
            block.gpsimd(pc_gpsimd_cc)
            CC1 = sC.n

            KC1 = sKC.n

            qkvC = contextlib.ExitStack()
            with qkvC:
                EQ = qkvC.enter_context
                wqkb = EQ(nc.sbuf_tensor("wqkb", [128, NDP * 1024], BF16))
                x2cb = EQ(nc.sbuf_tensor("x2cb", [128, NDP * 512], BF16))
                x2c = [x2cb, x2cb]
                costt = EQ(nc.sbuf_tensor("costt", [128, NTF * 32], BF16))
                sintt = EQ(nc.sbuf_tensor("sintt", [128, NTF * 32], BF16))
                rsc = [EQ(nc.sbuf_tensor(f"rsc{i}", [128, 256], F32)) for i in range(2)]
                qrop = EQ(nc.sbuf_tensor("qrop", [128, 512], BF16))

                def pc_sync_qkw(sync):
                    sync.wait_ge(sKC.h, KC1)
                    sKC.inc(sync.dma_start(out=costt[:], in_=cos_t), 16)
                    sKC.inc(sync.dma_start(out=sintt[:], in_=sin_t), 16)
                    SQ0 = sS.n
                    for p in range(NDP):
                        sync.wait_ge(sS.h, SQ0 + p)
                        sX.inc(sync.dma_start(out=stg[:, 0:1024],
                                              in_=wqkv[128 * p:128 * (p + 1), 0:1024]), 16)
                block.sync(pc_sync_qkw)
                XQ1 = sX.n
                SQC0 = sS.n

                def pc_scalar_qkw(scalar):
                    scalar.wait_ge(sS.h, SQC0)
                    for p in range(NDP):
                        scalar.wait_ge(sX.h, XQ1 - 16 * (NDP - 1 - p))
                        scalar.mul(out=wqkb[:, 1024 * p:1024 * p + 512],
                                   in_=stg[:, 0:512], mul=0.125)
                        sS.inc(scalar.copy(out=wqkb[:, 1024 * p + 512:1024 * (p + 1)],
                                           in_=stg[:, 512:1024]))
                block.scalar(pc_scalar_qkw)
                SQC1 = sS.n

                TQ0, VQ0, SQ1 = sT.n, sV.n, sS.n
                TPQ, VPQ, SPQ = 4 * 4, 4 * 14, 4 * 3

                def tq_n(tc, k):
                    return TQ0 + TPQ * tc + k

                def vq_n(tc, k):
                    return VQ0 + VPQ * tc + k

                def sq_n(tc, k):
                    return SQ1 + SPQ * tc + k

                def x2src(tc, p):
                    slab = tc // 4
                    t = (cc_x2_out_a if tc % 4 < 2 else cc_x2_out_b).ap()
                    return t[1024 * slab + 128 * p:1024 * slab + 128 * (p + 1),
                             512 * (tc % 2):512 * (tc % 2 + 1)]

                TCA = [0, 1, 4, 5, 2, 3, 6, 7]

                def pc_gpsimd_x2(gpsimd):
                    for ci, tc in enumerate(TCA):
                        gpsimd.wait_ge(sC.h, CC_A if tc % 4 < 2 else CC1)
                        if ci >= 1:
                            gpsimd.wait_ge(sT.h, tq_n(ci - 1, 14))
                        for p in range(NDP):
                            sQG[ci % 2].inc(gpsimd.dma_start(
                                out=x2c[ci % 2][:, 512 * p:512 * (p + 1)],
                                in_=x2src(tc, p)), 16)
                block.gpsimd(pc_gpsimd_x2)

                def pc_tensor_qk(tensor):
                    tensor.wait_ge(sS.h, SQC1)
                    tensor.wait_ge(sV.h, PB_V)
                    for ci, tc in enumerate(TCA):
                        tensor.wait_ge(sQG[ci % 2].h, 128 * (ci // 2 + 1))
                        for ts_ in range(4):
                            kt = 4 * tc + ts_
                            if ci > 0 or ts_ > 0:
                                tensor.wait_ge(sV.h, vq_n(ci, 14 * ts_) if ts_ > 0
                                               else vq_n(ci - 1, 14 * 4))
                            for p in range(NDP):
                                tensor.matmul(out=pb[0][:, :],
                                              lhsT=x2c[ci % 2][:, 512 * p + 128 * ts_:
                                                               512 * p + 128 * (ts_ + 1)],
                                              rhs=wqkb[:, 1024 * p:1024 * p + 512],
                                              start=(p == 0), stop=(p == NDP - 1))
                            for p in range(NDP):
                                ins = tensor.matmul(out=pb[1][:, :],
                                                    lhsT=x2c[ci % 2][:, 512 * p + 128 * ts_:
                                                                     512 * p + 128 * (ts_ + 1)],
                                                    rhs=wqkb[:, 1024 * p + 512:1024 * (p + 1)],
                                                    start=(p == 0), stop=(p == NDP - 1))
                            sT.inc(ins, 2)
                            tensor.wait_ge(sV.h, vq_n(ci, 14 * ts_ + 7))
                            if ci > 0 or ts_ > 0:
                                tensor.wait_ge(sS.h, sq_n(ci, 3 * ts_) if ts_ > 0
                                               else sq_n(ci - 1, 12))
                            for j in range(4):
                                ins = tensor.transpose(out=ptb[:, 128 * j:128 * (j + 1)],
                                                       in_=qrop[:, 128 * j:128 * (j + 1)],
                                                       identity=idn[:])
                            sT.inc(ins)
                            tensor.wait_ge(sV.h, vq_n(ci, 14 * ts_ + 14))
                            tensor.wait_ge(sS.h, sq_n(ci, 3 * ts_ + 1))
                            for j in range(4):
                                ins = tensor.transpose(out=ptb[:, 512 + 128 * j:512 + 128 * (j + 1)],
                                                       in_=qrop[:, 128 * j:128 * (j + 1)],
                                                       identity=idn[:])
                            sT.inc(ins)
                block.tensor(pc_tensor_qk)

                def rope_ops(vector, src_bank, kt, out_buf, n0):
                    pv = src_bank.rearrange("t (h c) -> t h c", c=64)
                    ov = out_buf.rearrange("t (h c) -> t h c", c=64)
                    cosb = costt[:, 32 * kt:32 * (kt + 1)].rearrange(
                        "t (h c) -> t h c", h=1).broadcast_to([128, 8, 32])
                    sinb = sintt[:, 32 * kt:32 * (kt + 1)].rearrange(
                        "t (h c) -> t h c", h=1).broadcast_to([128, 8, 32])
                    r0 = rsc[0].rearrange("t (h c) -> t h c", c=32)
                    r1 = rsc[1].rearrange("t (h c) -> t h c", c=32)
                    sV.inc(vector.tensor_tensor(out=r0[:, :, :], in0=pv[:, :, 0:32],
                                                in1=cosb, op=ALU.mult))
                    sV.inc(vector.tensor_tensor(out=r1[:, :, :], in0=pv[:, :, 32:64],
                                                in1=sinb, op=ALU.mult))
                    vector.wait_ge(sV.h, sV.n)
                    sV.inc(vector.tensor_tensor(out=ov[:, :, 0:32], in0=r0[:, :, :],
                                                in1=r1[:, :, :], op=ALU.subtract))
                    vector.wait_ge(sV.h, sV.n)
                    sV.inc(vector.tensor_tensor(out=r0[:, :, :], in0=pv[:, :, 0:32],
                                                in1=sinb, op=ALU.mult))
                    sV.inc(vector.tensor_tensor(out=r1[:, :, :], in0=pv[:, :, 32:64],
                                                in1=cosb, op=ALU.mult))
                    vector.wait_ge(sV.h, sV.n)
                    sV.inc(vector.tensor_tensor(out=ov[:, :, 32:64], in0=r0[:, :, :],
                                                in1=r1[:, :, :], op=ALU.add))
                    vector.wait_ge(sV.h, sV.n)
                    sV.inc(vector.engine_nop())
                    return

                def pc_vector_rope(vector):
                    vector.wait_ge(sKC.h, KC1 + 32)
                    for ci, tc in enumerate(TCA):
                        for ts_ in range(4):
                            kt = 4 * tc + ts_
                            vector.wait_ge(sT.h, tq_n(ci, 4 * ts_ + 2))
                            rope_ops(vector, pb[0], kt, qrop, 0)
                            vector.wait_ge(sT.h, tq_n(ci, 4 * ts_ + 3))
                            rope_ops(vector, pb[1], kt, qrop, 7)
                block.vector(pc_vector_rope)

                def pc_scalar_qev(scalar):
                    for ci, tc in enumerate(TCA):
                        for ts_ in range(4):
                            kt = 4 * tc + ts_
                            scalar.wait_ge(sT.h, tq_n(ci, 4 * ts_ + 3))
                            ovq = qTs.rearrange("r (j c) -> r j c", c=4096)
                            sS.inc(scalar.copy(
                                out=ovq[:, :, 128 * kt:128 * (kt + 1)],
                                in_=ptb.rearrange("r (j c) -> r j c", c=128)[:, 0:4, :]))
                            scalar.wait_ge(sT.h, tq_n(ci, 4 * ts_ + 4))
                            ovk = kTs.rearrange("r (j c) -> r j c", c=4096)
                            sS.inc(scalar.copy(
                                out=ovk[:, :, 128 * kt:128 * (kt + 1)],
                                in_=ptb.rearrange("r (j c) -> r j c", c=128)[:, 4:8, :]))
                            sS.inc(scalar.copy(out=dmy[:, 0:8], in_=dmy[:, 0:8]))
                block.scalar(pc_scalar_qev)
                TQ_END, VQ_END, SQ_END = sT.n, sV.n, sS.n

                def pc_sync_vw(sync):
                    sync.wait_ge(sS.h, SQ_END)
                    SQ2 = sS.n
                    for p in range(NDP):
                        sync.wait_ge(sS.h, SQ_END + p)
                        sX.inc(sync.dma_start(out=stg[:, 0:512],
                                              in_=wqkv[128 * p:128 * (p + 1), 1024:1536]), 16)
                block.sync(pc_sync_vw)
                XV1 = sX.n

                def pc_scalar_vw(scalar):
                    for p in range(NDP):
                        scalar.wait_ge(sX.h, XV1 - 16 * (NDP - 1 - p))
                        sS.inc(scalar.copy(out=wqkb[:, 512 * p:512 * (p + 1)],
                                           in_=stg[:, 0:512]))
                block.scalar(pc_scalar_vw)
                SV1 = sS.n

                def pc_vector_vones(vector):
                    vv = vAll.rearrange("t (k h c) -> t k h c", h=8, c=65)
                    sV.inc(vector.memset(vv[:, :, :, 64:65], 1.0))
                block.vector(pc_vector_vones)
                VV1 = sV.n

                TV0, SV2 = sT.n, sS.n

                def pc_gpsimd_x2v(gpsimd):
                    gpsimd.wait_ge(sT.h, TQ_END)
                    gpsimd.wait_ge(sC.h, CC1)
                    for tc in range(8):
                        if tc >= 1:
                            gpsimd.wait_ge(sT.h, TV0 + 4 * tc)
                        for p in range(NDP):
                            sQG[tc % 2].inc(gpsimd.dma_start(
                                out=x2c[tc % 2][:, 512 * p:512 * (p + 1)],
                                in_=x2src(tc, p)), 16)
                block.gpsimd(pc_gpsimd_x2v)

                def pc_tensor_v(tensor):
                    tensor.wait_ge(sS.h, SV1)
                    tensor.wait_ge(sV.h, PB_V)
                    for tc in range(8):
                        tensor.wait_ge(sQG[tc % 2].h, 128 * 4 + 128 * (tc // 2 + 1))
                        for ts_ in range(4):
                            if tc > 0 or ts_ > 0:
                                tensor.wait_ge(sS.h, SV2 + 4 * tc + ts_)
                            for p in range(NDP):
                                ins = tensor.matmul(out=pb[2][:, :],
                                                    lhsT=x2c[tc % 2][:, 512 * p + 128 * ts_:
                                                                     512 * p + 128 * (ts_ + 1)],
                                                    rhs=wqkb[:, 512 * p:512 * (p + 1)],
                                                    start=(p == 0), stop=(p == NDP - 1))
                            sT.inc(ins)
                block.tensor(pc_tensor_v)

                def pc_scalar_vev(scalar):
                    for tc in range(8):
                        for ts_ in range(4):
                            kt = 4 * tc + ts_
                            scalar.wait_ge(sT.h, TV0 + 4 * tc + ts_ + 1)
                            vv = vAll.rearrange("t (k h c) -> t k h c", h=8, c=65)
                            sS.inc(scalar.copy(
                                out=vv[:, kt, :, 0:64],
                                in_=pb[2].rearrange("t (h c) -> t h c", c=64)[:, :, :]))
                block.scalar(pc_scalar_vev)
                SV3 = sS.n
            hdC = pcC.enter_context(contextlib.ExitStack())
            _EH = hdC.enter_context
            qhat = _EH(nc.sbuf_tensor("qhat", [65, S], BF16))
            khat = _EH(nc.sbuf_tensor("khat", [65, S], BF16))
            pS = [_EH(nc.sbuf_tensor(f"pS{i}", [128, 512], BF16)) for i in range(2)]
            accev = _EH(nc.sbuf_tensor("accev", [65, 512], F32))
            rlc = _EH(nc.sbuf_tensor("rlc", [128, 1], F32))
            ysel = _EH(nc.sbuf_tensor("ysel", [128, 256], BF16))
            msk1s = _EH(nc.sbuf_tensor("msk1s", [128, 128], F32))
            msk2s = _EH(nc.sbuf_tensor("msk2s", [128, 512], F32))
            zb512 = _EH(nc.sbuf_tensor("zb512", [128, 512], BF16))

            def pc_sync_k(sync):
                sync.wait_ge(sS.h, SV3)
                sKC.inc(sync.dma_start(out=msk1s[:], in_=mask1), 16)
                sKC.inc(sync.dma_start(out=msk2s[:], in_=mask2), 16)
            block.sync(pc_sync_k)
            KC2 = sKC.n

            NCV = 168

            def _cv_src(m):
                e, which = m // 3, m % 3
                if e == 6:
                    t = (fc1T, fc2T, fc3T)[which]
                    return t.rearrange("(a p) d -> p a d", p=128)
                t = (ew1T, ew3T, ew2T)[which]
                return t[e].rearrange("(a p) d -> p a d", p=128)

            ewbv = ewbf.rearrange("(m a p) d -> m p a d", p=128, a=8)
            cvfv = stg.rearrange("p (a d) -> p a d", d=1024)
            cvbv = cvb.rearrange("p (a d) -> p a d", d=1024)

            def pc_sync_cv(sync):
                sync.wait_ge(sWC.h, NWC)
                def _store(st):
                    m, k = st // 8, st % 8
                    sCVs[st % 2].inc(sync.dma_start(
                        out=ewbv[m][:, k:k + 1, :],
                        in_=cvbv[:, st % 2:st % 2 + 1, :]), 16)
                for s in range(NCV):
                    if s >= 2:
                        sync.wait_ge(sCVv.h, s - 1)
                        _store(s - 2)
                    m, k = s // 8, s % 8
                    sCVl[s % 2].inc(sync.dma_start(
                        out=cvfv[:, s % 2:s % 2 + 1, :],
                        in_=_cv_src(m)[:, k:k + 1, :]), 16)
                for s in (NCV - 2, NCV - 1):
                    sync.wait_ge(sCVv.h, s + 1)
                    _store(s)
            block.sync(pc_sync_cv)
            CV_F = [sCVs[0].n, sCVs[1].n]

            _cv_next = [0]

            def _cv_cast(vector, budget):
                for _ in range(budget):
                    c = _cv_next[0]
                    if c >= NCV:
                        return
                    _cv_next[0] = c + 1
                    vector.wait_ge(sCVl[c % 2].h, 16 * (c // 2 + 1))
                    if c >= 2:
                        vector.wait_ge(sCVs[c % 2].h, 16 * (c // 2))
                    sCVv.inc(vector.tensor_scalar(
                        out=cvbv[:, c % 2:c % 2 + 1, :],
                        in0=cvfv[:, c % 2:c % 2 + 1, :],
                        scalar1=0.0, scalar2=None, op0=ALU.add))

            T1_per = [2] * NTF
            V1_per = [2] * NTF
            T1_head = sum(T1_per)
            V1_head = sum(V1_per)
            S1_head = 8
            P2P = []
            for G in range(8):
                ps_ = []
                for j in range(4):
                    if G == 0:
                        ps_.append(("d", j, 4 * G + j))
                        if j < 3:
                            ps_.append(("s", j, 4 * G + j))
                    else:
                        ps_.append(("m", j, 4 * G + j))
                for kt in range(4 * G):
                    ps_.append(("f", -1, kt))
                P2P.append(ps_)
            T2G = [1 + 2 * len(P2P[G]) + 4 for G in range(8)]
            S2G = [len(P2P[G]) + 1 + 5 for G in range(8)]
            V2G = [4 + 4 for G in range(8)]

            def T1QK(qt):
                return qt + 1 if qt < 2 else 2 * qt

            def T1TR(qt):
                return 3 + 2 * qt if qt < NTF - 1 else 2 * NTF

            def T2ST(pp):
                return pp + 2 if pp < 2 else 2 * pp + 1

            def T2AV(pp, np2):
                return 4 + 2 * pp if pp < np2 - 1 else 2 * np2 + 1
            T2_head = sum(T2G)
            S2_head = sum(S2G)
            V2_head = sum(V2G)
            TPH = T1_head + T2_head
            SPH = S1_head + S2_head + sum(S2G) * 0 + 0
            VPH = V1_head + V2_head
            SPH = S1_head + S2_head

            TH0, VH0, SH0 = sT.n, sV.n, sS.n

            def th_n(hh, k):
                return TH0 + TPH * hh + k

            def vh_n(hh, k):
                return VH0 + VPH * hh + k

            def sh_n(hh, k):
                return SH0 + SPH * hh + k

            import itertools
            T1_cum = [0] + list(itertools.accumulate(T1_per))
            V1_cum = [0] + list(itertools.accumulate(V1_per))
            T2_cum = [T1_head] + [T1_head + c for c in itertools.accumulate(T2G)]
            S2_cum = [S1_head] + [S1_head + c for c in itertools.accumulate(S2G)]
            V2_cum = [V1_head] + [V1_head + c for c in itertools.accumulate(V2G)]


            def pc_vector_kones(vector):
                vector.wait_ge(sS.h, SV3)
                sV.inc(vector.memset(khat[64:65, :], 1.0))
                sV.inc(vector.memset(zb512[:, :], 0.0))
            block.vector(pc_vector_kones)
            VK1 = sV.n
            VH0b = sV.n
            TH0b, SH0b = sT.n, sS.n
            TH0, VH0, SH0 = sT.n, sV.n, sS.n

            def pc_tensor_heads(tensor):
                tensor.wait_ge(sV.h, VK1)
                tensor.wait_ge(sS.h, SV3)

                def _qk(hh, qt):
                    if qt == 0 and hh >= 1:
                        tensor.wait_ge(sV.h, vh_n(hh - 1, VPH))
                    ins = tensor.matmul(out=pb[qt % 2][:, 0:128],
                                        lhsT=qhat[0:64, 128 * qt:128 * (qt + 1)],
                                        rhs=khat[0:64, 128 * qt:128 * (qt + 1)],
                                        start=True, stop=True,
                                        skip_group_check=True)
                    sT.inc(ins)

                def _tr1(hh, qt):
                    tensor.wait_ge(sV.h, vh_n(hh, 2 * qt + 2))
                    if qt >= 4:
                        tensor.wait_ge(sS.h, sh_n(hh, qt // 4))
                    ins = tensor.transpose(out=psm[0:1, 128 * (qt % 4):128 * (qt % 4) + 128],
                                           in_=mcol[:, qt % 2:qt % 2 + 1], identity=idnf[:])
                    sT.inc(ins)

                for hh in range(8):
                    tensor.wait_ge(sHQ.h, 32 * (hh + 1))
                    _qk(hh, 0)
                    _qk(hh, 1)
                    for qt in range(NTF):
                        _tr1(hh, qt)
                        if qt + 2 < NTF:
                            _qk(hh, qt + 2)
                    for G in range(8):
                        abank = pb[4]
                        np2 = len(P2P[G])

                        def _st(pp):
                            kind, i, kt = P2P[G][pp]
                            sbank = pb[2 + pp % 2]
                            if kind == "f":
                                ins = tensor.matmul(out=sbank[:, 0:512],
                                                    lhsT=khat[:, 128 * kt:128 * (kt + 1)],
                                                    rhs=qhat[:, 512 * G:512 * (G + 1)],
                                                    start=True, stop=True,
                                                    skip_group_check=True)
                            elif kind == "d":
                                ins = tensor.matmul(out=sbank[:, 0:128],
                                                    lhsT=khat[:, 128 * kt:128 * (kt + 1)],
                                                    rhs=qhat[:, 512 * G + 128 * i:512 * G + 128 * (i + 1)],
                                                    start=True, stop=True,
                                                    skip_group_check=True)
                            elif kind == "m":
                                w = 512 - 128 * i
                                ins = tensor.matmul(out=sbank[:, 0:w],
                                                    lhsT=khat[:, 128 * kt:128 * (kt + 1)],
                                                    rhs=qhat[:, 512 * G + 128 * i:512 * (G + 1)],
                                                    start=True, stop=True,
                                                    skip_group_check=True)
                            else:
                                w = 128 * (3 - i)
                                ins = tensor.matmul(out=sbank[:, 0:w],
                                                    lhsT=khat[:, 128 * kt:128 * (kt + 1)],
                                                    rhs=qhat[:, 512 * G + 128 * (i + 1):512 * (G + 1)],
                                                    start=True, stop=True,
                                                    skip_group_check=True)
                            sT.inc(ins)

                        def _av(pp):
                            kind, i, kt = P2P[G][pp]
                            tensor.wait_ge(sS.h, sh_n(hh, S2_cum[G] + pp + 1))
                            srt = False
                            stp = (pp == np2 - 1) or (G == 0 and kind == "d")
                            vsl = vAll[:, 520 * kt + 65 * hh:520 * kt + 65 * (hh + 1)]
                            if kind == "f":
                                ins = tensor.matmul(out=abank[0:65, 0:512],
                                                    lhsT=vsl, rhs=pS[pp % 2][:, 0:512],
                                                    start=False, stop=stp,
                                                    skip_group_check=True)
                            elif kind == "d":
                                ins = tensor.matmul(out=abank[0:65, 128 * i:128 * (i + 1)],
                                                    lhsT=vsl, rhs=pS[pp % 2][:, 0:128],
                                                    start=srt, stop=stp,
                                                    skip_group_check=True)
                            elif kind == "m":
                                w = 512 - 128 * i
                                ins = tensor.matmul(out=abank[0:65, 128 * i:512],
                                                    lhsT=vsl, rhs=pS[pp % 2][:, 0:w],
                                                    start=srt, stop=stp,
                                                    skip_group_check=True)
                            else:
                                w = 128 * (3 - i)
                                ins = tensor.matmul(out=abank[0:65, 128 * (i + 1):512],
                                                    lhsT=vsl, rhs=pS[pp % 2][:, 0:w],
                                                    start=srt, stop=stp,
                                                    skip_group_check=True)
                            sT.inc(ins)

                        tensor.wait_ge(sS.h, sh_n(hh, (G + 1)))
                        tensor.wait_ge(sS.h, sh_n(hh, S2_cum[G]))
                        ins = tensor.matmul(out=abank[0:65, 0:512],
                                            lhsT=vAll[:, 0:65],
                                            rhs=zb512[:, :], start=True, stop=False,
                                            skip_group_check=True)
                        sT.inc(ins)
                        _st(0)
                        _st(1)
                        for pp in range(np2):
                            _av(pp)
                            if pp + 2 < np2:
                                _st(pp + 2)
                        tensor.wait_ge(sS.h, sh_n(hh, S2_cum[G] + np2 + 1))
                        for i in range(4):
                            ins = tensor.transpose(out=psm[0:128, 128 * i:128 * i + 65],
                                                   in_=accev[:, 128 * i:128 * (i + 1)],
                                                   identity=idnf[0:65, 0:65])
                        sT.inc(ins, 4)
            block.tensor(pc_tensor_heads)

            def pc_vector_heads2(vector):
                vector.wait_ge(sKC.h, KC2)
                for hh in range(8):
                    for qt in range(NTF):
                        bank = pb[qt % 2]
                        vector.wait_ge(sT.h, th_n(hh, T1QK(qt)))
                        sV.inc(vector.tensor_tensor(out=bank[:, 0:128], in0=bank[:, 0:128],
                                                    in1=msk1s[:, :], op=ALU.add))
                        vector.wait_ge(sV.h, sV.n)
                        sV.inc(vector.tensor_reduce(out=mcol[:, qt % 2:qt % 2 + 1],
                                                    in_=bank[:, 0:128],
                                                    axis=AX, op=ALU.max))
                    for G in range(8):
                        np2 = len(P2P[G])
                        for pp, (kind, i, kt) in enumerate(P2P[G]):
                            if kind not in ("d", "m"):
                                continue
                            w = 128 if kind == "d" else 512 - 128 * i
                            vector.wait_ge(sT.h, th_n(hh, T2_cum[G] + T2ST(pp)))
                            sV.inc(vector.tensor_tensor(out=pb[2 + pp % 2][:, 0:w],
                                                        in0=pb[2 + pp % 2][:, 0:w],
                                                        in1=msk2s[:, 0:w], op=ALU.add))
                        _cv_cast(vector, 3)
                        vector.wait_ge(sT.h, th_n(hh, T2_cum[G + 1]))
                        vector.wait_ge(sS.h, sh_n(hh, S2_cum[G + 1]))
                        if G >= 1:
                            vector.wait_ge(sY.h, 16 * (8 * hh + G))
                        elif hh >= 1:
                            vector.wait_ge(sY.h, 16 * 8 * hh)
                        for i in range(4):
                            sV.inc(vector.tensor_scalar(
                                out=ysel[:, 64 * i:64 * (i + 1)],
                                in0=psm[0:128, 128 * i:128 * i + 64],
                                scalar1=stat[:, 10 + i:11 + i],
                                scalar2=None, op0=ALU.mult))
            block.vector(pc_vector_heads2)
            ATTV = sV.n

            def pc_scalar_heads(scalar):
                for hh in range(8):
                    for G in range(8):
                        scalar.wait_ge(sT.h, th_n(hh, T1TR(4 * G + 3)))
                        sS.inc(scalar.activation(out=qhat[64:65, 512 * G:512 * (G + 1)],
                                                 in_=psm[0:1, 0:512], func=ACT.Identity,
                                                 scale=-1.0))
                    for G in range(8):
                        np2 = len(P2P[G])
                        nmask = 0
                        for pp, (kind, i, kt) in enumerate(P2P[G]):
                            sbank = pb[2 + pp % 2]
                            scalar.wait_ge(sT.h, th_n(hh, T2_cum[G] + T2ST(pp)))
                            if kind in ("d", "m"):
                                w = 128 if kind == "d" else 512 - 128 * i
                                scalar.wait_ge(sV.h, vh_n(hh, V2_cum[G] + nmask + 1))
                                nmask += 1
                                sS.inc(scalar.activation(out=pS[pp % 2][:, 0:w],
                                                         in_=sbank[:, 0:w], func=ACT.Exp))
                            elif kind == "s":
                                w = 128 * (3 - i)
                                sS.inc(scalar.activation(out=pS[pp % 2][:, 0:w],
                                                         in_=sbank[:, 0:w], func=ACT.Exp))
                            else:
                                sS.inc(scalar.activation(out=pS[pp % 2][:, 0:512],
                                                         in_=sbank[:, 0:512], func=ACT.Exp))
                        scalar.wait_ge(sT.h, th_n(hh, T2_cum[G] + 1 + 2 * np2))
                        sS.inc(scalar.copy(out=accev[:, :], in_=pb[4][0:65, 0:512]))
                        scalar.wait_ge(sT.h, th_n(hh, T2_cum[G + 1]))
                        for i in range(4):
                            sS.inc(scalar.activation(out=stat[:, 6 + i:7 + i],
                                                     in_=psm[0:128, 128 * i + 64:128 * i + 65],
                                                     func=ACT.Ln))
                        scalar.wait_ge(sS.h, sS.n)
                        sS.inc(scalar.activation(out=stat[:, 10:14],
                                                 in_=stat[:, 6:10],
                                                 func=ACT.Exp, scale=-1.0))
            block.scalar(pc_scalar_heads)

            def pc_gpsimd_heads(gpsimd):
                gpsimd.wait_ge(sS.h, SV3)
                for hh in range(8):
                    pt_ = hh // 2
                    base = 64 * (hh % 2)
                    if hh >= 1:
                        gpsimd.wait_ge(sT.h, th_n(hh - 1, TPH))
                    sHQ.inc(gpsimd.dma_start(
                        out=qhat[0:64, :],
                        in_=qTs[base:base + 64, 4096 * pt_:4096 * (pt_ + 1)]), 16)
                    sHQ.inc(gpsimd.dma_start(
                        out=khat[0:64, :],
                        in_=kTs[base:base + 64, 4096 * pt_:4096 * (pt_ + 1)]), 16)
                    for G in range(8):
                        gpsimd.wait_ge(sV.h, vh_n(hh, V2_cum[G + 1]))
                        ov = cc_y_in.ap().rearrange("(g i t) c -> g t i c", i=4, t=128)
                        iv = ysel.rearrange("t (i c) -> t i c", c=64)
                        sY.inc(gpsimd.dma_start(out=ov[G][:, :, 64 * hh:64 * (hh + 1)],
                                                in_=iv[:, :, :]), 16)
            block.gpsimd(pc_gpsimd_heads)
            Y_T = sY.n

            def pc_gpsimd_cc2(gpsimd):
                gpsimd.wait_ge(sY.h, Y_T)
                sC.inc(gpsimd.collective_compute(
                    "AllGather", ALU.bypass, replica_groups=RG,
                    ins=[cc_y_in.ap()], outs=[cc_y_out.ap()]))
            block.gpsimd(pc_gpsimd_cc2)
            CC2 = sC.n

            pcC.close()
            accF = pdC.enter_context(nc.sbuf_tensor("accF", [128, NT * D], F32))

            opC = contextlib.ExitStack()
            with opC:
                EO = opC.enter_context
                wob = EO(nc.sbuf_tensor("wob", [128, NDP * 1024], BF16))
                ygt = [EO(nc.sbuf_tensor(f"ygt{i}", [128, 2048], BF16)) for i in range(2)]
                yfull = EO(nc.sbuf_tensor("yfull", [128, D], BF16))
                yTt = EO(nc.sbuf_tensor("yTt", [128, D], BF16))

                XO0 = sX.n
                SO0 = sS.n

                def pc_sync_wo(sync):
                    for p in range(NDP):
                        sync.wait_ge(sS.h, SO0 + p)
                        sX.inc(sync.dma_start(out=stg[:, 0:1024],
                                              in_=wo[128 * p:128 * (p + 1), :]), 16)
                block.sync(pc_sync_wo)
                XO1 = sX.n

                def pc_scalar_wo(scalar):
                    scalar.wait_ge(sS.h, SO0)
                    for p in range(NDP):
                        scalar.wait_ge(sX.h, XO1 - 16 * (NDP - 1 - p))
                        sS.inc(scalar.copy(out=wob[:, 1024 * p:1024 * (p + 1)],
                                           in_=stg[:, 0:1024]))
                block.scalar(pc_scalar_wo)
                SO1 = sS.n

                TO0, VO0, SO2 = sT.n, sV.n, sS.n
                TPO, VPO, SPO = 4, 9, 7
                R_B = [sR[0].n, sR[1].n]

                def to_n(i, k):
                    return TO0 + TPO * i + k

                def vo_n(i, k):
                    return VO0 + VPO * i + k

                def so_n(i, k):
                    return SO2 + SPO * i + k

                def pc_gpsimd_yg(gpsimd):
                    gpsimd.wait_ge(sC.h, CC2)
                    for i in range(NT):
                        if i >= 2:
                            gpsimd.wait_ge(sV.h, vo_n(i - 2, 4))
                        for sl in range(2):
                            for ab in range(2):
                                sYG[i % 2].inc(gpsimd.dma_start(
                                    out=ygt[i % 2][:, 1024 * sl + 512 * ab:
                                                   1024 * sl + 512 * (ab + 1)],
                                    in_=cc_y_out.ap()[4096 * sl + 2048 * ab + 128 * i:
                                                      4096 * sl + 2048 * ab + 128 * (i + 1), :]), 16)
                block.gpsimd(pc_gpsimd_yg)

                def pc_vector_op(vector):
                    vector.wait_ge(sS.h, SO1)
                    vector.wait_ge(sP[0].h, PB_P0)
                    vector.wait_ge(sP[1].h, PB_P1)
                    for i in range(NT):
                        vector.wait_ge(sYG[i % 2].h, 64 * (i // 2 + 1))
                        if i >= 1:
                            vector.wait_ge(sT.h, to_n(i - 1, 1))
                        for sl in range(2):
                            a = ygt[i % 2][:, 1024 * sl:1024 * sl + 512]
                            b = ygt[i % 2][:, 1024 * sl + 512:1024 * sl + 1024]
                            sV.inc(vector.tensor_sub(out=a, in0=a, in1=b))
                            vector.wait_ge(sV.h, sV.n)
                            sV.inc(vector.scalar_tensor_tensor(
                                out=yfull[:, 512 * sl:512 * (sl + 1)], in0=a,
                                scalar=omsk[:, 0:1], in1=b, op0=ALU.mult, op1=ALU.add))
                        vector.wait_ge(sT.h, to_n(i, 2))
                        vector.wait_ge(sR[i % 2].h, R_B[i % 2] + 16 * (i // 2 + 1))
                        if i >= 2:
                            vector.wait_ge(sS.h, so_n(i - 2, 4))
                        A = (i % 3) * 2
                        vector.scalar_tensor_tensor(out=wf0[i % 2][:, 0:512],
                                                    in0=pb[A][:, :], scalar=1.0,
                                                    in1=wf1[i % 2][:, 0:512],
                                                    op0=ALU.mult, op1=ALU.add)
                        sV.inc(vector.scalar_tensor_tensor(out=wf0[i % 2][:, 512:1024],
                                                           in0=pb[A + 1][:, :], scalar=1.0,
                                                           in1=wf1[i % 2][:, 512:1024],
                                                           op0=ALU.mult, op1=ALU.add), 2)
                        vector.wait_ge(sV.h, sV.n)
                        sV.inc(vector.tensor_reduce(out=stat[:, 0:1], in_=wf0[i % 2][:, :],
                                                    axis=AX, op=ALU.add))
                        vector.wait_ge(sS.h, so_n(i, 5))
                        sV.inc(vector.reciprocal(out=stat[:, 4:5], in_=stat[:, 3:4]))
                        vector.wait_ge(sV.h, sV.n)
                        sV.inc(vector.tensor_scalar(out=accF[:, 1024 * i:1024 * (i + 1)],
                                                    in0=wf0[i % 2][:, :],
                                                    scalar1=stat[:, 4:5], scalar2=None,
                                                    op0=ALU.mult))
                block.vector(pc_vector_op)

                def pc_sync_res(sync):
                    for i in range(NT):
                        if i >= 2:
                            sync.wait_ge(sV.h, vo_n(i - 2, 6))
                        sR[i % 2].inc(sync.dma_start(
                            out=wf1[i % 2][:, :],
                            in_=sp_x2[128 * i:128 * (i + 1), :]), 16)
                block.sync(pc_sync_res)

                def pc_tensor_op(tensor):
                    tensor.wait_ge(sS.h, SO1)
                    tensor.wait_ge(sV.h, ATTV)
                    for i in range(NT):
                        tensor.wait_ge(sV.h, vo_n(i, 4))
                        if i >= 1:
                            tensor.wait_ge(sS.h, so_n(i - 1, 7))
                        for j in range(NDP):
                            ins = tensor.transpose(out=ptb[:, 128 * j:128 * (j + 1)],
                                                   in_=yfull[:, 128 * j:128 * (j + 1)],
                                                   identity=idn[:])
                        sT.inc(ins)
                        tensor.wait_ge(sS.h, so_n(i, 1))
                        if i >= 3:
                            tensor.wait_ge(sV.h, vo_n(i - 3, 6))
                        A = (i % 3) * 2
                        for p in range(NDP):
                            for nc2 in range(2):
                                ins = tensor.matmul(out=pb[A + nc2][:, :],
                                                    lhsT=yTt[:, 128 * p:128 * (p + 1)],
                                                    rhs=wob[:, 1024 * p + 512 * nc2:
                                                            1024 * p + 512 * (nc2 + 1)],
                                                    start=(p == 0), stop=(p == NDP - 1))
                        sT.inc(ins)
                        tensor.wait_ge(sS.h, so_n(i, 6))
                        for j in range(4):
                            ins = tensor.transpose(out=ptb[:, 128 * j:128 * (j + 1)],
                                                   in_=xbft[:, 128 * j:128 * (j + 1)],
                                                   identity=idn[:])
                        sT.inc(ins)
                        for j in range(4, NDP):
                            ins = tensor.transpose(out=ptb[:, 128 * j:128 * (j + 1)],
                                                   in_=xbft[:, 128 * j:128 * (j + 1)],
                                                   identity=idn[:])
                        sT.inc(ins)
                block.tensor(pc_tensor_op)

                def pc_scalar_op(scalar):
                    for i in range(NT):
                        scalar.wait_ge(sT.h, to_n(i, 1))
                        sS.inc(scalar.copy(out=yTt[:, :], in_=ptb[:, :]))
                        scalar.wait_ge(sV.h, vo_n(i, 7))
                        sS.inc(scalar.mul(out=stat[:, 1:2], in_=stat[:, 0:1], mul=-1.0 / D))
                        scalar.wait_ge(sS.h, sS.n)
                        sS.inc(scalar.activation(out=wf0[i % 2][:, :], in_=wf0[i % 2][:, :],
                                                 func=ACT.Identity, bias=stat[:, 1:2],
                                                 scale=1.0))
                        scalar.wait_ge(sS.h, sS.n)
                        sS.inc(scalar.activation(out=dmy[:, :], in_=wf0[i % 2][:, :],
                                                 func=ACT.Square, accum_out=stat[:, 2:3]))
                        scalar.wait_ge(sS.h, sS.n)
                        sS.inc(scalar.activation(out=stat[:, 3:4], in_=stat[:, 2:3],
                                                 func=ACT.Sqrt, scale=1.0 / D,
                                                 bias=stat[:, 5:6]))
                        scalar.wait_ge(sV.h, vo_n(i, 9))
                        sS.inc(scalar.copy(out=xbft[:, :],
                                           in_=accF[:, 1024 * i:1024 * (i + 1)]))
                        scalar.wait_ge(sT.h, to_n(i, 4))
                        ov = XT.rearrange("q (p c) -> q p c", c=TOK)
                        sS.inc(scalar.copy(
                            out=ov[:, :, 128 * i:128 * (i + 1)],
                            in_=ptb.rearrange("q (p c) -> q p c", c=128)[:, :, :]))
                block.scalar(pc_scalar_op)
                PC_T, PC_V, PC_S = sT.n, sV.n, sS.n

            TAP3 = sO.n
            if TAPS:
                def pc_tap3(sync):
                    sync.wait_ge(sV.h, PC_V)
                    for i in range(NT):
                        sO.inc(sync.dma_start(
                            out=taps["tap_x3"][128 * i:128 * (i + 1), :],
                            in_=accF[:, 1024 * i:1024 * (i + 1)]), 16)
                block.sync(pc_tap3)
            TAP3E = sO.n

            if PHASES >= 4:
                pdE = pdC.enter_context
                ewA = pdE(nc.sbuf_tensor("ewA", [128, NDP * 1024], BF16))
                ewB = pdE(nc.sbuf_tensor("ewB", [128, NDP * 1024], BF16))
                ewC = pdE(nc.sbuf_tensor("ewC", [128, NDP * 1024], BF16))
                uT = pdE(nc.sbuf_tensor("uT", [128, NDP * 1024], BF16))
                silb = pdE(nc.sbuf_tensor("silb", [128, 512], BF16))
                lnsum = pdE(nc.sbuf_tensor("lnsum", [128, NT], F32))
                gwTb = pdE(nc.sbuf_tensor("gwTb", [128, 48], BF16))
                gbs = pdE(nc.sbuf_tensor("gbs", [128, 6], F32))
                iot6 = pdE(nc.sbuf_tensor("iot6", [128, 6], F32))
                geb = pdE(nc.sbuf_tensor("geb", [128, NT * 6], F32))
                gwf = pdE(nc.sbuf_tensor("gwf", [128, 48], F32))

                KD0 = sKC.n

                def pd_sync_k(sync):
                    sync.wait_ge(sS.h, PC_S)
                    sKC.inc(sync.dma_start(out=gbs[:], in_=gb_bc), 16)
                    sKC.inc(sync.dma_start(out=iot6[:], in_=iota6), 16)
                    sKC.inc(sync.dma_start(out=gwf[:], in_=gwT), 16)
                block.sync(pd_sync_k)
                KD1 = sKC.n

                def pd_scalar_k(scalar):
                    scalar.wait_ge(sKC.h, KD1)
                    sS.inc(scalar.copy(out=gwTb[:, :], in_=gwf[:, :]))
                block.scalar(pd_scalar_k)
                SD0 = sS.n

                TD0, VD0, SD1 = sT.n, sV.n, sS.n
                TPD, VPD, SPD = 1, 11, 1

                def td_n(i, k):
                    return TD0 + TPD * i + k

                def vd_n(i, k):
                    return VD0 + VPD * i + k

                def sd_n(i, k):
                    return SD1 + SPD * i + k

                def pd_tensor_g(tensor):
                    tensor.wait_ge(sS.h, SD0)
                    tensor.wait_ge(sV.h, PC_V)
                    for i in range(NT):
                        if i >= 1:
                            tensor.wait_ge(sV.h, vd_n(i - 1, 1))
                        for p in range(NDP):
                            ins = tensor.matmul(
                                out=psm[0:128, 0:6],
                                lhsT=XT[:, 2048 * p + 128 * i:2048 * p + 128 * (i + 1)],
                                rhs=gwTb[:, 6 * p:6 * (p + 1)],
                                start=(p == 0), stop=(p == NDP - 1),
                                skip_group_check=True)
                        sT.inc(ins)
                block.tensor(pd_tensor_g)

                def pd_vector_g(vector):
                    vector.wait_ge(sS.h, SD0)
                    for i in range(NT):
                        vector.wait_ge(sT.h, td_n(i, 1))
                        sV.inc(vector.scalar_tensor_tensor(
                            out=mparts[:, 0:6], in0=psm[0:128, 0:6], scalar=1.0,
                            in1=gbs[:, :], op0=ALU.mult, op1=ALU.add))
                        vector.wait_ge(sS.h, sd_n(i, 1))
                        sV.inc(vector.tensor_reduce(out=stat[:, 6:7], in_=mparts[:, 8:14],
                                                    axis=AX, op=ALU.max))
                        vector.wait_ge(sV.h, sV.n)
                        sV.inc(vector.tensor_scalar(out=mparts[:, 0:6], in0=mparts[:, 8:14],
                                                    scalar1=stat[:, 6:7], scalar2=None,
                                                    op0=ALU.is_ge))
                        vector.wait_ge(sV.h, sV.n)
                        sV.inc(vector.scalar_tensor_tensor(
                            out=mparts[:, 0:6], in0=mparts[:, 0:6], scalar=-1e30,
                            in1=mparts[:, 8:14], op0=ALU.mult, op1=ALU.add))
                        vector.wait_ge(sV.h, sV.n)
                        sV.inc(vector.tensor_reduce(out=stat[:, 7:8], in_=mparts[:, 0:6],
                                                    axis=AX, op=ALU.max))
                        vector.wait_ge(sV.h, sV.n)
                        sV.inc(vector.tensor_tensor(out=stat[:, 8:9], in0=stat[:, 6:7],
                                                    in1=stat[:, 7:8], op=ALU.add))
                        vector.wait_ge(sV.h, sV.n)
                        sV.inc(vector.tensor_scalar(out=stat[:, 8:9], in0=stat[:, 8:9],
                                                    scalar1=1e-8, scalar2=None, op0=ALU.add))
                        vector.wait_ge(sV.h, sV.n)
                        sV.inc(vector.reciprocal(out=stat[:, 9:10], in_=stat[:, 8:9]))
                        vector.wait_ge(sV.h, sV.n)
                        sV.inc(vector.tensor_scalar(out=mparts[:, 0:6], in0=mparts[:, 8:14],
                                                    scalar1=stat[:, 7:8], scalar2=None,
                                                    op0=ALU.is_ge))
                        vector.wait_ge(sV.h, sV.n)
                        sV.inc(vector.tensor_tensor(out=mparts[:, 0:6], in0=mparts[:, 0:6],
                                                    in1=mparts[:, 8:14], op=ALU.mult))
                        vector.wait_ge(sV.h, sV.n)
                        sV.inc(vector.tensor_scalar(out=geb[:, 6 * i:6 * (i + 1)],
                                                    in0=mparts[:, 0:6],
                                                    scalar1=stat[:, 9:10], scalar2=None,
                                                    op0=ALU.mult))
                        vector.wait_ge(sV.h, sV.n)
                block.vector(pd_vector_g)

                def pd_scalar_g(scalar):
                    for i in range(NT):
                        scalar.wait_ge(sV.h, vd_n(i, 1))
                        sS.inc(scalar.activation(out=mparts[:, 8:14], in_=mparts[:, 0:6],
                                                 func=ACT.Sigmoid))
                block.scalar(pd_scalar_g)
                GATE_V, GATE_S, GATE_T = sV.n, sS.n, sT.n

                def ew_src(e, which):
                    if e == 6:
                        t = (fc1T, fc2T, fc3T)[which]
                        return t.rearrange("(a p) d -> p a d", p=128)
                    t = (ew1T, ew3T, ew2T)[which]
                    return t[e].rearrange("(a p) d -> p a d", p=128)

                TPE, SPE, VPE = 80, 32, 64
                TE0, SE0, VE0 = sT.n, sS.n, sV.n

                def te_n(e, k):
                    return TE0 + TPE * e + k

                def se_n(e, k):
                    return SE0 + SPE * e + k

                def ve_n(e, k):
                    return VE0 + VPE * e + k

                ewbv2 = ewbf.rearrange("(m a p) d -> m p a d", p=128, a=8)

                def pd_gpsimd_w(gpsimd):
                    gpsimd.wait_ge(sS.h, SE0)
                    gpsimd.wait_ge(sV.h, GATE_V)
                    gpsimd.wait_ge(sCVs[0].h, CV_F[0])
                    gpsimd.wait_ge(sCVs[1].h, CV_F[1])
                    for e in range(7):
                        for j in range(6):
                            which, hf = j // 2, j % 2
                            if e >= 1:
                                gpsimd.wait_ge(sT.h,
                                               te_n(e - 1, 72 if j < 4 else 80))
                            dst = (ewA, ewA, ewB, ewB, ewC, ewC)[j]
                            sem = sGC if j >= 4 else sGAB
                            sem.inc(gpsimd.dma_start(
                                out=dst.rearrange("p (a d) -> p a d",
                                                  d=1024)[:, 4 * hf:4 * (hf + 1), :],
                                in_=ewbv2[3 * e + which][:, 4 * hf:4 * (hf + 1), :]),
                                16)
                block.gpsimd(pd_gpsimd_w)

                def pd_scalar_e(scalar):
                    for e in range(7):
                        for ip in range(16):
                            scalar.wait_ge(sT.h, te_n(e, 2 * ip + 1))
                            if ip >= 1:
                                scalar.wait_ge(sV.h, ve_n(e, ip))
                            sS.inc(scalar.activation(out=silb[:, :],
                                                     in_=pb[(2 * ip) % 4][:, :],
                                                     func=ACT.Silu))
                        for ip in range(16):
                            scalar.wait_ge(sT.h, te_n(e, 40 + 2 * ip + 1))
                            scalar.wait_ge(sV.h, ve_n(e, 32 + ip))
                            sS.inc(scalar.activation(out=silb[:, :],
                                                     in_=pb[(2 * ip) % 4][:, :],
                                                     func=ACT.Silu))
                block.scalar(pd_scalar_e)

                LV_R = {}

                def pd_vector_e(vector):
                    vector.wait_ge(sV.h, GATE_V)
                    if TAPS:
                        vector.wait_ge(sO.h, TAP3E)
                    for e in range(7):
                        for h2 in range(2):
                            for ip in range(16):
                                hs, tc2 = ip // 2, ip % 2
                                vector.wait_ge(sS.h, se_n(e, (1 if h2 == 0 else 17) + ip))
                                vector.wait_ge(sT.h, te_n(e, 40 * h2 + 2 * ip + 2))
                                sV.inc(vector.tensor_tensor(
                                    out=uT[:, 1024 * hs + 512 * tc2:1024 * hs + 512 * (tc2 + 1)],
                                    in0=silb[:, :], in1=pb[(2 * ip) % 4 + 1][:, :],
                                    op=ALU.mult))
                            for ti in range(8):
                                i = 8 * h2 + ti
                                vector.wait_ge(sT.h, te_n(e, 40 * h2 + 33 + ti))
                                for nc2 in range(2):
                                    a = accF[:, 1024 * i + 512 * nc2:1024 * i + 512 * (nc2 + 1)]
                                    if e == 6:
                                        sV.inc(vector.scalar_tensor_tensor(
                                            out=a, in0=pb[4 + nc2][:, :], scalar=1.0,
                                            in1=a, op0=ALU.mult, op1=ALU.add))
                                    else:
                                        sV.inc(vector.scalar_tensor_tensor(
                                            out=a, in0=pb[4 + nc2][:, :],
                                            scalar=geb[:, 6 * i + e:6 * i + e + 1],
                                            in1=a, op0=ALU.mult, op1=ALU.add))
                                if e == 6:
                                    vector.wait_ge(sV.h, sV.n)
                                    sLN.inc(vector.tensor_reduce(
                                        out=lnsum[:, i:i + 1],
                                        in_=accF[:, 1024 * i:1024 * (i + 1)],
                                        axis=AX, op=ALU.add))
                                    LV_R[i] = sLN.n
                block.vector(pd_vector_e)

                def pd_tensor_e(tensor):
                    tensor.wait_ge(sV.h, GATE_V)
                    tensor.wait_ge(sV.h, PC_V)
                    for e in range(7):
                        for h2 in range(2):
                            tb = 40 * h2
                            for ip in range(16):
                                hs, tc2 = ip // 2, ip % 2
                                bA = pb[(2 * ip) % 4]
                                bB = pb[(2 * ip) % 4 + 1]
                                if ip == 0:
                                    tensor.wait_ge(sGAB.h, 64 * (e + 1))
                                if ip >= 2:
                                    tensor.wait_ge(sV.h, ve_n(e, 32 * h2 + ip - 1))
                                elif h2 == 1:
                                    tensor.wait_ge(sV.h, ve_n(e, 16))
                                elif e >= 1:
                                    tensor.wait_ge(sV.h, ve_n(e - 1, VPE))
                                for p in range(NDP):
                                    ins = tensor.matmul(
                                        out=bA[:, :],
                                        lhsT=ewA[:, 1024 * p + 128 * hs:1024 * p + 128 * (hs + 1)],
                                        rhs=XT[:, 2048 * p + 1024 * h2 + 512 * tc2:
                                               2048 * p + 1024 * h2 + 512 * (tc2 + 1)],
                                        start=(p == 0), stop=(p == NDP - 1))
                                sT.inc(ins)
                                for p in range(NDP):
                                    ins = tensor.matmul(
                                        out=bB[:, :],
                                        lhsT=ewB[:, 1024 * p + 128 * hs:1024 * p + 128 * (hs + 1)],
                                        rhs=XT[:, 2048 * p + 1024 * h2 + 512 * tc2:
                                               2048 * p + 1024 * h2 + 512 * (tc2 + 1)],
                                        start=(p == 0), stop=(p == NDP - 1))
                                sT.inc(ins)
                            tensor.wait_ge(sGC.h, 32 * (e + 1))
                            tensor.wait_ge(sV.h, ve_n(e, 32 * h2 + 16))
                            for ti in range(8):
                                if ti >= 1:
                                    tensor.wait_ge(sV.h, ve_n(e, 32 * h2 + 16 + 2 * ti))
                                elif h2 == 1:
                                    tensor.wait_ge(sV.h, ve_n(e, 48))
                                elif e >= 1:
                                    tensor.wait_ge(sV.h, ve_n(e - 1, VPE))
                                for hs in range(8):
                                    for nc2 in range(2):
                                        ins = tensor.matmul(
                                            out=pb[4 + nc2][:, :],
                                            lhsT=uT[:, 1024 * hs + 128 * ti:1024 * hs + 128 * (ti + 1)],
                                            rhs=ewC[:, 1024 * hs + 512 * nc2:1024 * hs + 512 * (nc2 + 1)],
                                            start=(hs == 0), stop=(hs == NDP - 1))
                                sT.inc(ins)
                block.tensor(pd_tensor_e)
                E_V, E_S, E_T = sV.n, sS.n, sT.n

                LT0, LV0, LS0 = sT.n, sV.n, sS.n
                VPL, SPL = 3, 4
                P_B = [sP[0].n, sP[1].n]

                def lv_n(i, k):
                    return LV0 + VPL * i + k

                def ls_n(i, k):
                    return LS0 + SPL * i + k

                LV_M = {}

                def _ln_cols(i):
                    o = 6 * (i % 2)
                    return o, o + 1, o + 2, o + 3, o + 4

                def pd_vector_ln(vector):
                    for i in range(NT):
                        a = accF[:, 1024 * i:1024 * (i + 1)]
                        _, _, _, c3, c4 = _ln_cols(i)
                        if i >= 2:
                            vector.wait_ge(sP[i % 2].h, P_B[i % 2] + 16 * (i // 2))
                        vector.wait_ge(sS.h, ls_n(i, 4))
                        sV.inc(vector.reciprocal(out=stat[:, c4:c4 + 1],
                                                 in_=stat[:, c3:c3 + 1]))
                        vector.wait_ge(sV.h, sV.n)
                        sV.inc(vector.tensor_scalar(out=wf0[i % 2][:, :], in0=a,
                                                    scalar1=stat[:, c4:c4 + 1],
                                                    scalar2=None, op0=ALU.mult))
                        LV_M[i] = sV.n
                block.vector(pd_vector_ln)

                def pd_scalar_ln(scalar):
                    for i in range(NT):
                        a = accF[:, 1024 * i:1024 * (i + 1)]
                        _, c1, c2, c3, _ = _ln_cols(i)
                        scalar.wait_ge(sLN.h, LV_R[i])
                        if i >= 2:
                            scalar.wait_ge(sV.h, LV_M[i - 2])
                        sS.inc(scalar.mul(out=stat[:, c1:c1 + 1],
                                          in_=lnsum[:, i:i + 1], mul=-1.0 / D))
                        scalar.wait_ge(sS.h, sS.n)
                        sS.inc(scalar.activation(out=a, in_=a, func=ACT.Identity,
                                                 bias=stat[:, c1:c1 + 1], scale=1.0))
                        scalar.wait_ge(sS.h, sS.n)
                        sS.inc(scalar.activation(out=dmy[:, :], in_=a, func=ACT.Square,
                                                 accum_out=stat[:, c2:c2 + 1]))
                        scalar.wait_ge(sS.h, sS.n)
                        sS.inc(scalar.activation(out=stat[:, c3:c3 + 1],
                                                 in_=stat[:, c2:c2 + 1],
                                                 func=ACT.Sqrt, scale=1.0 / D,
                                                 bias=stat[:, 5:6]))
                block.scalar(pd_scalar_ln)

                def pd_sync_out(sync):
                    for i in range(NT):
                        sync.wait_ge(sV.h, LV_M[i])
                        sP[i % 2].inc(sync.dma_start(
                            out=y_out[128 * i:128 * (i + 1), :],
                            in_=wf0[i % 2][:, :]), 16)
                    sync.wait_ge(sP[0].h, sP[0].n)
                    sync.wait_ge(sP[1].h, sP[1].n)
                block.sync(pd_sync_out)



        FIN_V, FIN_S, FIN_T = sV.n, sS.n, sT.n

        src_sp = sp_x2 if PHASES >= 2 else sp_x1

        def tmp_out(sync):
            if PHASES >= 4:
                return
            sync.wait_ge(sP[0].h, sP[0].n)
            sync.wait_ge(sP[1].h, sP[1].n)
            sync.wait_ge(sS.h, FIN_S)
            for t in range(NT):
                sO.inc(sync.dma_start(out=y_out[128 * t:128 * (t + 1), :],
                                      in_=src_sp[128 * t:128 * (t + 1), :]), 16)
            sync.wait_ge(sO.h, sO.n)
        block.sync(tmp_out)

        if TAPS:
            def tap_sync(sync):
                sync.wait_ge(sP[0].h, PB_P0)
                sync.wait_ge(sP[1].h, PB_P1)
                for t in range(NT):
                    sO.inc(sync.dma_start(out=taps["tap_x1"][128 * t:128 * (t + 1), :],
                                          in_=sp_x1[128 * t:128 * (t + 1), :]), 16)
                sO.inc(sync.dma_start(out=taps["tap_dbc"], in_=dbcT_hbm[:, :]), 16)
                sO.inc(sync.dma_start(out=taps["tap_yc"], in_=ycols[:, :]), 16)
                sync.wait_ge(sO.h, sO.n)
            block.sync(tap_sync)

    return nc


def _prep(inputs):
    x = np.ascontiguousarray(np.asarray(inputs["L0_attn_x"], dtype=np.float32))
    W_in = np.asarray(inputs["L0_attn_W_in"], dtype=np.float32)
    b_in = np.asarray(inputs["L0_attn_b_in"], dtype=np.float32)

    base = {
        "wz": np.ascontiguousarray(W_in[:, D:2 * D]),
        "wdbc": np.ascontiguousarray(W_in[:, 2 * D:2 * D + 48]),
        "bz_bc": _bcast(b_in[D:2 * D]),
        "bdbc_bc": _bcast(b_in[2 * D:2 * D + 48]),
        "alog": np.ascontiguousarray(np.asarray(inputs["L0_attn_A_log"],
                                                dtype=np.float32).reshape(16, 1)),
        "wout": np.ascontiguousarray(np.asarray(inputs["L0_attn_W_out"], dtype=np.float32)),
        "bout_bc": _bcast(np.asarray(inputs["L0_attn_b_out"], dtype=np.float32)),
        "w1T": np.ascontiguousarray(np.asarray(inputs["L0_ffn_W1"], dtype=np.float32).T),
        "b1c": np.ascontiguousarray(np.asarray(inputs["L0_ffn_b1"],
                                               dtype=np.float32).reshape(HF // 128, 128).T),
        "w2T": np.ascontiguousarray(np.asarray(inputs["L0_ffn_W2"], dtype=np.float32).T),
        "b2_bc": _bcast(np.asarray(inputs["L0_ffn_b2"], dtype=np.float32)),
        "scale_bc": _bcast(np.asarray(inputs["L0_ffn_scale"],
                                      dtype=np.float32).reshape(1)).reshape(128, 1).copy(),
        "wo": np.ascontiguousarray(np.asarray(inputs["L1_attn_W_o"], dtype=np.float32)),
        "gwT": np.ascontiguousarray(np.asarray(inputs["L1_ffn_gate_w"],
                                               dtype=np.float32).T.reshape(8, 128, 6)
                                    .transpose(1, 0, 2).reshape(128, 48)),
        "gb_bc": _bcast(np.asarray(inputs["L1_ffn_gate_b"], dtype=np.float32)),
        "ew1T": np.ascontiguousarray(np.asarray(inputs["L1_ffn_ew1"],
                                                dtype=np.float32).transpose(0, 2, 1)),
        "ew2T": np.ascontiguousarray(np.asarray(inputs["L1_ffn_ew2"],
                                                dtype=np.float32).transpose(0, 2, 1)),
        "ew3T": np.ascontiguousarray(np.asarray(inputs["L1_ffn_ew3"],
                                                dtype=np.float32).transpose(0, 2, 1)),
        "fc1T": np.ascontiguousarray(np.asarray(inputs["L1_ffn_fc1"], dtype=np.float32).T),
        "fc2T": np.ascontiguousarray(np.asarray(inputs["L1_ffn_fc2"], dtype=np.float32).T),
        "fc3T": np.ascontiguousarray(np.asarray(inputs["L1_ffn_fc3"], dtype=np.float32).T),
        "ident": np.eye(128, dtype=BF),
        "identf": np.eye(128, dtype=np.float32),
        "mask1": np.where(np.arange(128)[None, :] <= np.arange(128)[:, None],
                          0.0, NEG).astype(np.float32),
        "mask2": np.concatenate([
            np.where(np.arange(128)[:, None] <= np.arange(128)[None, :],
                     0.0, NEG).astype(np.float32),
            np.zeros((128, 384), dtype=np.float32)], axis=1),
        "iota6": np.ascontiguousarray(np.broadcast_to(
            np.arange(6, dtype=np.float32), (128, 6))),
    }
    half = 32
    theta = 1.0 / (10000.0 ** (np.arange(half, dtype=np.float64) / half))
    tpos = np.arange(S, dtype=np.float64)
    ang = tpos[:, None] * theta[None, :]
    cosv = np.cos(ang).astype(np.float32).astype(BF)
    sinv = np.sin(ang).astype(np.float32).astype(BF)
    base["cos_t"] = np.ascontiguousarray(
        cosv.reshape(NTF, 128, half).transpose(1, 0, 2).reshape(128, NTF * half))
    base["sin_t"] = np.ascontiguousarray(
        sinv.reshape(NTF, 128, half).transpose(1, 0, 2).reshape(128, NTF * half))

    W_qkv = np.asarray(inputs["L1_attn_W_qkv"], dtype=np.float32)
    in_maps = []
    for c in range(8):
        b, h = c // 2, c % 2
        m = dict(base)
        m["x_own"] = np.ascontiguousarray(x[b, h * TOK:(h + 1) * TOK])
        m["xT_full"] = np.ascontiguousarray(x[b].T)
        m["xT_own"] = np.ascontiguousarray(x[b, h * TOK:(h + 1) * TOK].T)
        m["x_t0"] = np.ascontiguousarray(x[b, 0:128])
        m["wqkv"] = np.ascontiguousarray(np.concatenate(
            [W_qkv[:, 512 * h:512 * (h + 1)],
             W_qkv[:, D + 512 * h:D + 512 * (h + 1)],
             W_qkv[:, 2 * D + 512 * h:2 * D + 512 * (h + 1)]], axis=1))
        m["own_mask"] = np.full((128, 1), 1.0 - h, dtype=np.float32)
        in_maps.append(m)
    return in_maps


def _bcast(v):
    v = np.asarray(v, dtype=np.float32).reshape(-1)
    return np.ascontiguousarray(np.broadcast_to(v[None, :], (128, v.size)))


def kernel(**inputs):
    in_maps = _prep(inputs)
    nc = build()
    res = run_bass_kernel_spmd(nc, in_maps, core_ids=list(range(8)))
    out = np.empty((4, S, D), dtype=np.float32)
    for c in range(8):
        b, h = c // 2, c % 2
        out[b, h * TOK:(h + 1) * TOK] = res.results[c]["y_out"]
    kernel.last_results = res
    return out


# version 19
# speedup vs baseline: 1.0680x; 1.0456x over previous
import contextlib
import numpy as np
import ml_dtypes
import concourse.bass as bass
import concourse.mybir as mybir
from concourse.bass_utils import run_bass_kernel_spmd

F32 = mybir.dt.float32
BF16 = mybir.dt.bfloat16
AX = mybir.AxisListType.X
ALU = mybir.AluOpType
ACT = mybir.ActivationFunctionType
BF = ml_dtypes.bfloat16

TAPS = False
PHASES = 4
D = 1024
S = 4096
TOK = 2048
NT = 16
NTF = 32
NDP = 8
HF = 4096
NEG = -1.0e30


class Sem:
    def __init__(self, h):
        self.h = h
        self.n = 0

    def inc(self, instr, k=1):
        instr.then_inc(self.h, k)
        self.n += k
        return self.n


def build():
    nc = bass.Bass("TRN2", target_bir_lowering=False, debug=False)

    def inp(name, shape, dt=F32):
        return nc.dram_tensor(name, shape, dt, kind="ExternalInput").ap()

    x_own = inp("x_own", [TOK, D])
    xT_full = inp("xT_full", [D, S])
    xT_own = inp("xT_own", [D, TOK])
    x_t0 = inp("x_t0", [128, D])

    wz = inp("wz", [D, D])
    wdbc = inp("wdbc", [D, 48])
    bz_bc = inp("bz_bc", [128, D])
    bdbc_bc = inp("bdbc_bc", [128, 48])
    alog = inp("alog", [16, 1])
    wout = inp("wout", [D, D])
    bout_bc = inp("bout_bc", [128, D])
    w1T = inp("w1T", [D, HF])
    b1c = inp("b1c", [128, HF // 128])
    w2T = inp("w2T", [HF, D])
    b2_bc = inp("b2_bc", [128, D])
    scale_bc = inp("scale_bc", [128, 1])

    wqkv = inp("wqkv", [D, 1536])
    wo = inp("wo", [D, D])
    gwT = inp("gwT", [128, 48])
    gb_bc = inp("gb_bc", [128, 6])
    ew1T = inp("ew1T", [6, D, D])
    ew2T = inp("ew2T", [6, D, D])
    ew3T = inp("ew3T", [6, D, D])
    fc1T = inp("fc1T", [D, D])
    fc2T = inp("fc2T", [D, D])
    fc3T = inp("fc3T", [D, D])

    ident = inp("ident", [128, 128], BF16)
    identf = inp("identf", [128, 128])
    cos_t = inp("cos_t", [128, NTF * 32], BF16)
    sin_t = inp("sin_t", [128, NTF * 32], BF16)
    mask1 = inp("mask1", [128, 128])
    mask2 = inp("mask2", [128, 512])
    iota6 = inp("iota6", [128, 6])
    own_mask = inp("own_mask", [128, 1])

    y_out = nc.dram_tensor("y_out", [TOK, D], F32, kind="ExternalOutput").ap()
    taps = {}
    if TAPS:
        for t in ("tap_x1", "tap_x2", "tap_x3"):
            taps[t] = nc.dram_tensor(t, [TOK, D], F32, kind="ExternalOutput").ap()
        taps["tap_dbc"] = nc.dram_tensor("tap_dbc", [16, 3 * S], F32, kind="ExternalOutput").ap()
        taps["tap_yc"] = nc.dram_tensor("tap_yc", [128, 34], F32, kind="ExternalOutput").ap()

    sp_x1 = nc.dram_tensor("sp_x1", [TOK + 128, D], F32).ap()
    sp_x2 = nc.dram_tensor("sp_x2", [TOK, D], F32).ap()
    w2T_bf = nc.dram_tensor("w2T_bf", [HF, D], BF16).ap()
    w1bf = nc.dram_tensor("w1bf", [D, HF], BF16).ap()
    ewbf = nc.dram_tensor("ewbf", [21 * 1024, 1024], BF16).ap()
    dbcT_hbm = nc.dram_tensor("dbcT_hbm", [16, 3 * S], F32).ap()
    cc_x2_in_a = nc.dram_tensor("cc_x2_in_a", [D, TOK // 2], BF16)
    cc_x2_out_a = nc.dram_tensor("cc_x2_out_a", [2 * D, TOK // 2], BF16)
    cc_x2_in_b = nc.dram_tensor("cc_x2_in_b", [D, TOK // 2], BF16)
    cc_x2_out_b = nc.dram_tensor("cc_x2_out_b", [2 * D, TOK // 2], BF16)
    cc_y_in_a = nc.dram_tensor("cc_y_in_a", [S, 256], BF16)
    cc_y_out_a = nc.dram_tensor("cc_y_out_a", [2 * S, 256], BF16)
    cc_y_in_b = nc.dram_tensor("cc_y_in_b", [S, 256], BF16)
    cc_y_out_b = nc.dram_tensor("cc_y_out_b", [2 * S, 256], BF16)
    RG = [[0, 1], [2, 3], [4, 5], [6, 7]]

    ctx = contextlib.ExitStack()
    E = ctx.enter_context
    with ctx:
        XT = E(nc.sbuf_tensor("XT", [128, NDP * TOK], BF16))
        stg = E(nc.sbuf_tensor("stg", [128, 4096], F32))
        wf0 = [E(nc.sbuf_tensor(f"wf0{i}", [128, D], F32)) for i in range(2)]
        wf1 = [E(nc.sbuf_tensor(f"wf1{i}", [128, D], F32)) for i in range(2)]
        xbft = E(nc.sbuf_tensor("xbft", [128, D], BF16))
        gTt = E(nc.sbuf_tensor("gTt", [128, D], BF16))
        stat = E(nc.sbuf_tensor("stat", [128, 16], F32))
        ycols = E(nc.sbuf_tensor("ycols", [128, 34], F32))
        idn = E(nc.sbuf_tensor("idn", [128, 128], BF16))
        idnf = E(nc.sbuf_tensor("idnf", [128, 128], F32))
        omsk = E(nc.sbuf_tensor("omsk", [128, 1], F32))
        x1t0 = E(nc.sbuf_tensor("x1t0", [128, D], F32))
        dmy = E(nc.sbuf_tensor("dmy", [128, D], BF16))

        pb = [E(nc.psum_tensor(f"pb{i}", [128, 512], F32)) for i in range(6)]
        ptb = E(nc.psum_tensor("ptb", [128, 1024], BF16))
        psm = E(nc.psum_tensor("psm", [128, 512], F32))

        sT = Sem(E(nc.semaphore("sT")))
        sV = Sem(E(nc.semaphore("sV")))
        sS = Sem(E(nc.semaphore("sS")))
        sC = Sem(E(nc.semaphore("sC")))
        sO = Sem(E(nc.semaphore("sO")))
        sK = Sem(E(nc.semaphore("sK")))
        sX = Sem(E(nc.semaphore("sX")))
        sGW = Sem(E(nc.semaphore("sGW")))
        sR = [Sem(E(nc.semaphore(f"sR{i}"))) for i in range(2)]
        sP = [Sem(E(nc.semaphore(f"sP{i}"))) for i in range(2)]
        sDB = [Sem(E(nc.semaphore(f"sDB{i}"))) for i in range(2)]
        sDB2 = Sem(E(nc.semaphore("sDB2")))
        sWS = [Sem(E(nc.semaphore(f"sWS{i}"))) for i in range(2)]
        sL = [Sem(E(nc.semaphore(f"sL{i}"))) for i in range(2)]
        sXS = [Sem(E(nc.semaphore(f"sXS{i}"))) for i in range(2)]
        sKB = Sem(E(nc.semaphore("sKB")))
        sY = Sem(E(nc.semaphore("sY")))
        sLT = Sem(E(nc.semaphore("sLT")))
        sCVl = [Sem(E(nc.semaphore(f"sCVl{i}"))) for i in range(2)]
        sCVs = [Sem(E(nc.semaphore(f"sCVs{i}"))) for i in range(2)]
        sCVv = Sem(E(nc.semaphore("sCVv")))
        sWL = [Sem(E(nc.semaphore(f"sWL{i}"))) for i in range(2)]
        sWC = Sem(E(nc.semaphore("sWC")))
        sWSt = [Sem(E(nc.semaphore(f"sWSt{i}"))) for i in range(2)]
        sW1L = Sem(E(nc.semaphore("sW1L")))
        sGAB = Sem(E(nc.semaphore("sGAB")))
        sGC = Sem(E(nc.semaphore("sGC")))
        sLN = Sem(E(nc.semaphore("sLN")))
        sX2 = [Sem(E(nc.semaphore(f"sX2{i}"))) for i in range(2)]
        sXSa = [Sem(E(nc.semaphore(f"sXSa{i}"))) for i in range(2)]
        sYa = Sem(E(nc.semaphore("sYa")))
        block = E(nc.Block())

        paCtx = contextlib.ExitStack()
        with paCtx:
            EA = paCtx.enter_context
            regCw = EA(nc.sbuf_tensor("regCw", [128, 16768], BF16))
            bzs = EA(nc.sbuf_tensor("bzs", [128, D], F32))
            bouts = EA(nc.sbuf_tensor("bouts", [128, D], F32))
            bdbcs = EA(nc.sbuf_tensor("bdbcs", [128, 48], F32))
            dbc_ev = [EA(nc.sbuf_tensor(f"dbc_ev{i}", [128, 48], F32)) for i in range(2)]
            acol = EA(nc.sbuf_tensor("acol", [16, 1], F32))
            dbcT_ev = [EA(nc.sbuf_tensor(f"dbcT_ev{i}", [16, 384], F32)) for i in range(2)]
            x0t0b = EA(nc.sbuf_tensor("x0t0b", [128, NDP * 128], BF16))
            ones16 = EA(nc.sbuf_tensor("ones16", [16, 1], F32))

            WOFF_Z, WOFF_DBC, WOFF_OUT = 0, 8192, 8576

            def wzb(p):
                return regCw[:, WOFF_Z + 1024 * p:WOFF_Z + 1024 * (p + 1)]

            def wdbcb(p):
                return regCw[:, WOFF_DBC + 48 * p:WOFF_DBC + 48 * (p + 1)]

            def woutb(p):
                return regCw[:, WOFF_OUT + 1024 * p:WOFF_OUT + 1024 * (p + 1)]

            def p0_sync(sync):
                for dst, src in ((bzs, bz_bc), (bouts, bout_bc), (bdbcs, bdbc_bc),
                                 (idn, ident), (idnf, identf), (omsk, own_mask)):
                    sK.inc(sync.dma_start(out=dst[:], in_=src), 16)
                sK.inc(sync.dma_start(out=stat[0:16, 8:9], in_=alog), 16)
            block.sync(p0_sync)
            P0_CONST = sK.n

            xtbCtx = contextlib.ExitStack()
            with xtbCtx:
                xTbBuf = xtbCtx.enter_context(
                    nc.sbuf_tensor("xTbBuf", [128, NDP * 4096], BF16))

                def xTb(p):
                    return xTbBuf[:, 4096 * p:4096 * (p + 1)]

                S0 = sS.n

                def p0_sync_xt(sync):
                    for u in range(2 * NDP):
                        if u >= 2:
                            sync.wait_ge(sS.h, S0 + u - 1)
                        p, hf = u // 2, u % 2
                        sX2[u % 2].inc(sync.dma_start(
                            out=stg[:, 2048 * (u % 2):2048 * (u % 2 + 1)],
                            in_=xT_full[128 * p:128 * (p + 1),
                                        2048 * hf:2048 * (hf + 1)]), 16)
                block.sync(p0_sync_xt)

                def p0_scalar_xt(scalar):
                    for u in range(2 * NDP):
                        p, hf = u // 2, u % 2
                        scalar.wait_ge(sX2[u % 2].h, 16 * (u // 2 + 1))
                        sS.inc(scalar.copy(
                            out=xTb(p)[:, 2048 * hf:2048 * (hf + 1)],
                            in_=stg[:, 2048 * (u % 2):2048 * (u % 2 + 1)]))
                block.scalar(p0_scalar_xt)

                S1 = sS.n

                def p0_sync_w(sync):
                    for p in range(NDP):
                        sync.wait_ge(sS.h, S1 + 3 * p)
                        sX.inc(sync.dma_start(out=stg[:, 0:1024],
                                              in_=wz[128 * p:128 * (p + 1), :]), 16)
                        sX.inc(sync.dma_start(out=stg[:, 1024:1072],
                                              in_=wdbc[128 * p:128 * (p + 1), :]), 16)
                        sX.inc(sync.dma_start(out=stg[:, 1072:2096],
                                              in_=wout[128 * p:128 * (p + 1), :]), 16)
                block.sync(p0_sync_w)
                W_W = sX.n

                def p0_scalar_w(scalar):
                    for p in range(NDP):
                        scalar.wait_ge(sX.h, W_W - 48 * (NDP - 1 - p))
                        sS.inc(scalar.copy(out=wzb(p), in_=stg[:, 0:1024]))
                        sS.inc(scalar.copy(out=wdbcb(p), in_=stg[:, 1024:1072]))
                        sS.inc(scalar.copy(out=woutb(p), in_=stg[:, 1072:2096]))
                block.scalar(p0_scalar_w)
                P0_WCAST = sS.n

                S2 = sS.n

                def p0_sync_xto(sync):
                    for p in range(NDP):
                        sync.wait_ge(sS.h, S2 + p)
                        sX.inc(sync.dma_start(out=stg[:, 0:2048],
                                              in_=xT_own[128 * p:128 * (p + 1), :]), 16)
                block.sync(p0_sync_xto)
                W_XTO = sX.n

                def p0_scalar_xto(scalar):
                    for p in range(NDP):
                        scalar.wait_ge(sX.h, W_XTO - 16 * (NDP - 1 - p))
                        sS.inc(scalar.copy(out=XT[:, 2048 * p:2048 * (p + 1)],
                                           in_=stg[:, 0:2048]))
                block.scalar(p0_scalar_xto)
                P0_S = sS.n

                def p0_vec_t0(vector):
                    vector.wait_ge(sS.h, S0 + 2 * NDP)
                    xv = xTbBuf.rearrange("q (p c) -> q p c", c=4096)
                    ov = x0t0b.rearrange("q (p c) -> q p c", c=128)
                    sV.inc(vector.tensor_copy(out=ov[:, :, :], in_=xv[:, :, 0:128]))
                block.vector(p0_vec_t0)
                P0_V = sV.n

                TA0, VA0, SA0 = sT.n, sV.n, sS.n

                def pa1_tensor(tensor):
                    tensor.wait_ge(sS.h, P0_WCAST)
                    tensor.wait_ge(sK.h, P0_CONST)
                    for t in range(NTF):
                        bank = pb[4 + (t % 2)]
                        if t >= 2:
                            tensor.wait_ge(sV.h, VA0 + (t - 2) + 1)
                        for p in range(NDP):
                            ins = tensor.matmul(out=bank[:, 0:48],
                                                lhsT=xTb(p)[:, 128 * t:128 * (t + 1)],
                                                rhs=wdbcb(p), start=(p == 0),
                                                stop=(p == NDP - 1))
                        sT.inc(ins)
                        if t >= 1:
                            tensor.wait_ge(sV.h, VA0 + t)
                            if t >= 2:
                                tensor.wait_ge(sS.h, SA0 + (t - 1))
                            for j in range(3):
                                ins = tensor.transpose(
                                    out=pb[3][0:16, 128 * j:128 * (j + 1)],
                                    in_=dbc_ev[(t - 1) % 2][:, 16 * j:16 * (j + 1)],
                                    identity=idnf[:])
                            sT.inc(ins)
                    tensor.wait_ge(sV.h, VA0 + NTF)
                    tensor.wait_ge(sS.h, SA0 + (NTF - 1))
                    for j in range(3):
                        ins = tensor.transpose(out=pb[3][0:16, 128 * j:128 * (j + 1)],
                                               in_=dbc_ev[(NTF - 1) % 2][:, 16 * j:16 * (j + 1)],
                                               identity=idnf[:])
                    sT.inc(ins)
                block.tensor(pa1_tensor)

                def pa1_mm_n(t):
                    return TA0 + (1 if t == 0 else 2 * t)

                def pa1_tr_n(t):
                    return TA0 + (2 * (t + 1) + 1 if t < NTF - 1 else 2 * NTF)

                def pa1_vector(vector):
                    for t in range(NTF):
                        vector.wait_ge(sT.h, pa1_mm_n(t))
                        if t >= 2:
                            vector.wait_ge(sS.h, SA0 + (t - 2) + 1)
                        bank = pb[4 + (t % 2)]
                        sV.inc(vector.scalar_tensor_tensor(
                            out=dbc_ev[t % 2][:, :], in0=bank[:, 0:48], scalar=1.0,
                            in1=bdbcs[:, :], op0=ALU.mult, op1=ALU.add))
                block.vector(pa1_vector)

                def pa1_scalar(scalar):
                    for t in range(NTF):
                        scalar.wait_ge(sT.h, pa1_tr_n(t))
                        if t >= 2:
                            scalar.wait_ge(sDB[t % 2].h, 16 * (t // 2))
                        sS.inc(scalar.copy(out=dbcT_ev[t % 2][:, :],
                                           in_=pb[3][0:16, 0:384]))
                block.scalar(pa1_scalar)
                SA1 = sS.n

                def pa1_gpsimd(gpsimd):
                    ov = dbcT_hbm.rearrange("p (k c) -> p k c", k=3)
                    for t in range(NTF):
                        gpsimd.wait_ge(sS.h, SA0 + t + 1)
                        iv = dbcT_ev[t % 2].rearrange("p (k c) -> p k c", k=3)
                        sDB[t % 2].inc(gpsimd.dma_start(
                            out=ov[:, :, 128 * t:128 * (t + 1)], in_=iv[:, :, :]), 16)
                block.gpsimd(pa1_gpsimd)
            scanCtx = contextlib.ExitStack()
            with scanCtx:
                scanBuf = scanCtx.enter_context(
                    nc.sbuf_tensor("scanBuf", [16, 4 * 4096], F32))
                dB = scanBuf[:, 0:4096]
                dC = scanBuf[:, 4096:8192]
                dDp = scanBuf[:, 8192:12288]
                sc1 = scanBuf[:, 12288:16384]
                sc2 = stg[0:16, 0:4096]

                def pa2_gpsimd(gpsimd):
                    gpsimd.wait_ge(sDB[0].h, sDB[0].n)
                    gpsimd.wait_ge(sDB[1].h, sDB[1].n)
                    sDB2.inc(gpsimd.dma_start(out=scanBuf[:, 0:12288],
                                              in_=dbcT_hbm[:, :]), 16)
                block.gpsimd(pa2_gpsimd)

                def pa2_scalar(scalar):
                    scalar.wait_ge(sS.h, SA1)
                    scalar.wait_ge(sDB2.h, 16)
                    sS.inc(scalar.activation(out=acol[:, 0:1], in_=stat[0:16, 8:9],
                                             func=ACT.Exp))
                    sS.inc(scalar.activation(out=sc1[:, :], in_=dDp[:, :], func=ACT.Relu))
                    sS.inc(scalar.activation(out=sc2[:, :], in_=dDp[:, :], func=ACT.Abs))
                    scalar.wait_ge(sS.h, sS.n)
                    sS.inc(scalar.activation(out=sc2[:, :], in_=sc2[:, :], func=ACT.Exp, scale=-1.0))
                    scalar.wait_ge(sS.h, sS.n)
                    sS.inc(scalar.activation(out=sc2[:, :], in_=sc2[:, :], func=ACT.Ln, bias=1.0))
                block.scalar(pa2_scalar)
                SA2 = sS.n

                def pa2_vector(vector):
                    vector.wait_ge(sS.h, SA2)
                    sV.inc(vector.tensor_tensor(out=sc1[:, :], in0=sc1[:, :], in1=sc2[:, :],
                                                op=ALU.add))
                    sV.inc(vector.tensor_scalar(out=acol[:, 0:1], in0=acol[:, 0:1],
                                                scalar1=-1.0, scalar2=None, op0=ALU.mult))
                    vector.wait_ge(sV.h, sV.n)
                    sV.inc(vector.tensor_scalar(out=sc1[:, :], in0=sc1[:, :],
                                                scalar1=acol[0:16, 0:1], scalar2=None,
                                                op0=ALU.mult))
                block.vector(pa2_vector)
                VA2 = sV.n

                def pa2_scalar2(scalar):
                    scalar.wait_ge(sV.h, VA2)
                    sS.inc(scalar.activation(out=sc1[:, :], in_=sc1[:, :], func=ACT.Exp))
                block.scalar(pa2_scalar2)
                SA3 = sS.n

                def pa2_vector2(vector):
                    vector.wait_ge(sS.h, SA3)
                    sV.inc(vector.tensor_tensor_scan(out=sc2[:, :], data0=sc1[:, :],
                                                     data1=dB[:, :], initial=0.0,
                                                     op0=ALU.mult, op1=ALU.add))
                    vector.wait_ge(sV.h, sV.n)
                    sV.inc(vector.tensor_tensor(out=sc2[:, :], in0=sc2[:, :], in1=dC[:, :],
                                                op=ALU.mult))
                    sV.inc(vector.memset(ones16[:, 0:1], 1.0))
                    sV.inc(vector.memset(stat[:, 5:6], 1e-5))
                    sV.inc(vector.memset(ycols[:, :], 0.0))
                block.vector(pa2_vector2)
                VA3 = sV.n

                def pa2_tensor(tensor):
                    tensor.wait_ge(sV.h, VA3)
                    for t in range(NTF):
                        ins = tensor.matmul(out=psm[:, t:t + 1],
                                            lhsT=sc2[:, 128 * t:128 * (t + 1)],
                                            rhs=ones16[:, 0:1], start=True, stop=True,
                                            skip_group_check=True)
                    sT.inc(ins)
                block.tensor(pa2_tensor)
                TA2 = sT.n

                def pa2_scalar3(scalar):
                    scalar.wait_ge(sT.h, TA2)
                    sS.inc(scalar.copy(out=ycols[:, 0:NTF], in_=psm[:, 0:NTF]))
                    sS.inc(scalar.copy(out=ycols[:, 32:33], in_=psm[:, 0:1]))
                block.scalar(pa2_scalar3)
                SA4 = sS.n

                def pa2_vector3(vector):
                    vector.wait_ge(sS.h, SA4)
                    vector.wait_ge(sK.h, P0_CONST)
                    sV.inc(vector.tensor_sub(out=ycols[:, 0:NT], in0=ycols[:, 0:NT],
                                              in1=ycols[:, NT:2 * NT]))
                    vector.wait_ge(sV.h, sV.n)
                    sV.inc(vector.scalar_tensor_tensor(
                        out=ycols[:, 0:NT], in0=ycols[:, 0:NT], scalar=omsk[:, 0:1],
                        in1=ycols[:, NT:2 * NT], op0=ALU.mult, op1=ALU.add))
                block.vector(pa2_vector3)
                VA4 = sV.n

            TB0, VB0, SB0 = sT.n, sV.n, sS.n
            TPS, VPS, SPS = 4, 8, 8

            def t_n(s, k):
                return TB0 + TPS * s + k

            def v_n(s, k):
                return VB0 + VPS * s + k

            def s_n(s, k):
                return SB0 + SPS * s + k

            def lhsrc(s, p):
                if s == 0:
                    return x0t0b[:, 128 * p:128 * (p + 1)]
                return XT[:, 2048 * p + 128 * (s - 1):2048 * p + 128 * s]

            wcb = EA(nc.sbuf_tensor("wcb", [128, 4096], BF16))
            wcbv = wcb.rearrange("p (a d) -> p a d", d=1024)
            stgv3 = stg.rearrange("p (a d) -> p a d", d=1024)
            w2v_pa = w2T.rearrange("(a p) d -> p a d", p=128)
            w2bv_pa = w2T_bf.rearrange("(a p) d -> p a d", p=128)
            NWC = 32

            def pa3_gpsimd_conv(gpsimd):
                def _wstore(c):
                    if c < 16:
                        p_, h_ = c // 2, c % 2
                        ins = gpsimd.dma_start(
                            out=w1bf[128 * p_:128 * (p_ + 1), 2048 * h_:2048 * (h_ + 1)],
                            in_=wcb[:, 2048 * (c % 2):2048 * (c % 2 + 1)])
                    else:
                        r = c - 16
                        ins = gpsimd.dma_start(
                            out=w2bv_pa[:, 2 * r:2 * r + 2, :],
                            in_=wcbv[:, 2 * (c % 2):2 * (c % 2) + 2, :])
                    sWSt[c % 2].inc(ins, 16)
                for c in range(NWC):
                    if c >= 2:
                        gpsimd.wait_ge(sWC.h, c - 1)
                        _wstore(c - 2)
                    else:
                        gpsimd.wait_ge(sT.h, TA2)
                    if c < 16:
                        p_, h_ = c // 2, c % 2
                        ins = gpsimd.dma_start(
                            out=stg[:, 2048 * (c % 2):2048 * (c % 2 + 1)],
                            in_=w1T[128 * p_:128 * (p_ + 1), 2048 * h_:2048 * (h_ + 1)])
                    else:
                        r = c - 16
                        ins = gpsimd.dma_start(
                            out=stgv3[:, 2 * (c % 2):2 * (c % 2) + 2, :],
                            in_=w2v_pa[:, 2 * r:2 * r + 2, :])
                    sWL[c % 2].inc(ins, 16)
                for c in (NWC - 2, NWC - 1):
                    gpsimd.wait_ge(sWC.h, c + 1)
                    _wstore(c)
            block.gpsimd(pa3_gpsimd_conv)
            WST_A = [sWSt[0].n, sWSt[1].n]

            _wc_next = [0]

            def _wc_cast(scalar, budget):
                for _ in range(budget):
                    c = _wc_next[0]
                    if c >= NWC:
                        return
                    _wc_next[0] = c + 1
                    scalar.wait_ge(sWL[c % 2].h, 16 * (c // 2 + 1))
                    if c >= 2:
                        scalar.wait_ge(sWSt[c % 2].h, 16 * (c // 2))
                    ins = scalar.copy(
                        out=wcb[:, 2048 * (c % 2):2048 * (c % 2 + 1)],
                        in_=stg[:, 2048 * (c % 2):2048 * (c % 2 + 1)])
                    sWC.inc(ins)

            def pa3_sync(sync):
                sync.wait_ge(sV.h, VA4)
                for s in range(NT + 1):
                    if s >= 2:
                        sync.wait_ge(sV.h, v_n(s - 2, 5))
                    src = x_t0 if s == 0 else x_own[128 * (s - 1):128 * s, :]
                    sR[s % 2].inc(sync.dma_start(out=wf1[s % 2][:, :], in_=src), 16)
                    if s >= 1:
                        sync.wait_ge(sV.h, v_n(s - 1, 8))
                        sP[(s - 1) % 2].inc(sync.dma_start(
                            out=sp_x1[128 * (NT if s - 1 == 0 else s - 2):
                                      128 * (NT + 1 if s - 1 == 0 else s - 1), :],
                            in_=wf0[(s - 1) % 2][:, :]), 16)
                sync.wait_ge(sV.h, v_n(NT, 8))
                sP[NT % 2].inc(sync.dma_start(out=sp_x1[128 * (NT - 1):128 * NT, :],
                                              in_=wf0[NT % 2][:, :]), 16)
            block.sync(pa3_sync)
            PA3_P0, PA3_P1 = sP[0].n, sP[1].n
            PA3_R0, PA3_R1 = sR[0].n, sR[1].n

            def pa3_tensor(tensor):
                tensor.wait_ge(sV.h, VA4)
                for s in range(NT + 1):
                    A = (s % 3) * 2
                    if s >= 3:
                        tensor.wait_ge(sV.h, v_n(s - 3, 5))
                    for p in range(NDP):
                        for nch in range(2):
                            ins = tensor.matmul(
                                out=pb[A + nch][:, :],
                                lhsT=lhsrc(s, p),
                                rhs=wzb(p)[:, 512 * nch:512 * (nch + 1)],
                                start=(p == 0), stop=(p == NDP - 1))
                    sT.inc(ins)
                    tensor.wait_ge(sV.h, v_n(s, 3))
                    if s >= 1:
                        tensor.wait_ge(sS.h, s_n(s - 1, 8))
                    for j in range(NDP):
                        ins = tensor.transpose(out=ptb[:, 128 * j:128 * (j + 1)],
                                               in_=xbft[:, 128 * j:128 * (j + 1)],
                                               identity=idn[:])
                    sT.inc(ins)
                    tensor.wait_ge(sS.h, s_n(s, 2))
                    for p in range(NDP):
                        for nch in range(2):
                            ins = tensor.matmul(
                                out=pb[A + nch][:, :],
                                lhsT=gTt[:, 128 * p:128 * (p + 1)],
                                rhs=woutb(p)[:, 512 * nch:512 * (nch + 1)],
                                start=(p == 0), stop=(p == NDP - 1))
                    sT.inc(ins)
                    tensor.wait_ge(sS.h, s_n(s, 7))
                    for j in range(NDP):
                        ins = tensor.transpose(out=ptb[:, 128 * j:128 * (j + 1)],
                                               in_=xbft[:, 128 * j:128 * (j + 1)],
                                               identity=idn[:])
                    sT.inc(ins)
            block.tensor(pa3_tensor)

            def pa3_vector(vector):
                for s in range(NT + 1):
                    A = (s % 3) * 2
                    vector.wait_ge(sT.h, t_n(s, 1))
                    if s >= 2:
                        vector.wait_ge(sS.h, s_n(s - 2, 8))
                        vector.wait_ge(sP[s % 2].h, 16 * (s // 2))
                    sV.inc(vector.scalar_tensor_tensor(out=wf0[s % 2][:, 0:512],
                                                       in0=pb[A][:, :],
                                                       scalar=1.0, in1=bzs[:, 0:512],
                                                       op0=ALU.mult, op1=ALU.add))
                    sV.inc(vector.scalar_tensor_tensor(out=wf0[s % 2][:, 512:1024],
                                                       in0=pb[A + 1][:, :], scalar=1.0,
                                                       in1=bzs[:, 512:1024],
                                                       op0=ALU.mult, op1=ALU.add))
                    vector.wait_ge(sS.h, s_n(s, 1))
                    if s >= 1:
                        vector.wait_ge(sT.h, t_n(s - 1, 4))
                    ycol = ycols[:, 32:33] if s == 0 else ycols[:, s - 1:s]
                    sV.inc(vector.tensor_scalar(out=xbft[:, :], in0=wf0[s % 2][:, :],
                                                scalar1=ycol, scalar2=None, op0=ALU.mult))
                    vector.wait_ge(sT.h, t_n(s, 3))
                    vector.wait_ge(sR[s % 2].h, 16 * (s // 2 + 1))
                    vector.scalar_tensor_tensor(out=wf0[s % 2][:, 0:512], in0=pb[A][:, :],
                                                scalar=1.0, in1=wf1[s % 2][:, 0:512],
                                                op0=ALU.mult, op1=ALU.add)
                    sV.inc(vector.scalar_tensor_tensor(out=wf0[s % 2][:, 512:1024],
                                                       in0=pb[A + 1][:, :], scalar=1.0,
                                                       in1=wf1[s % 2][:, 512:1024],
                                                       op0=ALU.mult, op1=ALU.add))
                    vector.wait_ge(sV.h, sV.n)
                    vector.tensor_tensor(out=wf0[s % 2][:, 0:512], in0=wf0[s % 2][:, 0:512],
                                         in1=bouts[:, 0:512], op=ALU.add)
                    sV.inc(vector.tensor_tensor(out=wf0[s % 2][:, 512:1024],
                                                in0=wf0[s % 2][:, 512:1024],
                                                in1=bouts[:, 512:1024], op=ALU.add))
                    vector.wait_ge(sV.h, sV.n)
                    sV.inc(vector.tensor_reduce(out=stat[:, 0:1], in_=wf0[s % 2][:, :],
                                                axis=AX, op=ALU.add))
                    vector.wait_ge(sS.h, s_n(s, 6))
                    sV.inc(vector.reciprocal(out=stat[:, 4:5], in_=stat[:, 3:4]))
                    vector.wait_ge(sV.h, sV.n)
                    sV.inc(vector.tensor_scalar(out=wf0[s % 2][:, :], in0=wf0[s % 2][:, :],
                                                scalar1=stat[:, 4:5], scalar2=None,
                                                op0=ALU.mult))
            block.vector(pa3_vector)

            def pa3_scalar(scalar):
                for s in range(NT + 1):
                    _wc_cast(scalar, 2)
                    scalar.wait_ge(sV.h, v_n(s, 2))
                    sS.inc(scalar.activation(out=wf0[s % 2][:, :], in_=wf0[s % 2][:, :],
                                             func=ACT.Gelu))
                    scalar.wait_ge(sT.h, t_n(s, 2))
                    if s >= 1:
                        scalar.wait_ge(sT.h, t_n(s - 1, 3))
                    sS.inc(scalar.copy(out=gTt[:, :], in_=ptb[:, :]))
                    scalar.wait_ge(sV.h, v_n(s, 6))
                    sS.inc(scalar.mul(out=stat[:, 1:2], in_=stat[:, 0:1], mul=-1.0 / D))
                    scalar.wait_ge(sS.h, sS.n)
                    sS.inc(scalar.activation(out=wf0[s % 2][:, :], in_=wf0[s % 2][:, :],
                                             func=ACT.Identity, bias=stat[:, 1:2], scale=1.0))
                    scalar.wait_ge(sS.h, sS.n)
                    sS.inc(scalar.activation(out=dmy[:, :], in_=wf0[s % 2][:, :],
                                             func=ACT.Square, accum_out=stat[:, 2:3]))
                    scalar.wait_ge(sS.h, sS.n)
                    sS.inc(scalar.activation(out=stat[:, 3:4], in_=stat[:, 2:3],
                                             func=ACT.Sqrt, scale=1.0 / D,
                                             bias=stat[:, 5:6]))
                    scalar.wait_ge(sV.h, v_n(s, 8))
                    sS.inc(scalar.copy(out=xbft[:, :], in_=wf0[s % 2][:, :]))
                    scalar.wait_ge(sT.h, t_n(s, 4))
                    if s == 0:
                        sS.inc(scalar.copy(out=x1t0[:, :], in_=wf0[0][:, :]))
                    else:
                        ov = XT.rearrange("q (p c) -> q p c", c=TOK)
                        sS.inc(scalar.copy(
                            out=ov[:, :, 128 * (s - 1):128 * s],
                            in_=ptb.rearrange("q (p c) -> q p c", c=128)[:, :, :]))
                _wc_cast(scalar, NWC)
            block.scalar(pa3_scalar)
            PA3_V, PA3_S, PA3_T = sV.n, sS.n, sT.n


        if PHASES >= 2:
            pbCtx = contextlib.ExitStack()
            with pbCtx:
                EB = pbCtx.enter_context
                w1TbBuf = EB(nc.sbuf_tensor("w1TbBuf", [128, NDP * 4096], BF16))
                hT = EB(nc.sbuf_tensor("hT", [128, 32 * 256], BF16))
                w2c = [EB(nc.sbuf_tensor(f"w2c{i}", [128, 1024], BF16)) for i in range(2)]
                b2s = EB(nc.sbuf_tensor("b2s", [128, D], F32))
                evodf = EB(nc.sbuf_tensor("evodf", [4, 4096], F32))
                ev1b = EB(nc.sbuf_tensor("ev1b", [4, D], BF16))
                ev4b = EB(nc.sbuf_tensor("ev4b", [4, D], BF16))
                od1b = EB(nc.sbuf_tensor("od1b", [4, D], BF16))
                od4b = EB(nc.sbuf_tensor("od4b", [4, D], BF16))
                ev1M = EB(nc.sbuf_tensor("ev1M", [4, HF], BF16))
                ErB = EB(nc.sbuf_tensor("ErB", [4, 4096], BF16))
                odT = EB(nc.sbuf_tensor("odT", [128, 64], BF16))
                u1b = EB(nc.sbuf_tensor("u1b", [4, 256], BF16))
                vtb = EB(nc.sbuf_tensor("vtb", [4, 1024], BF16))
                scls = EB(nc.sbuf_tensor("scls", [128, 1], F32))
                b1cs = EB(nc.sbuf_tensor("b1cs", [128, 32], F32))

                def w1s(ds, hs):
                    return w1TbBuf[:, 4096 * ds + 128 * hs:4096 * ds + 128 * (hs + 1)]

                SB_PA3 = PA3_S

                def pb_gpsimd_w(gpsimd):
                    gpsimd.wait_ge(sV.h, VA4)
                    sKB.inc(gpsimd.dma_start(out=b2s[:], in_=b2_bc), 16)
                    sKB.inc(gpsimd.dma_start(out=scls[:], in_=scale_bc), 16)
                    sKB.inc(gpsimd.dma_start(out=b1cs[:, :], in_=b1c), 16)
                    gpsimd.wait_ge(sT.h, PA3_T)
                    gpsimd.wait_ge(sWSt[0].h, WST_A[0])
                    gpsimd.wait_ge(sWSt[1].h, WST_A[1])
                    for p in range(NDP):
                        sW1L.inc(gpsimd.dma_start(
                            out=w1TbBuf[:, 4096 * p:4096 * (p + 1)],
                            in_=w1bf[128 * p:128 * (p + 1), :]), 16)
                block.gpsimd(pb_gpsimd_w)
                W1L = sW1L.n

                def pb_gpsimd_lora(gpsimd):
                    gpsimd.wait_ge(sS.h, SA4 + 8)
                    for j in range(4):
                        sLT.inc(gpsimd.dma_start(out=evodf[j:j + 1, 0:1024],
                                                 in_=x1t0[1 + 2 * j:2 + 2 * j, :]), 16)
                        sLT.inc(gpsimd.dma_start(out=evodf[j:j + 1, 1024:2048],
                                                 in_=x1t0[9 + 2 * j:10 + 2 * j, :]), 16)
                        sLT.inc(gpsimd.dma_start(out=evodf[j:j + 1, 2048:3072],
                                                 in_=x1t0[2 + 2 * j:3 + 2 * j, :]), 16)
                        sLT.inc(gpsimd.dma_start(out=evodf[j:j + 1, 3072:4096],
                                                 in_=x1t0[10 + 2 * j:11 + 2 * j, :]), 16)
                block.gpsimd(pb_gpsimd_lora)
                GB4 = sLT.n

                def pb_scalar_lora(scalar):
                    scalar.wait_ge(sLT.h, GB4)
                    sS.inc(scalar.memzero(ev1M[:, :]))
                    sS.inc(scalar.memzero(ErB[:, :]))
                    sS.inc(scalar.copy(out=ev1b[:, :], in_=evodf[:, 0:1024]))
                    sS.inc(scalar.copy(out=ev4b[:, :], in_=evodf[:, 1024:2048]))
                    sS.inc(scalar.copy(out=od1b[:, :], in_=evodf[:, 2048:3072]))
                    sS.inc(scalar.copy(out=od4b[:, :], in_=evodf[:, 3072:4096]))
                block.scalar(pb_scalar_lora)
                SB_L1 = sS.n

                def pb_gpsimd_lora2(gpsimd):
                    gpsimd.wait_ge(sS.h, SB_L1)
                    for j in range(4):
                        sLT.inc(gpsimd.dma_start(out=ev1M[j:j + 1, 1024 * j:1024 * (j + 1)],
                                                 in_=ev1b[j:j + 1, :]), 16)
                    for r in range(4):
                        for j in range(4):
                            sLT.inc(gpsimd.dma_start(
                                out=ErB[j:j + 1, 1024 * r + 256 * j:1024 * r + 256 * (j + 1)],
                                in_=ev4b[j:j + 1, r:1024:4],
                                allow_slow_non_contiguous=True), 16)
                block.gpsimd(pb_gpsimd_lora2)
                GB5 = sLT.n

                def pb_tensor_lora(tensor):
                    tensor.wait_ge(sS.h, SB_L1)
                    tensor.wait_ge(sS.h, SB_PA3)
                    for ds in range(NDP):
                        if ds > 0:
                            tensor.wait_ge(sS.h, SB_L1 + 2 * ds)
                        ins = tensor.transpose(out=ptb[:, 0:4], in_=od1b[:, 128 * ds:128 * (ds + 1)],
                                               identity=idn[0:4, 0:4])
                        sT.inc(ins)
                        ins = tensor.transpose(out=ptb[:, 4:8], in_=od4b[:, 128 * ds:128 * (ds + 1)],
                                               identity=idn[0:4, 0:4])
                        sT.inc(ins)
                block.tensor(pb_tensor_lora)
                TB_L = sT.n

                def pb_scalar_lora2(scalar):
                    for ds in range(NDP):
                        scalar.wait_ge(sT.h, TB_L - (2 * (NDP - 1 - ds) + 1))
                        sS.inc(scalar.copy(out=odT[:, 4 * ds:4 * (ds + 1)], in_=ptb[:, 0:4]))
                        scalar.wait_ge(sT.h, TB_L - 2 * (NDP - 1 - ds))
                        sS.inc(scalar.copy(out=odT[:, 32 + 4 * ds:32 + 4 * (ds + 1)],
                                           in_=ptb[:, 4:8]))
                block.scalar(pb_scalar_lora2)
                SB_L2 = sS.n

                TB0, VB0, SB0 = sT.n, sV.n, sS.n
                TPB, VPB, SPB = 70, 10, 47

                def bt_n(g, k):
                    return TB0 + TPB * g + k

                def bv_n(g, k):
                    return VB0 + VPB * g + k

                def bs_n(g, k):
                    return SB0 + SPB * g + k

                def bl_slab(g, hs):
                    return 16 * (16 * g + hs // 2 + 1)

                def pb_sync(sync):
                    sync.wait_ge(sP[0].h, PA3_P0)
                    sync.wait_ge(sP[1].h, PA3_P1)
                    for g in range(8):
                        for tt in range(2):
                            i = 2 * g + tt
                            if g >= 1:
                                sync.wait_ge(sV.h, bv_n(g - 1, 5 * tt + 5))
                            sR[tt].inc(sync.dma_start(out=wf1[tt][:, :],
                                                      in_=sp_x1[128 * i:128 * (i + 1), :]), 16)
                        for tt in range(2):
                            sync.wait_ge(sV.h, bv_n(g, 5 * tt + 5))
                            sP[tt].inc(sync.dma_start(out=sp_x2[256 * g + 128 * tt:
                                                                256 * g + 128 * (tt + 1), :],
                                                      in_=wf0[tt][:, :]), 16)
                block.sync(pb_sync)
                PB_R0, PB_P0, PB_P1 = sR[0].n, sP[0].n, sP[1].n

                def pb_tensor(tensor):
                    tensor.wait_ge(sW1L.h, W1L)
                    tensor.wait_ge(sLT.h, GB5)
                    tensor.wait_ge(sS.h, SB_L2)
                    for g in range(8):
                        if g >= 1:
                            tensor.wait_ge(sS.h, bs_n(g - 1, 35))
                        for ds in range(NDP):
                            ins = tensor.matmul(out=psm[0:4, 0:256], lhsT=odT[:, 4 * ds:4 * (ds + 1)],
                                                rhs=XT[:, 2048 * ds + 256 * g:2048 * ds + 256 * (g + 1)],
                                                start=(ds == 0), stop=(ds == NDP - 1),
                                                skip_group_check=True)
                        sT.inc(ins)
                        tensor.wait_ge(sS.h, bs_n(g, 1))
                        if g >= 1:
                            tensor.wait_ge(sT.h, bt_n(g - 1, 68))
                        for hs in range(32):
                            bank = pb[hs % 2]
                            if hs >= 2:
                                tensor.wait_ge(sS.h, bs_n(g, hs))
                            for ds in range(NDP):
                                tensor.matmul(out=bank[:, 0:256], lhsT=w1s(ds, hs),
                                              rhs=XT[:, 2048 * ds + 256 * g:2048 * ds + 256 * (g + 1)],
                                              start=(ds == 0), stop=False)
                            ins = tensor.matmul(out=bank[:, 0:256],
                                                lhsT=ev1M[0:4, 128 * hs:128 * (hs + 1)],
                                                rhs=u1b[0:4, :], start=False, stop=True)
                            sT.inc(ins)
                        tensor.wait_ge(sS.h, bs_n(g, 33))
                        for tt in range(2):
                            if g >= 1 or tt == 1:
                                tensor.wait_ge(sS.h, bs_n(g - 1 + tt, 35 - tt))
                            for r in range(4):
                                for ds in range(NDP):
                                    ins = tensor.matmul(
                                        out=psm[0:4, 128 * r:128 * (r + 1)],
                                        lhsT=odT[:, 32 + 4 * ds:32 + 4 * (ds + 1)],
                                        rhs=hT[:, 256 * (8 * r + ds) + 128 * tt:
                                               256 * (8 * r + ds) + 128 * (tt + 1)],
                                        start=(ds == 0), stop=(ds == NDP - 1),
                                        skip_group_check=True)
                            sT.inc(ins)
                        if g >= 1:
                            tensor.wait_ge(sV.h, bv_n(g - 1, 10))
                        for hs in range(32):
                            tensor.wait_ge(sL[hs % 2].h, bl_slab(g, hs))
                            for tt in range(2):
                                for nc2 in range(2):
                                    ins = tensor.matmul(
                                        out=pb[2 + 2 * tt + nc2][:, :],
                                        lhsT=hT[:, 256 * hs + 128 * tt:256 * hs + 128 * (tt + 1)],
                                        rhs=w2c[hs % 2][:, 512 * nc2:512 * (nc2 + 1)],
                                        start=(hs == 0), stop=False)
                            sT.inc(ins)
                        tensor.wait_ge(sS.h, bs_n(g, 35))
                        for tt in range(2):
                            for nc2 in range(2):
                                for r in range(4):
                                    ins = tensor.matmul(
                                        out=pb[2 + 2 * tt + nc2][:, :],
                                        lhsT=vtb[0:4, 512 * tt + 128 * r:512 * tt + 128 * (r + 1)],
                                        rhs=ErB[0:4, 1024 * r + 512 * nc2:1024 * r + 512 * (nc2 + 1)],
                                        start=False, stop=(r == 3))
                        sT.inc(ins)
                        for tt in range(2):
                            tensor.wait_ge(sS.h, bs_n(g, 40 + 6 * tt))
                            if g >= 1 or tt == 1:
                                tensor.wait_ge(sS.h, bs_n(g - 1 + tt, 47 - 6 * tt))
                            for j in range(NDP):
                                ins = tensor.transpose(out=ptb[:, 128 * j:128 * (j + 1)],
                                                       in_=xbft[:, 128 * j:128 * (j + 1)],
                                                       identity=idn[:])
                            sT.inc(ins)
                block.tensor(pb_tensor)

                CC_A = sC.n + 1

                def pb_gpsimd_main(gpsimd):
                    gpsimd.wait_ge(sLT.h, GB5)
                    gpsimd.wait_ge(sWSt[0].h, WST_A[0])
                    gpsimd.wait_ge(sWSt[1].h, WST_A[1])
                    for g in range(8):
                        for hs in range(32):
                            if hs >= 2:
                                gpsimd.wait_ge(sT.h, bt_n(g, 36 + hs - 2))
                            elif g >= 1:
                                gpsimd.wait_ge(sT.h, bt_n(g - 1, 68))
                            sL[hs % 2].inc(gpsimd.dma_start(
                                out=w2c[hs % 2][:, :],
                                in_=w2bv_pa[:, hs, :]), 16)
                        for tt in range(2):
                            i = 2 * g + tt
                            gpsimd.wait_ge(sS.h, bs_n(g, 41 + 6 * tt))
                            dst = cc_x2_in_a if i < 8 else cc_x2_in_b
                            col = 128 * (i % 8)
                            ov = dst.ap().rearrange("(j r) t -> r j t", r=128)
                            iv = xbft.rearrange("r (j c) -> r j c", c=128)
                            sem = sXSa[tt] if i < 8 else sXS[tt]
                            sem.inc(gpsimd.dma_start(out=ov[:, :, col:col + 128],
                                                     in_=iv[:, :, :]), 16)
                        if g == 3:
                            gpsimd.wait_ge(sXSa[0].h, sXSa[0].n)
                            gpsimd.wait_ge(sXSa[1].h, sXSa[1].n)
                            sC.inc(gpsimd.collective_compute(
                                "AllGather", ALU.bypass, replica_groups=RG,
                                ins=[cc_x2_in_a.ap()], outs=[cc_x2_out_a.ap()]))
                block.gpsimd(pb_gpsimd_main)

                def pb_vector(vector):
                    vector.wait_ge(sKB.h, 48)
                    for g in range(8):
                        for tt in range(2):
                            vector.wait_ge(sT.h, bt_n(g, 68))
                            vector.wait_ge(sR[tt].h, (PA3_R0 if tt == 0 else PA3_R1)
                                           + 16 * (g + 1))
                            if g >= 1:
                                vector.wait_ge(sP[tt].h, (PA3_P0 if tt == 0 else PA3_P1)
                                               + 16 * g)
                            b0, b1 = pb[2 + 2 * tt], pb[3 + 2 * tt]
                            vector.scalar_tensor_tensor(out=wf0[tt][:, 0:512], in0=b0[:, :],
                                                        scalar=1.0, in1=wf1[tt][:, 0:512],
                                                        op0=ALU.mult, op1=ALU.add)
                            sV.inc(vector.scalar_tensor_tensor(out=wf0[tt][:, 512:1024],
                                                               in0=b1[:, :],
                                                               scalar=1.0, in1=wf1[tt][:, 512:1024],
                                                               op0=ALU.mult, op1=ALU.add))
                            vector.wait_ge(sV.h, sV.n)
                            vector.tensor_tensor(out=wf0[tt][:, 0:512], in0=wf0[tt][:, 0:512],
                                                 in1=b2s[:, 0:512], op=ALU.add)
                            sV.inc(vector.tensor_tensor(out=wf0[tt][:, 512:1024],
                                                        in0=wf0[tt][:, 512:1024],
                                                        in1=b2s[:, 512:1024], op=ALU.add))
                            vector.wait_ge(sV.h, sV.n)
                            sV.inc(vector.tensor_reduce(out=stat[:, 0:1], in_=wf0[tt][:, :],
                                                        axis=AX, op=ALU.add))
                            vector.wait_ge(sS.h, bs_n(g, 39 + 6 * tt))
                            sV.inc(vector.reciprocal(out=stat[:, 4:5], in_=stat[:, 3:4]))
                            vector.wait_ge(sV.h, sV.n)
                            sV.inc(vector.tensor_scalar(out=wf0[tt][:, :], in0=wf0[tt][:, :],
                                                        scalar1=stat[:, 4:5], scalar2=None,
                                                        op0=ALU.mult))
                block.vector(pb_vector)

                def pb_scalar_main(scalar):
                    scalar.wait_ge(sKB.h, 48)
                    for g in range(8):
                        scalar.wait_ge(sT.h, bt_n(g, 1))
                        sS.inc(scalar.activation(out=u1b[:, :], in_=psm[0:4, 0:256],
                                                 func=ACT.Identity, scale=scls[0:4, 0:1]))
                        for hs in range(32):
                            scalar.wait_ge(sT.h, bt_n(g, 1 + hs + 1))
                            sS.inc(scalar.activation(out=hT[:, 256 * hs:256 * (hs + 1)],
                                                     in_=pb[hs % 2][:, 0:256], func=ACT.Gelu,
                                                     bias=b1cs[:, hs:hs + 1], scale=1.0))
                        for tt in range(2):
                            scalar.wait_ge(sT.h, bt_n(g, 34 + tt))
                            sS.inc(scalar.activation(out=vtb[:, 512 * tt:512 * (tt + 1)],
                                                     in_=psm[0:4, 0:512],
                                                     func=ACT.Identity, scale=scls[0:4, 0:1]))
                        for tt in range(2):
                            scalar.wait_ge(sV.h, bv_n(g, 5 * tt + 3))
                            sS.inc(scalar.mul(out=stat[:, 1:2], in_=stat[:, 0:1], mul=-1.0 / D))
                            scalar.wait_ge(sS.h, sS.n)
                            sS.inc(scalar.activation(out=wf0[tt][:, :], in_=wf0[tt][:, :],
                                                     func=ACT.Identity, bias=stat[:, 1:2],
                                                     scale=1.0))
                            scalar.wait_ge(sS.h, sS.n)
                            sS.inc(scalar.activation(out=dmy[:, :], in_=wf0[tt][:, :],
                                                     func=ACT.Square, accum_out=stat[:, 2:3]))
                            scalar.wait_ge(sS.h, sS.n)
                            sS.inc(scalar.activation(out=stat[:, 3:4], in_=stat[:, 2:3],
                                                     func=ACT.Sqrt, scale=1.0 / D,
                                                     bias=stat[:, 5:6]))
                            scalar.wait_ge(sV.h, bv_n(g, 5 * tt + 5))
                            if g > 0 or tt == 1:
                                if 2 * g + tt <= 8:
                                    scalar.wait_ge(sXSa[1 - tt].h, 16 * (g + tt))
                                else:
                                    scalar.wait_ge(sXS[1 - tt].h, 16 * (g + tt - 4))
                            sS.inc(scalar.copy(out=xbft[:, :], in_=wf0[tt][:, :]))
                            scalar.wait_ge(sT.h, bt_n(g, 69 + tt))
                            sS.inc(scalar.copy(out=xbft[:, :], in_=ptb[:, :]))
                block.scalar(pb_scalar_main)
                PB_V, PB_S, PB_T = sV.n, sS.n, sT.n



        if PHASES >= 3:
          pdC = contextlib.ExitStack()
          with pdC:
            pcC = contextlib.ExitStack()
            EC = pcC.enter_context
            qTs = EC(nc.sbuf_tensor("qTs", [128, 4 * S], BF16))
            kTs = EC(nc.sbuf_tensor("kTs", [128, 4 * S], BF16))
            vAll = EC(nc.sbuf_tensor("vAll", [128, NTF * 520], BF16))
            mparts = EC(nc.sbuf_tensor("mparts", [128, 16], F32))
            mcol = EC(nc.sbuf_tensor("mcol", [128, 2], F32))
            cvb = EC(nc.sbuf_tensor("cvb", [128, 2048], BF16))
            sKC = Sem(E(nc.semaphore("sKC")))
            sQG = [Sem(E(nc.semaphore(f"sQG{i}"))) for i in range(2)]
            sHQ = Sem(E(nc.semaphore("sHQ")))
            sYG = [Sem(E(nc.semaphore(f"sYG{i}"))) for i in range(2)]

            def pc_gpsimd_cc(gpsimd):
                gpsimd.wait_ge(sXS[0].h, sXS[0].n)
                gpsimd.wait_ge(sXS[1].h, sXS[1].n)
                sC.inc(gpsimd.collective_compute(
                    "AllGather", ALU.bypass, replica_groups=RG,
                    ins=[cc_x2_in_b.ap()], outs=[cc_x2_out_b.ap()]))
            block.gpsimd(pc_gpsimd_cc)
            CC1 = sC.n

            KC1 = sKC.n

            qkvC = contextlib.ExitStack()
            with qkvC:
                EQ = qkvC.enter_context
                wqkb = EQ(nc.sbuf_tensor("wqkb", [128, NDP * 1024], BF16))
                x2cb = EQ(nc.sbuf_tensor("x2cb", [128, NDP * 512], BF16))
                x2c = [x2cb, x2cb]
                costt = EQ(nc.sbuf_tensor("costt", [128, NTF * 32], BF16))
                sintt = EQ(nc.sbuf_tensor("sintt", [128, NTF * 32], BF16))
                rsc = [EQ(nc.sbuf_tensor(f"rsc{i}", [128, 256], F32)) for i in range(2)]
                qrop = EQ(nc.sbuf_tensor("qrop", [128, 512], BF16))

                def pc_sync_qkw(sync):
                    sync.wait_ge(sKC.h, KC1)
                    sKC.inc(sync.dma_start(out=costt[:], in_=cos_t), 16)
                    sKC.inc(sync.dma_start(out=sintt[:], in_=sin_t), 16)
                    SQ0 = sS.n
                    for p in range(NDP):
                        sync.wait_ge(sS.h, SQ0 + p)
                        sX.inc(sync.dma_start(out=stg[:, 0:1024],
                                              in_=wqkv[128 * p:128 * (p + 1), 0:1024]), 16)
                block.sync(pc_sync_qkw)
                XQ1 = sX.n
                SQC0 = sS.n

                def pc_scalar_qkw(scalar):
                    scalar.wait_ge(sS.h, SQC0)
                    for p in range(NDP):
                        scalar.wait_ge(sX.h, XQ1 - 16 * (NDP - 1 - p))
                        scalar.mul(out=wqkb[:, 1024 * p:1024 * p + 512],
                                   in_=stg[:, 0:512], mul=0.125)
                        sS.inc(scalar.copy(out=wqkb[:, 1024 * p + 512:1024 * (p + 1)],
                                           in_=stg[:, 512:1024]))
                block.scalar(pc_scalar_qkw)
                SQC1 = sS.n

                TQ0, VQ0, SQ1 = sT.n, sV.n, sS.n
                TPQ, VPQ, SPQ = 4 * 4, 4 * 14, 4 * 3

                def tq_n(tc, k):
                    return TQ0 + TPQ * tc + k

                def vq_n(tc, k):
                    return VQ0 + VPQ * tc + k

                def sq_n(tc, k):
                    return SQ1 + SPQ * tc + k

                def x2src(tc, p):
                    slab = tc // 4
                    t = (cc_x2_out_a if tc % 4 < 2 else cc_x2_out_b).ap()
                    return t[1024 * slab + 128 * p:1024 * slab + 128 * (p + 1),
                             512 * (tc % 2):512 * (tc % 2 + 1)]

                TCA = [0, 1, 4, 5, 2, 3, 6, 7]

                def pc_gpsimd_x2(gpsimd):
                    for ci, tc in enumerate(TCA):
                        gpsimd.wait_ge(sC.h, CC_A if tc % 4 < 2 else CC1)
                        if ci >= 1:
                            gpsimd.wait_ge(sT.h, tq_n(ci - 1, 14))
                        for p in range(NDP):
                            sQG[ci % 2].inc(gpsimd.dma_start(
                                out=x2c[ci % 2][:, 512 * p:512 * (p + 1)],
                                in_=x2src(tc, p)), 16)
                block.gpsimd(pc_gpsimd_x2)

                def pc_tensor_qk(tensor):
                    tensor.wait_ge(sS.h, SQC1)
                    tensor.wait_ge(sV.h, PB_V)
                    for ci, tc in enumerate(TCA):
                        tensor.wait_ge(sQG[ci % 2].h, 128 * (ci // 2 + 1))
                        for ts_ in range(4):
                            kt = 4 * tc + ts_
                            if ci > 0 or ts_ > 0:
                                tensor.wait_ge(sV.h, vq_n(ci, 14 * ts_) if ts_ > 0
                                               else vq_n(ci - 1, 14 * 4))
                            for p in range(NDP):
                                tensor.matmul(out=pb[0][:, :],
                                              lhsT=x2c[ci % 2][:, 512 * p + 128 * ts_:
                                                               512 * p + 128 * (ts_ + 1)],
                                              rhs=wqkb[:, 1024 * p:1024 * p + 512],
                                              start=(p == 0), stop=(p == NDP - 1))
                            for p in range(NDP):
                                ins = tensor.matmul(out=pb[1][:, :],
                                                    lhsT=x2c[ci % 2][:, 512 * p + 128 * ts_:
                                                                     512 * p + 128 * (ts_ + 1)],
                                                    rhs=wqkb[:, 1024 * p + 512:1024 * (p + 1)],
                                                    start=(p == 0), stop=(p == NDP - 1))
                            sT.inc(ins, 2)
                            tensor.wait_ge(sV.h, vq_n(ci, 14 * ts_ + 7))
                            if ci > 0 or ts_ > 0:
                                tensor.wait_ge(sS.h, sq_n(ci, 3 * ts_) if ts_ > 0
                                               else sq_n(ci - 1, 12))
                            for j in range(4):
                                ins = tensor.transpose(out=ptb[:, 128 * j:128 * (j + 1)],
                                                       in_=qrop[:, 128 * j:128 * (j + 1)],
                                                       identity=idn[:])
                            sT.inc(ins)
                            tensor.wait_ge(sV.h, vq_n(ci, 14 * ts_ + 14))
                            tensor.wait_ge(sS.h, sq_n(ci, 3 * ts_ + 1))
                            for j in range(4):
                                ins = tensor.transpose(out=ptb[:, 512 + 128 * j:512 + 128 * (j + 1)],
                                                       in_=qrop[:, 128 * j:128 * (j + 1)],
                                                       identity=idn[:])
                            sT.inc(ins)
                block.tensor(pc_tensor_qk)

                def rope_ops(vector, src_bank, kt, out_buf, n0):
                    pv = src_bank.rearrange("t (h c) -> t h c", c=64)
                    ov = out_buf.rearrange("t (h c) -> t h c", c=64)
                    cosb = costt[:, 32 * kt:32 * (kt + 1)].rearrange(
                        "t (h c) -> t h c", h=1).broadcast_to([128, 8, 32])
                    sinb = sintt[:, 32 * kt:32 * (kt + 1)].rearrange(
                        "t (h c) -> t h c", h=1).broadcast_to([128, 8, 32])
                    r0 = rsc[0].rearrange("t (h c) -> t h c", c=32)
                    r1 = rsc[1].rearrange("t (h c) -> t h c", c=32)
                    sV.inc(vector.tensor_tensor(out=r0[:, :, :], in0=pv[:, :, 0:32],
                                                in1=cosb, op=ALU.mult))
                    sV.inc(vector.tensor_tensor(out=r1[:, :, :], in0=pv[:, :, 32:64],
                                                in1=sinb, op=ALU.mult))
                    vector.wait_ge(sV.h, sV.n)
                    sV.inc(vector.tensor_tensor(out=ov[:, :, 0:32], in0=r0[:, :, :],
                                                in1=r1[:, :, :], op=ALU.subtract))
                    vector.wait_ge(sV.h, sV.n)
                    sV.inc(vector.tensor_tensor(out=r0[:, :, :], in0=pv[:, :, 0:32],
                                                in1=sinb, op=ALU.mult))
                    sV.inc(vector.tensor_tensor(out=r1[:, :, :], in0=pv[:, :, 32:64],
                                                in1=cosb, op=ALU.mult))
                    vector.wait_ge(sV.h, sV.n)
                    sV.inc(vector.tensor_tensor(out=ov[:, :, 32:64], in0=r0[:, :, :],
                                                in1=r1[:, :, :], op=ALU.add))
                    vector.wait_ge(sV.h, sV.n)
                    sV.inc(vector.engine_nop())
                    return

                def pc_vector_rope(vector):
                    vector.wait_ge(sKC.h, KC1 + 32)
                    for ci, tc in enumerate(TCA):
                        for ts_ in range(4):
                            kt = 4 * tc + ts_
                            vector.wait_ge(sT.h, tq_n(ci, 4 * ts_ + 2))
                            rope_ops(vector, pb[0], kt, qrop, 0)
                            vector.wait_ge(sT.h, tq_n(ci, 4 * ts_ + 3))
                            rope_ops(vector, pb[1], kt, qrop, 7)
                block.vector(pc_vector_rope)

                def pc_scalar_qev(scalar):
                    for ci, tc in enumerate(TCA):
                        for ts_ in range(4):
                            kt = 4 * tc + ts_
                            scalar.wait_ge(sT.h, tq_n(ci, 4 * ts_ + 3))
                            ovq = qTs.rearrange("r (j c) -> r j c", c=4096)
                            sS.inc(scalar.copy(
                                out=ovq[:, :, 128 * kt:128 * (kt + 1)],
                                in_=ptb.rearrange("r (j c) -> r j c", c=128)[:, 0:4, :]))
                            scalar.wait_ge(sT.h, tq_n(ci, 4 * ts_ + 4))
                            ovk = kTs.rearrange("r (j c) -> r j c", c=4096)
                            sS.inc(scalar.copy(
                                out=ovk[:, :, 128 * kt:128 * (kt + 1)],
                                in_=ptb.rearrange("r (j c) -> r j c", c=128)[:, 4:8, :]))
                            sS.inc(scalar.copy(out=dmy[:, 0:8], in_=dmy[:, 0:8]))
                block.scalar(pc_scalar_qev)
                TQ_END, VQ_END, SQ_END = sT.n, sV.n, sS.n

                def pc_sync_vw(sync):
                    sync.wait_ge(sS.h, SQ_END)
                    SQ2 = sS.n
                    for p in range(NDP):
                        sync.wait_ge(sS.h, SQ_END + p)
                        sX.inc(sync.dma_start(out=stg[:, 0:512],
                                              in_=wqkv[128 * p:128 * (p + 1), 1024:1536]), 16)
                block.sync(pc_sync_vw)
                XV1 = sX.n

                def pc_scalar_vw(scalar):
                    for p in range(NDP):
                        scalar.wait_ge(sX.h, XV1 - 16 * (NDP - 1 - p))
                        sS.inc(scalar.copy(out=wqkb[:, 512 * p:512 * (p + 1)],
                                           in_=stg[:, 0:512]))
                block.scalar(pc_scalar_vw)
                SV1 = sS.n

                def pc_vector_vones(vector):
                    vv = vAll.rearrange("t (k h c) -> t k h c", h=8, c=65)
                    sV.inc(vector.memset(vv[:, :, :, 64:65], 1.0))
                block.vector(pc_vector_vones)
                VV1 = sV.n

                TV0, SV2 = sT.n, sS.n

                def pc_gpsimd_x2v(gpsimd):
                    gpsimd.wait_ge(sT.h, TQ_END)
                    gpsimd.wait_ge(sC.h, CC1)
                    for tc in range(8):
                        if tc >= 1:
                            gpsimd.wait_ge(sT.h, TV0 + 4 * tc)
                        for p in range(NDP):
                            sQG[tc % 2].inc(gpsimd.dma_start(
                                out=x2c[tc % 2][:, 512 * p:512 * (p + 1)],
                                in_=x2src(tc, p)), 16)
                block.gpsimd(pc_gpsimd_x2v)

                def pc_tensor_v(tensor):
                    tensor.wait_ge(sS.h, SV1)
                    tensor.wait_ge(sV.h, PB_V)
                    for tc in range(8):
                        tensor.wait_ge(sQG[tc % 2].h, 128 * 4 + 128 * (tc // 2 + 1))
                        for ts_ in range(4):
                            if tc > 0 or ts_ > 0:
                                tensor.wait_ge(sS.h, SV2 + 4 * tc + ts_)
                            for p in range(NDP):
                                ins = tensor.matmul(out=pb[2][:, :],
                                                    lhsT=x2c[tc % 2][:, 512 * p + 128 * ts_:
                                                                     512 * p + 128 * (ts_ + 1)],
                                                    rhs=wqkb[:, 512 * p:512 * (p + 1)],
                                                    start=(p == 0), stop=(p == NDP - 1))
                            sT.inc(ins)
                block.tensor(pc_tensor_v)

                def pc_scalar_vev(scalar):
                    for tc in range(8):
                        for ts_ in range(4):
                            kt = 4 * tc + ts_
                            scalar.wait_ge(sT.h, TV0 + 4 * tc + ts_ + 1)
                            vv = vAll.rearrange("t (k h c) -> t k h c", h=8, c=65)
                            sS.inc(scalar.copy(
                                out=vv[:, kt, :, 0:64],
                                in_=pb[2].rearrange("t (h c) -> t h c", c=64)[:, :, :]))
                block.scalar(pc_scalar_vev)
                SV3 = sS.n
            hdC = pcC.enter_context(contextlib.ExitStack())
            _EH = hdC.enter_context
            qhat = _EH(nc.sbuf_tensor("qhat", [65, S], BF16))
            khat = _EH(nc.sbuf_tensor("khat", [65, S], BF16))
            pS = [_EH(nc.sbuf_tensor(f"pS{i}", [128, 512], BF16)) for i in range(2)]
            accev = _EH(nc.sbuf_tensor("accev", [65, 512], F32))
            rlc = _EH(nc.sbuf_tensor("rlc", [128, 1], F32))
            ysel = _EH(nc.sbuf_tensor("ysel", [128, 256], BF16))
            msk1s = _EH(nc.sbuf_tensor("msk1s", [128, 128], F32))
            msk2s = _EH(nc.sbuf_tensor("msk2s", [128, 512], F32))
            zb512 = _EH(nc.sbuf_tensor("zb512", [128, 512], BF16))

            def pc_sync_k(sync):
                sync.wait_ge(sS.h, SV3)
                sKC.inc(sync.dma_start(out=msk1s[:], in_=mask1), 16)
                sKC.inc(sync.dma_start(out=msk2s[:], in_=mask2), 16)
            block.sync(pc_sync_k)
            KC2 = sKC.n

            NCV = 168

            def _cv_src(m):
                e, which = m // 3, m % 3
                if e == 6:
                    t = (fc1T, fc2T, fc3T)[which]
                    return t.rearrange("(a p) d -> p a d", p=128)
                t = (ew1T, ew3T, ew2T)[which]
                return t[e].rearrange("(a p) d -> p a d", p=128)

            ewbv = ewbf.rearrange("(m a p) d -> m p a d", p=128, a=8)
            cvfv = stg.rearrange("p (a d) -> p a d", d=1024)
            cvbv = cvb.rearrange("p (a d) -> p a d", d=1024)

            def pc_sync_cv(sync):
                sync.wait_ge(sWC.h, NWC)
                def _store(st):
                    m, k = st // 8, st % 8
                    sCVs[st % 2].inc(sync.dma_start(
                        out=ewbv[m][:, k:k + 1, :],
                        in_=cvbv[:, st % 2:st % 2 + 1, :]), 16)
                for s in range(NCV):
                    if s >= 2:
                        sync.wait_ge(sCVv.h, s - 1)
                        _store(s - 2)
                    m, k = s // 8, s % 8
                    sCVl[s % 2].inc(sync.dma_start(
                        out=cvfv[:, s % 2:s % 2 + 1, :],
                        in_=_cv_src(m)[:, k:k + 1, :]), 16)
                for s in (NCV - 2, NCV - 1):
                    sync.wait_ge(sCVv.h, s + 1)
                    _store(s)
            block.sync(pc_sync_cv)
            CV_F = [sCVs[0].n, sCVs[1].n]

            _cv_next = [0]

            def _cv_cast(vector, budget):
                for _ in range(budget):
                    c = _cv_next[0]
                    if c >= NCV:
                        return
                    _cv_next[0] = c + 1
                    vector.wait_ge(sCVl[c % 2].h, 16 * (c // 2 + 1))
                    if c >= 2:
                        vector.wait_ge(sCVs[c % 2].h, 16 * (c // 2))
                    sCVv.inc(vector.tensor_scalar(
                        out=cvbv[:, c % 2:c % 2 + 1, :],
                        in0=cvfv[:, c % 2:c % 2 + 1, :],
                        scalar1=0.0, scalar2=None, op0=ALU.add))

            T1_per = [2] * NTF
            V1_per = [2] * NTF
            T1_head = sum(T1_per)
            V1_head = sum(V1_per)
            S1_head = 8
            P2P = []
            for G in range(8):
                ps_ = []
                for j in range(4):
                    if G == 0:
                        ps_.append(("d", j, 4 * G + j))
                        if j < 3:
                            ps_.append(("s", j, 4 * G + j))
                    else:
                        ps_.append(("m", j, 4 * G + j))
                for kt in range(4 * G):
                    ps_.append(("f", -1, kt))
                P2P.append(ps_)
            T2G = [1 + 2 * len(P2P[G]) + 4 for G in range(8)]
            S2G = [len(P2P[G]) + 1 + 5 for G in range(8)]
            V2G = [4 + 4 for G in range(8)]

            def T1QK(qt):
                return qt + 1 if qt < 2 else 2 * qt

            def T1TR(qt):
                return 3 + 2 * qt if qt < NTF - 1 else 2 * NTF

            def T2ST(pp):
                return pp + 2 if pp < 2 else 2 * pp + 1

            def T2AV(pp, np2):
                return 4 + 2 * pp if pp < np2 - 1 else 2 * np2 + 1
            T2_head = sum(T2G)
            S2_head = sum(S2G)
            V2_head = sum(V2G)
            TPH = T1_head + T2_head
            SPH = S1_head + S2_head + sum(S2G) * 0 + 0
            VPH = V1_head + V2_head
            SPH = S1_head + S2_head

            TH0, VH0, SH0 = sT.n, sV.n, sS.n

            def th_n(hh, k):
                return TH0 + TPH * hh + k

            def vh_n(hh, k):
                return VH0 + VPH * hh + k

            def sh_n(hh, k):
                return SH0 + SPH * hh + k

            import itertools
            T1_cum = [0] + list(itertools.accumulate(T1_per))
            V1_cum = [0] + list(itertools.accumulate(V1_per))
            T2_cum = [T1_head] + [T1_head + c for c in itertools.accumulate(T2G)]
            S2_cum = [S1_head] + [S1_head + c for c in itertools.accumulate(S2G)]
            V2_cum = [V1_head] + [V1_head + c for c in itertools.accumulate(V2G)]


            def pc_vector_kones(vector):
                vector.wait_ge(sS.h, SV3)
                sV.inc(vector.memset(khat[64:65, :], 1.0))
                sV.inc(vector.memset(zb512[:, :], 0.0))
            block.vector(pc_vector_kones)
            VK1 = sV.n
            VH0b = sV.n
            TH0b, SH0b = sT.n, sS.n
            TH0, VH0, SH0 = sT.n, sV.n, sS.n

            def pc_tensor_heads(tensor):
                tensor.wait_ge(sV.h, VK1)
                tensor.wait_ge(sS.h, SV3)

                def _qk(hh, qt):
                    if qt == 0 and hh >= 1:
                        tensor.wait_ge(sV.h, vh_n(hh - 1, VPH))
                    ins = tensor.matmul(out=pb[qt % 2][:, 0:128],
                                        lhsT=qhat[0:64, 128 * qt:128 * (qt + 1)],
                                        rhs=khat[0:64, 128 * qt:128 * (qt + 1)],
                                        start=True, stop=True,
                                        skip_group_check=True)
                    sT.inc(ins)

                def _tr1(hh, qt):
                    tensor.wait_ge(sV.h, vh_n(hh, 2 * qt + 2))
                    if qt >= 4:
                        tensor.wait_ge(sS.h, sh_n(hh, qt // 4))
                    ins = tensor.transpose(out=psm[0:1, 128 * (qt % 4):128 * (qt % 4) + 128],
                                           in_=mcol[:, qt % 2:qt % 2 + 1], identity=idnf[:])
                    sT.inc(ins)

                for hh in range(8):
                    tensor.wait_ge(sHQ.h, 32 * (hh + 1))
                    _qk(hh, 0)
                    _qk(hh, 1)
                    for qt in range(NTF):
                        _tr1(hh, qt)
                        if qt + 2 < NTF:
                            _qk(hh, qt + 2)
                    for G in range(8):
                        abank = pb[4]
                        np2 = len(P2P[G])

                        def _st(pp):
                            kind, i, kt = P2P[G][pp]
                            sbank = pb[2 + pp % 2]
                            if kind == "f":
                                ins = tensor.matmul(out=sbank[:, 0:512],
                                                    lhsT=khat[:, 128 * kt:128 * (kt + 1)],
                                                    rhs=qhat[:, 512 * G:512 * (G + 1)],
                                                    start=True, stop=True,
                                                    skip_group_check=True)
                            elif kind == "d":
                                ins = tensor.matmul(out=sbank[:, 0:128],
                                                    lhsT=khat[:, 128 * kt:128 * (kt + 1)],
                                                    rhs=qhat[:, 512 * G + 128 * i:512 * G + 128 * (i + 1)],
                                                    start=True, stop=True,
                                                    skip_group_check=True)
                            elif kind == "m":
                                w = 512 - 128 * i
                                ins = tensor.matmul(out=sbank[:, 0:w],
                                                    lhsT=khat[:, 128 * kt:128 * (kt + 1)],
                                                    rhs=qhat[:, 512 * G + 128 * i:512 * (G + 1)],
                                                    start=True, stop=True,
                                                    skip_group_check=True)
                            else:
                                w = 128 * (3 - i)
                                ins = tensor.matmul(out=sbank[:, 0:w],
                                                    lhsT=khat[:, 128 * kt:128 * (kt + 1)],
                                                    rhs=qhat[:, 512 * G + 128 * (i + 1):512 * (G + 1)],
                                                    start=True, stop=True,
                                                    skip_group_check=True)
                            sT.inc(ins)

                        def _av(pp):
                            kind, i, kt = P2P[G][pp]
                            tensor.wait_ge(sS.h, sh_n(hh, S2_cum[G] + pp + 1))
                            srt = False
                            stp = (pp == np2 - 1) or (G == 0 and kind == "d")
                            vsl = vAll[:, 520 * kt + 65 * hh:520 * kt + 65 * (hh + 1)]
                            if kind == "f":
                                ins = tensor.matmul(out=abank[0:65, 0:512],
                                                    lhsT=vsl, rhs=pS[pp % 2][:, 0:512],
                                                    start=False, stop=stp,
                                                    skip_group_check=True)
                            elif kind == "d":
                                ins = tensor.matmul(out=abank[0:65, 128 * i:128 * (i + 1)],
                                                    lhsT=vsl, rhs=pS[pp % 2][:, 0:128],
                                                    start=srt, stop=stp,
                                                    skip_group_check=True)
                            elif kind == "m":
                                w = 512 - 128 * i
                                ins = tensor.matmul(out=abank[0:65, 128 * i:512],
                                                    lhsT=vsl, rhs=pS[pp % 2][:, 0:w],
                                                    start=srt, stop=stp,
                                                    skip_group_check=True)
                            else:
                                w = 128 * (3 - i)
                                ins = tensor.matmul(out=abank[0:65, 128 * (i + 1):512],
                                                    lhsT=vsl, rhs=pS[pp % 2][:, 0:w],
                                                    start=srt, stop=stp,
                                                    skip_group_check=True)
                            sT.inc(ins)

                        tensor.wait_ge(sS.h, sh_n(hh, (G + 1)))
                        tensor.wait_ge(sS.h, sh_n(hh, S2_cum[G]))
                        ins = tensor.matmul(out=abank[0:65, 0:512],
                                            lhsT=vAll[:, 0:65],
                                            rhs=zb512[:, :], start=True, stop=False,
                                            skip_group_check=True)
                        sT.inc(ins)
                        _st(0)
                        _st(1)
                        for pp in range(np2):
                            _av(pp)
                            if pp + 2 < np2:
                                _st(pp + 2)
                        tensor.wait_ge(sS.h, sh_n(hh, S2_cum[G] + np2 + 1))
                        for i in range(4):
                            ins = tensor.transpose(out=psm[0:128, 128 * i:128 * i + 65],
                                                   in_=accev[:, 128 * i:128 * (i + 1)],
                                                   identity=idnf[0:65, 0:65])
                        sT.inc(ins, 4)
            block.tensor(pc_tensor_heads)

            def pc_vector_heads2(vector):
                vector.wait_ge(sKC.h, KC2)
                for hh in range(8):
                    for qt in range(NTF):
                        bank = pb[qt % 2]
                        vector.wait_ge(sT.h, th_n(hh, T1QK(qt)))
                        sV.inc(vector.tensor_tensor(out=bank[:, 0:128], in0=bank[:, 0:128],
                                                    in1=msk1s[:, :], op=ALU.add))
                        vector.wait_ge(sV.h, sV.n)
                        sV.inc(vector.tensor_reduce(out=mcol[:, qt % 2:qt % 2 + 1],
                                                    in_=bank[:, 0:128],
                                                    axis=AX, op=ALU.max))
                    for G in range(8):
                        np2 = len(P2P[G])
                        for pp, (kind, i, kt) in enumerate(P2P[G]):
                            if kind not in ("d", "m"):
                                continue
                            w = 128 if kind == "d" else 512 - 128 * i
                            vector.wait_ge(sT.h, th_n(hh, T2_cum[G] + T2ST(pp)))
                            sV.inc(vector.tensor_tensor(out=pb[2 + pp % 2][:, 0:w],
                                                        in0=pb[2 + pp % 2][:, 0:w],
                                                        in1=msk2s[:, 0:w], op=ALU.add))
                        _cv_cast(vector, 3)
                        vector.wait_ge(sT.h, th_n(hh, T2_cum[G + 1]))
                        vector.wait_ge(sS.h, sh_n(hh, S2_cum[G + 1]))
                        ysem = sYa if hh < 4 else sY
                        if G >= 1:
                            vector.wait_ge(ysem.h, 16 * (8 * (hh % 4) + G))
                        elif hh == 4:
                            vector.wait_ge(sYa.h, 16 * 32)
                        elif hh >= 1:
                            vector.wait_ge(ysem.h, 16 * 8 * (hh % 4))
                        for i in range(4):
                            sV.inc(vector.tensor_scalar(
                                out=ysel[:, 64 * i:64 * (i + 1)],
                                in0=psm[0:128, 128 * i:128 * i + 64],
                                scalar1=stat[:, 10 + i:11 + i],
                                scalar2=None, op0=ALU.mult))
            block.vector(pc_vector_heads2)
            ATTV = sV.n

            def pc_scalar_heads(scalar):
                for hh in range(8):
                    for G in range(8):
                        scalar.wait_ge(sT.h, th_n(hh, T1TR(4 * G + 3)))
                        sS.inc(scalar.activation(out=qhat[64:65, 512 * G:512 * (G + 1)],
                                                 in_=psm[0:1, 0:512], func=ACT.Identity,
                                                 scale=-1.0))
                    for G in range(8):
                        np2 = len(P2P[G])
                        nmask = 0
                        for pp, (kind, i, kt) in enumerate(P2P[G]):
                            sbank = pb[2 + pp % 2]
                            scalar.wait_ge(sT.h, th_n(hh, T2_cum[G] + T2ST(pp)))
                            if kind in ("d", "m"):
                                w = 128 if kind == "d" else 512 - 128 * i
                                scalar.wait_ge(sV.h, vh_n(hh, V2_cum[G] + nmask + 1))
                                nmask += 1
                                sS.inc(scalar.activation(out=pS[pp % 2][:, 0:w],
                                                         in_=sbank[:, 0:w], func=ACT.Exp))
                            elif kind == "s":
                                w = 128 * (3 - i)
                                sS.inc(scalar.activation(out=pS[pp % 2][:, 0:w],
                                                         in_=sbank[:, 0:w], func=ACT.Exp))
                            else:
                                sS.inc(scalar.activation(out=pS[pp % 2][:, 0:512],
                                                         in_=sbank[:, 0:512], func=ACT.Exp))
                        scalar.wait_ge(sT.h, th_n(hh, T2_cum[G] + 1 + 2 * np2))
                        sS.inc(scalar.copy(out=accev[:, :], in_=pb[4][0:65, 0:512]))
                        scalar.wait_ge(sT.h, th_n(hh, T2_cum[G + 1]))
                        for i in range(4):
                            sS.inc(scalar.activation(out=stat[:, 6 + i:7 + i],
                                                     in_=psm[0:128, 128 * i + 64:128 * i + 65],
                                                     func=ACT.Ln))
                        scalar.wait_ge(sS.h, sS.n)
                        sS.inc(scalar.activation(out=stat[:, 10:14],
                                                 in_=stat[:, 6:10],
                                                 func=ACT.Exp, scale=-1.0))
            block.scalar(pc_scalar_heads)

            CC_YA = sC.n + 1

            def pc_gpsimd_heads(gpsimd):
                gpsimd.wait_ge(sS.h, SV3)
                for hh in range(8):
                    pt_ = hh // 2
                    base = 64 * (hh % 2)
                    if hh >= 1:
                        gpsimd.wait_ge(sT.h, th_n(hh - 1, TPH))
                    if hh == 4:
                        gpsimd.wait_ge(sYa.h, sYa.n)
                        sC.inc(gpsimd.collective_compute(
                            "AllGather", ALU.bypass, replica_groups=RG,
                            ins=[cc_y_in_a.ap()], outs=[cc_y_out_a.ap()]))
                    sHQ.inc(gpsimd.dma_start(
                        out=qhat[0:64, :],
                        in_=qTs[base:base + 64, 4096 * pt_:4096 * (pt_ + 1)]), 16)
                    sHQ.inc(gpsimd.dma_start(
                        out=khat[0:64, :],
                        in_=kTs[base:base + 64, 4096 * pt_:4096 * (pt_ + 1)]), 16)
                    for G in range(8):
                        gpsimd.wait_ge(sV.h, vh_n(hh, V2_cum[G + 1]))
                        dst = cc_y_in_a if hh < 4 else cc_y_in_b
                        ov = dst.ap().rearrange("(g i t) c -> g t i c", i=4, t=128)
                        iv = ysel.rearrange("t (i c) -> t i c", c=64)
                        sem = sYa if hh < 4 else sY
                        sem.inc(gpsimd.dma_start(
                            out=ov[G][:, :, 64 * (hh % 4):64 * (hh % 4) + 64],
                            in_=iv[:, :, :]), 16)
            block.gpsimd(pc_gpsimd_heads)
            Y_T = sY.n
            assert CC_YA == sC.n

            def pc_gpsimd_cc2(gpsimd):
                gpsimd.wait_ge(sY.h, Y_T)
                sC.inc(gpsimd.collective_compute(
                    "AllGather", ALU.bypass, replica_groups=RG,
                    ins=[cc_y_in_b.ap()], outs=[cc_y_out_b.ap()]))
            block.gpsimd(pc_gpsimd_cc2)
            CC2 = sC.n

            pcC.close()
            accF = pdC.enter_context(nc.sbuf_tensor("accF", [128, NT * D], F32))

            opC = contextlib.ExitStack()
            with opC:
                EO = opC.enter_context
                wob = EO(nc.sbuf_tensor("wob", [128, NDP * 1024], BF16))
                ygt = [EO(nc.sbuf_tensor(f"ygt{i}", [128, 2048], BF16)) for i in range(2)]
                yfull = EO(nc.sbuf_tensor("yfull", [128, D], BF16))
                yTt = EO(nc.sbuf_tensor("yTt", [128, D], BF16))

                XO0 = sX.n
                SO0 = sS.n

                def pc_sync_wo(sync):
                    for p in range(NDP):
                        sync.wait_ge(sS.h, SO0 + p)
                        sX.inc(sync.dma_start(out=stg[:, 0:1024],
                                              in_=wo[128 * p:128 * (p + 1), :]), 16)
                block.sync(pc_sync_wo)
                XO1 = sX.n

                def pc_scalar_wo(scalar):
                    scalar.wait_ge(sS.h, SO0)
                    for p in range(NDP):
                        scalar.wait_ge(sX.h, XO1 - 16 * (NDP - 1 - p))
                        sS.inc(scalar.copy(out=wob[:, 1024 * p:1024 * (p + 1)],
                                           in_=stg[:, 0:1024]))
                block.scalar(pc_scalar_wo)
                SO1 = sS.n

                TO0, VO0, SO2 = sT.n, sV.n, sS.n
                TPO, VPO, SPO = 4, 9, 7
                R_B = [sR[0].n, sR[1].n]

                def to_n(i, k):
                    return TO0 + TPO * i + k

                def vo_n(i, k):
                    return VO0 + VPO * i + k

                def so_n(i, k):
                    return SO2 + SPO * i + k

                def pc_gpsimd_yg(gpsimd):
                    gpsimd.wait_ge(sC.h, CC2)
                    for i in range(NT):
                        if i >= 2:
                            gpsimd.wait_ge(sV.h, vo_n(i - 2, 4))
                        for sl in range(2):
                            for ab in range(2):
                                rows = slice(4096 * sl + 2048 * ab + 128 * i,
                                             4096 * sl + 2048 * ab + 128 * (i + 1))
                                base = 1024 * sl + 512 * ab
                                sYG[i % 2].inc(gpsimd.dma_start(
                                    out=ygt[i % 2][:, base:base + 256],
                                    in_=cc_y_out_a.ap()[rows, :]), 16)
                                sYG[i % 2].inc(gpsimd.dma_start(
                                    out=ygt[i % 2][:, base + 256:base + 512],
                                    in_=cc_y_out_b.ap()[rows, :]), 16)
                block.gpsimd(pc_gpsimd_yg)

                def pc_vector_op(vector):
                    vector.wait_ge(sS.h, SO1)
                    vector.wait_ge(sP[0].h, PB_P0)
                    vector.wait_ge(sP[1].h, PB_P1)
                    for i in range(NT):
                        vector.wait_ge(sYG[i % 2].h, 128 * (i // 2 + 1))
                        if i >= 1:
                            vector.wait_ge(sT.h, to_n(i - 1, 1))
                        for sl in range(2):
                            a = ygt[i % 2][:, 1024 * sl:1024 * sl + 512]
                            b = ygt[i % 2][:, 1024 * sl + 512:1024 * sl + 1024]
                            sV.inc(vector.tensor_sub(out=a, in0=a, in1=b))
                            vector.wait_ge(sV.h, sV.n)
                            sV.inc(vector.scalar_tensor_tensor(
                                out=yfull[:, 512 * sl:512 * (sl + 1)], in0=a,
                                scalar=omsk[:, 0:1], in1=b, op0=ALU.mult, op1=ALU.add))
                        vector.wait_ge(sT.h, to_n(i, 2))
                        vector.wait_ge(sR[i % 2].h, R_B[i % 2] + 16 * (i // 2 + 1))
                        if i >= 2:
                            vector.wait_ge(sS.h, so_n(i - 2, 4))
                        A = (i % 3) * 2
                        vector.scalar_tensor_tensor(out=wf0[i % 2][:, 0:512],
                                                    in0=pb[A][:, :], scalar=1.0,
                                                    in1=wf1[i % 2][:, 0:512],
                                                    op0=ALU.mult, op1=ALU.add)
                        sV.inc(vector.scalar_tensor_tensor(out=wf0[i % 2][:, 512:1024],
                                                           in0=pb[A + 1][:, :], scalar=1.0,
                                                           in1=wf1[i % 2][:, 512:1024],
                                                           op0=ALU.mult, op1=ALU.add), 2)
                        vector.wait_ge(sV.h, sV.n)
                        sV.inc(vector.tensor_reduce(out=stat[:, 0:1], in_=wf0[i % 2][:, :],
                                                    axis=AX, op=ALU.add))
                        vector.wait_ge(sS.h, so_n(i, 5))
                        sV.inc(vector.reciprocal(out=stat[:, 4:5], in_=stat[:, 3:4]))
                        vector.wait_ge(sV.h, sV.n)
                        sV.inc(vector.tensor_scalar(out=accF[:, 1024 * i:1024 * (i + 1)],
                                                    in0=wf0[i % 2][:, :],
                                                    scalar1=stat[:, 4:5], scalar2=None,
                                                    op0=ALU.mult))
                block.vector(pc_vector_op)

                def pc_sync_res(sync):
                    for i in range(NT):
                        if i >= 2:
                            sync.wait_ge(sV.h, vo_n(i - 2, 6))
                        sR[i % 2].inc(sync.dma_start(
                            out=wf1[i % 2][:, :],
                            in_=sp_x2[128 * i:128 * (i + 1), :]), 16)
                block.sync(pc_sync_res)

                def pc_tensor_op(tensor):
                    tensor.wait_ge(sS.h, SO1)
                    tensor.wait_ge(sV.h, ATTV)
                    for i in range(NT):
                        tensor.wait_ge(sV.h, vo_n(i, 4))
                        if i >= 1:
                            tensor.wait_ge(sS.h, so_n(i - 1, 7))
                        for j in range(NDP):
                            ins = tensor.transpose(out=ptb[:, 128 * j:128 * (j + 1)],
                                                   in_=yfull[:, 128 * j:128 * (j + 1)],
                                                   identity=idn[:])
                        sT.inc(ins)
                        tensor.wait_ge(sS.h, so_n(i, 1))
                        if i >= 3:
                            tensor.wait_ge(sV.h, vo_n(i - 3, 6))
                        A = (i % 3) * 2
                        for p in range(NDP):
                            for nc2 in range(2):
                                ins = tensor.matmul(out=pb[A + nc2][:, :],
                                                    lhsT=yTt[:, 128 * p:128 * (p + 1)],
                                                    rhs=wob[:, 1024 * p + 512 * nc2:
                                                            1024 * p + 512 * (nc2 + 1)],
                                                    start=(p == 0), stop=(p == NDP - 1))
                        sT.inc(ins)
                        tensor.wait_ge(sS.h, so_n(i, 6))
                        for j in range(4):
                            ins = tensor.transpose(out=ptb[:, 128 * j:128 * (j + 1)],
                                                   in_=xbft[:, 128 * j:128 * (j + 1)],
                                                   identity=idn[:])
                        sT.inc(ins)
                        for j in range(4, NDP):
                            ins = tensor.transpose(out=ptb[:, 128 * j:128 * (j + 1)],
                                                   in_=xbft[:, 128 * j:128 * (j + 1)],
                                                   identity=idn[:])
                        sT.inc(ins)
                block.tensor(pc_tensor_op)

                def pc_scalar_op(scalar):
                    for i in range(NT):
                        scalar.wait_ge(sT.h, to_n(i, 1))
                        sS.inc(scalar.copy(out=yTt[:, :], in_=ptb[:, :]))
                        scalar.wait_ge(sV.h, vo_n(i, 7))
                        sS.inc(scalar.mul(out=stat[:, 1:2], in_=stat[:, 0:1], mul=-1.0 / D))
                        scalar.wait_ge(sS.h, sS.n)
                        sS.inc(scalar.activation(out=wf0[i % 2][:, :], in_=wf0[i % 2][:, :],
                                                 func=ACT.Identity, bias=stat[:, 1:2],
                                                 scale=1.0))
                        scalar.wait_ge(sS.h, sS.n)
                        sS.inc(scalar.activation(out=dmy[:, :], in_=wf0[i % 2][:, :],
                                                 func=ACT.Square, accum_out=stat[:, 2:3]))
                        scalar.wait_ge(sS.h, sS.n)
                        sS.inc(scalar.activation(out=stat[:, 3:4], in_=stat[:, 2:3],
                                                 func=ACT.Sqrt, scale=1.0 / D,
                                                 bias=stat[:, 5:6]))
                        scalar.wait_ge(sV.h, vo_n(i, 9))
                        sS.inc(scalar.copy(out=xbft[:, :],
                                           in_=accF[:, 1024 * i:1024 * (i + 1)]))
                        scalar.wait_ge(sT.h, to_n(i, 4))
                        ov = XT.rearrange("q (p c) -> q p c", c=TOK)
                        sS.inc(scalar.copy(
                            out=ov[:, :, 128 * i:128 * (i + 1)],
                            in_=ptb.rearrange("q (p c) -> q p c", c=128)[:, :, :]))
                block.scalar(pc_scalar_op)
                PC_T, PC_V, PC_S = sT.n, sV.n, sS.n

            TAP3 = sO.n
            if TAPS:
                def pc_tap3(sync):
                    sync.wait_ge(sV.h, PC_V)
                    for i in range(NT):
                        sO.inc(sync.dma_start(
                            out=taps["tap_x3"][128 * i:128 * (i + 1), :],
                            in_=accF[:, 1024 * i:1024 * (i + 1)]), 16)
                block.sync(pc_tap3)
            TAP3E = sO.n

            if PHASES >= 4:
                pdE = pdC.enter_context
                ewA = pdE(nc.sbuf_tensor("ewA", [128, NDP * 1024], BF16))
                ewB = pdE(nc.sbuf_tensor("ewB", [128, NDP * 1024], BF16))
                ewC = pdE(nc.sbuf_tensor("ewC", [128, NDP * 1024], BF16))
                uT = pdE(nc.sbuf_tensor("uT", [128, NDP * 1024], BF16))
                silb = pdE(nc.sbuf_tensor("silb", [128, 512], BF16))
                lnsum = pdE(nc.sbuf_tensor("lnsum", [128, NT], F32))
                gwTb = pdE(nc.sbuf_tensor("gwTb", [128, 48], BF16))
                gbs = pdE(nc.sbuf_tensor("gbs", [128, 6], F32))
                iot6 = pdE(nc.sbuf_tensor("iot6", [128, 6], F32))
                geb = pdE(nc.sbuf_tensor("geb", [128, NT * 6], F32))
                gwf = pdE(nc.sbuf_tensor("gwf", [128, 48], F32))

                KD0 = sKC.n

                def pd_sync_k(sync):
                    sync.wait_ge(sS.h, PC_S)
                    sKC.inc(sync.dma_start(out=gbs[:], in_=gb_bc), 16)
                    sKC.inc(sync.dma_start(out=iot6[:], in_=iota6), 16)
                    sKC.inc(sync.dma_start(out=gwf[:], in_=gwT), 16)
                block.sync(pd_sync_k)
                KD1 = sKC.n

                def pd_scalar_k(scalar):
                    scalar.wait_ge(sKC.h, KD1)
                    sS.inc(scalar.copy(out=gwTb[:, :], in_=gwf[:, :]))
                block.scalar(pd_scalar_k)
                SD0 = sS.n

                TD0, VD0, SD1 = sT.n, sV.n, sS.n
                TPD, VPD, SPD = 1, 11, 1

                def td_n(i, k):
                    return TD0 + TPD * i + k

                def vd_n(i, k):
                    return VD0 + VPD * i + k

                def sd_n(i, k):
                    return SD1 + SPD * i + k

                def pd_tensor_g(tensor):
                    tensor.wait_ge(sS.h, SD0)
                    tensor.wait_ge(sV.h, PC_V)
                    for i in range(NT):
                        if i >= 1:
                            tensor.wait_ge(sV.h, vd_n(i - 1, 1))
                        for p in range(NDP):
                            ins = tensor.matmul(
                                out=psm[0:128, 0:6],
                                lhsT=XT[:, 2048 * p + 128 * i:2048 * p + 128 * (i + 1)],
                                rhs=gwTb[:, 6 * p:6 * (p + 1)],
                                start=(p == 0), stop=(p == NDP - 1),
                                skip_group_check=True)
                        sT.inc(ins)
                block.tensor(pd_tensor_g)

                def pd_vector_g(vector):
                    vector.wait_ge(sS.h, SD0)
                    for i in range(NT):
                        vector.wait_ge(sT.h, td_n(i, 1))
                        sV.inc(vector.scalar_tensor_tensor(
                            out=mparts[:, 0:6], in0=psm[0:128, 0:6], scalar=1.0,
                            in1=gbs[:, :], op0=ALU.mult, op1=ALU.add))
                        vector.wait_ge(sS.h, sd_n(i, 1))
                        sV.inc(vector.tensor_reduce(out=stat[:, 6:7], in_=mparts[:, 8:14],
                                                    axis=AX, op=ALU.max))
                        vector.wait_ge(sV.h, sV.n)
                        sV.inc(vector.tensor_scalar(out=mparts[:, 0:6], in0=mparts[:, 8:14],
                                                    scalar1=stat[:, 6:7], scalar2=None,
                                                    op0=ALU.is_ge))
                        vector.wait_ge(sV.h, sV.n)
                        sV.inc(vector.scalar_tensor_tensor(
                            out=mparts[:, 0:6], in0=mparts[:, 0:6], scalar=-1e30,
                            in1=mparts[:, 8:14], op0=ALU.mult, op1=ALU.add))
                        vector.wait_ge(sV.h, sV.n)
                        sV.inc(vector.tensor_reduce(out=stat[:, 7:8], in_=mparts[:, 0:6],
                                                    axis=AX, op=ALU.max))
                        vector.wait_ge(sV.h, sV.n)
                        sV.inc(vector.tensor_tensor(out=stat[:, 8:9], in0=stat[:, 6:7],
                                                    in1=stat[:, 7:8], op=ALU.add))
                        vector.wait_ge(sV.h, sV.n)
                        sV.inc(vector.tensor_scalar(out=stat[:, 8:9], in0=stat[:, 8:9],
                                                    scalar1=1e-8, scalar2=None, op0=ALU.add))
                        vector.wait_ge(sV.h, sV.n)
                        sV.inc(vector.reciprocal(out=stat[:, 9:10], in_=stat[:, 8:9]))
                        vector.wait_ge(sV.h, sV.n)
                        sV.inc(vector.tensor_scalar(out=mparts[:, 0:6], in0=mparts[:, 8:14],
                                                    scalar1=stat[:, 7:8], scalar2=None,
                                                    op0=ALU.is_ge))
                        vector.wait_ge(sV.h, sV.n)
                        sV.inc(vector.tensor_tensor(out=mparts[:, 0:6], in0=mparts[:, 0:6],
                                                    in1=mparts[:, 8:14], op=ALU.mult))
                        vector.wait_ge(sV.h, sV.n)
                        sV.inc(vector.tensor_scalar(out=geb[:, 6 * i:6 * (i + 1)],
                                                    in0=mparts[:, 0:6],
                                                    scalar1=stat[:, 9:10], scalar2=None,
                                                    op0=ALU.mult))
                        vector.wait_ge(sV.h, sV.n)
                block.vector(pd_vector_g)

                def pd_scalar_g(scalar):
                    for i in range(NT):
                        scalar.wait_ge(sV.h, vd_n(i, 1))
                        sS.inc(scalar.activation(out=mparts[:, 8:14], in_=mparts[:, 0:6],
                                                 func=ACT.Sigmoid))
                block.scalar(pd_scalar_g)
                GATE_V, GATE_S, GATE_T = sV.n, sS.n, sT.n

                def ew_src(e, which):
                    if e == 6:
                        t = (fc1T, fc2T, fc3T)[which]
                        return t.rearrange("(a p) d -> p a d", p=128)
                    t = (ew1T, ew3T, ew2T)[which]
                    return t[e].rearrange("(a p) d -> p a d", p=128)

                TPE, SPE, VPE = 80, 32, 64
                TE0, SE0, VE0 = sT.n, sS.n, sV.n

                def te_n(e, k):
                    return TE0 + TPE * e + k

                def se_n(e, k):
                    return SE0 + SPE * e + k

                def ve_n(e, k):
                    return VE0 + VPE * e + k

                ewbv2 = ewbf.rearrange("(m a p) d -> m p a d", p=128, a=8)

                def pd_gpsimd_w(gpsimd):
                    gpsimd.wait_ge(sS.h, SE0)
                    gpsimd.wait_ge(sV.h, GATE_V)
                    gpsimd.wait_ge(sCVs[0].h, CV_F[0])
                    gpsimd.wait_ge(sCVs[1].h, CV_F[1])
                    for e in range(7):
                        for j in range(6):
                            which, hf = j // 2, j % 2
                            if e >= 1:
                                gpsimd.wait_ge(sT.h,
                                               te_n(e - 1, 72 if j < 4 else 80))
                            dst = (ewA, ewA, ewB, ewB, ewC, ewC)[j]
                            sem = sGC if j >= 4 else sGAB
                            sem.inc(gpsimd.dma_start(
                                out=dst.rearrange("p (a d) -> p a d",
                                                  d=1024)[:, 4 * hf:4 * (hf + 1), :],
                                in_=ewbv2[3 * e + which][:, 4 * hf:4 * (hf + 1), :]),
                                16)
                block.gpsimd(pd_gpsimd_w)

                def pd_scalar_e(scalar):
                    for e in range(7):
                        for ip in range(16):
                            scalar.wait_ge(sT.h, te_n(e, 2 * ip + 1))
                            if ip >= 1:
                                scalar.wait_ge(sV.h, ve_n(e, ip))
                            sS.inc(scalar.activation(out=silb[:, :],
                                                     in_=pb[(2 * ip) % 4][:, :],
                                                     func=ACT.Silu))
                        for ip in range(16):
                            scalar.wait_ge(sT.h, te_n(e, 40 + 2 * ip + 1))
                            scalar.wait_ge(sV.h, ve_n(e, 32 + ip))
                            sS.inc(scalar.activation(out=silb[:, :],
                                                     in_=pb[(2 * ip) % 4][:, :],
                                                     func=ACT.Silu))
                block.scalar(pd_scalar_e)

                LV_R = {}

                def pd_vector_e(vector):
                    vector.wait_ge(sV.h, GATE_V)
                    if TAPS:
                        vector.wait_ge(sO.h, TAP3E)
                    for e in range(7):
                        for h2 in range(2):
                            for ip in range(16):
                                hs, tc2 = ip // 2, ip % 2
                                vector.wait_ge(sS.h, se_n(e, (1 if h2 == 0 else 17) + ip))
                                vector.wait_ge(sT.h, te_n(e, 40 * h2 + 2 * ip + 2))
                                sV.inc(vector.tensor_tensor(
                                    out=uT[:, 1024 * hs + 512 * tc2:1024 * hs + 512 * (tc2 + 1)],
                                    in0=silb[:, :], in1=pb[(2 * ip) % 4 + 1][:, :],
                                    op=ALU.mult))
                            for ti in range(8):
                                i = 8 * h2 + ti
                                vector.wait_ge(sT.h, te_n(e, 40 * h2 + 33 + ti))
                                for nc2 in range(2):
                                    a = accF[:, 1024 * i + 512 * nc2:1024 * i + 512 * (nc2 + 1)]
                                    if e == 6:
                                        sV.inc(vector.scalar_tensor_tensor(
                                            out=a, in0=pb[4 + nc2][:, :], scalar=1.0,
                                            in1=a, op0=ALU.mult, op1=ALU.add))
                                    else:
                                        sV.inc(vector.scalar_tensor_tensor(
                                            out=a, in0=pb[4 + nc2][:, :],
                                            scalar=geb[:, 6 * i + e:6 * i + e + 1],
                                            in1=a, op0=ALU.mult, op1=ALU.add))
                                if e == 6:
                                    vector.wait_ge(sV.h, sV.n)
                                    sLN.inc(vector.tensor_reduce(
                                        out=lnsum[:, i:i + 1],
                                        in_=accF[:, 1024 * i:1024 * (i + 1)],
                                        axis=AX, op=ALU.add))
                                    LV_R[i] = sLN.n
                block.vector(pd_vector_e)

                def pd_tensor_e(tensor):
                    tensor.wait_ge(sV.h, GATE_V)
                    tensor.wait_ge(sV.h, PC_V)
                    for e in range(7):
                        for h2 in range(2):
                            tb = 40 * h2
                            for ip in range(16):
                                hs, tc2 = ip // 2, ip % 2
                                bA = pb[(2 * ip) % 4]
                                bB = pb[(2 * ip) % 4 + 1]
                                if ip == 0:
                                    tensor.wait_ge(sGAB.h, 64 * (e + 1))
                                if ip >= 2:
                                    tensor.wait_ge(sV.h, ve_n(e, 32 * h2 + ip - 1))
                                elif h2 == 1:
                                    tensor.wait_ge(sV.h, ve_n(e, 16))
                                elif e >= 1:
                                    tensor.wait_ge(sV.h, ve_n(e - 1, VPE))
                                for p in range(NDP):
                                    ins = tensor.matmul(
                                        out=bA[:, :],
                                        lhsT=ewA[:, 1024 * p + 128 * hs:1024 * p + 128 * (hs + 1)],
                                        rhs=XT[:, 2048 * p + 1024 * h2 + 512 * tc2:
                                               2048 * p + 1024 * h2 + 512 * (tc2 + 1)],
                                        start=(p == 0), stop=(p == NDP - 1))
                                sT.inc(ins)
                                for p in range(NDP):
                                    ins = tensor.matmul(
                                        out=bB[:, :],
                                        lhsT=ewB[:, 1024 * p + 128 * hs:1024 * p + 128 * (hs + 1)],
                                        rhs=XT[:, 2048 * p + 1024 * h2 + 512 * tc2:
                                               2048 * p + 1024 * h2 + 512 * (tc2 + 1)],
                                        start=(p == 0), stop=(p == NDP - 1))
                                sT.inc(ins)
                            tensor.wait_ge(sGC.h, 32 * (e + 1))
                            tensor.wait_ge(sV.h, ve_n(e, 32 * h2 + 16))
                            for ti in range(8):
                                if ti >= 1:
                                    tensor.wait_ge(sV.h, ve_n(e, 32 * h2 + 16 + 2 * ti))
                                elif h2 == 1:
                                    tensor.wait_ge(sV.h, ve_n(e, 48))
                                elif e >= 1:
                                    tensor.wait_ge(sV.h, ve_n(e - 1, VPE))
                                for hs in range(8):
                                    for nc2 in range(2):
                                        ins = tensor.matmul(
                                            out=pb[4 + nc2][:, :],
                                            lhsT=uT[:, 1024 * hs + 128 * ti:1024 * hs + 128 * (ti + 1)],
                                            rhs=ewC[:, 1024 * hs + 512 * nc2:1024 * hs + 512 * (nc2 + 1)],
                                            start=(hs == 0), stop=(hs == NDP - 1))
                                sT.inc(ins)
                block.tensor(pd_tensor_e)
                E_V, E_S, E_T = sV.n, sS.n, sT.n

                LT0, LV0, LS0 = sT.n, sV.n, sS.n
                VPL, SPL = 3, 4
                P_B = [sP[0].n, sP[1].n]

                def lv_n(i, k):
                    return LV0 + VPL * i + k

                def ls_n(i, k):
                    return LS0 + SPL * i + k

                LV_M = {}

                def _ln_cols(i):
                    o = 6 * (i % 2)
                    return o, o + 1, o + 2, o + 3, o + 4

                def pd_vector_ln(vector):
                    for i in range(NT):
                        a = accF[:, 1024 * i:1024 * (i + 1)]
                        _, _, _, c3, c4 = _ln_cols(i)
                        if i >= 2:
                            vector.wait_ge(sP[i % 2].h, P_B[i % 2] + 16 * (i // 2))
                        vector.wait_ge(sS.h, ls_n(i, 4))
                        sV.inc(vector.reciprocal(out=stat[:, c4:c4 + 1],
                                                 in_=stat[:, c3:c3 + 1]))
                        vector.wait_ge(sV.h, sV.n)
                        sV.inc(vector.tensor_scalar(out=wf0[i % 2][:, :], in0=a,
                                                    scalar1=stat[:, c4:c4 + 1],
                                                    scalar2=None, op0=ALU.mult))
                        LV_M[i] = sV.n
                block.vector(pd_vector_ln)

                def pd_scalar_ln(scalar):
                    for i in range(NT):
                        a = accF[:, 1024 * i:1024 * (i + 1)]
                        _, c1, c2, c3, _ = _ln_cols(i)
                        scalar.wait_ge(sLN.h, LV_R[i])
                        if i >= 2:
                            scalar.wait_ge(sV.h, LV_M[i - 2])
                        sS.inc(scalar.mul(out=stat[:, c1:c1 + 1],
                                          in_=lnsum[:, i:i + 1], mul=-1.0 / D))
                        scalar.wait_ge(sS.h, sS.n)
                        sS.inc(scalar.activation(out=a, in_=a, func=ACT.Identity,
                                                 bias=stat[:, c1:c1 + 1], scale=1.0))
                        scalar.wait_ge(sS.h, sS.n)
                        sS.inc(scalar.activation(out=dmy[:, :], in_=a, func=ACT.Square,
                                                 accum_out=stat[:, c2:c2 + 1]))
                        scalar.wait_ge(sS.h, sS.n)
                        sS.inc(scalar.activation(out=stat[:, c3:c3 + 1],
                                                 in_=stat[:, c2:c2 + 1],
                                                 func=ACT.Sqrt, scale=1.0 / D,
                                                 bias=stat[:, 5:6]))
                block.scalar(pd_scalar_ln)

                def pd_sync_out(sync):
                    for i in range(NT):
                        sync.wait_ge(sV.h, LV_M[i])
                        sP[i % 2].inc(sync.dma_start(
                            out=y_out[128 * i:128 * (i + 1), :],
                            in_=wf0[i % 2][:, :]), 16)
                    sync.wait_ge(sP[0].h, sP[0].n)
                    sync.wait_ge(sP[1].h, sP[1].n)
                block.sync(pd_sync_out)



        FIN_V, FIN_S, FIN_T = sV.n, sS.n, sT.n

        src_sp = sp_x2 if PHASES >= 2 else sp_x1

        def tmp_out(sync):
            if PHASES >= 4:
                return
            sync.wait_ge(sP[0].h, sP[0].n)
            sync.wait_ge(sP[1].h, sP[1].n)
            sync.wait_ge(sS.h, FIN_S)
            for t in range(NT):
                sO.inc(sync.dma_start(out=y_out[128 * t:128 * (t + 1), :],
                                      in_=src_sp[128 * t:128 * (t + 1), :]), 16)
            sync.wait_ge(sO.h, sO.n)
        block.sync(tmp_out)

        if TAPS:
            def tap_sync(sync):
                sync.wait_ge(sP[0].h, PB_P0)
                sync.wait_ge(sP[1].h, PB_P1)
                for t in range(NT):
                    sO.inc(sync.dma_start(out=taps["tap_x1"][128 * t:128 * (t + 1), :],
                                          in_=sp_x1[128 * t:128 * (t + 1), :]), 16)
                sO.inc(sync.dma_start(out=taps["tap_dbc"], in_=dbcT_hbm[:, :]), 16)
                sO.inc(sync.dma_start(out=taps["tap_yc"], in_=ycols[:, :]), 16)
                sync.wait_ge(sO.h, sO.n)
            block.sync(tap_sync)

    return nc


def _prep(inputs):
    x = np.ascontiguousarray(np.asarray(inputs["L0_attn_x"], dtype=np.float32))
    W_in = np.asarray(inputs["L0_attn_W_in"], dtype=np.float32)
    b_in = np.asarray(inputs["L0_attn_b_in"], dtype=np.float32)

    base = {
        "wz": np.ascontiguousarray(W_in[:, D:2 * D]),
        "wdbc": np.ascontiguousarray(W_in[:, 2 * D:2 * D + 48]),
        "bz_bc": _bcast(b_in[D:2 * D]),
        "bdbc_bc": _bcast(b_in[2 * D:2 * D + 48]),
        "alog": np.ascontiguousarray(np.asarray(inputs["L0_attn_A_log"],
                                                dtype=np.float32).reshape(16, 1)),
        "wout": np.ascontiguousarray(np.asarray(inputs["L0_attn_W_out"], dtype=np.float32)),
        "bout_bc": _bcast(np.asarray(inputs["L0_attn_b_out"], dtype=np.float32)),
        "w1T": np.ascontiguousarray(np.asarray(inputs["L0_ffn_W1"], dtype=np.float32).T),
        "b1c": np.ascontiguousarray(np.asarray(inputs["L0_ffn_b1"],
                                               dtype=np.float32).reshape(HF // 128, 128).T),
        "w2T": np.ascontiguousarray(np.asarray(inputs["L0_ffn_W2"], dtype=np.float32).T),
        "b2_bc": _bcast(np.asarray(inputs["L0_ffn_b2"], dtype=np.float32)),
        "scale_bc": _bcast(np.asarray(inputs["L0_ffn_scale"],
                                      dtype=np.float32).reshape(1)).reshape(128, 1).copy(),
        "wo": np.ascontiguousarray(np.asarray(inputs["L1_attn_W_o"], dtype=np.float32)),
        "gwT": np.ascontiguousarray(np.asarray(inputs["L1_ffn_gate_w"],
                                               dtype=np.float32).T.reshape(8, 128, 6)
                                    .transpose(1, 0, 2).reshape(128, 48)),
        "gb_bc": _bcast(np.asarray(inputs["L1_ffn_gate_b"], dtype=np.float32)),
        "ew1T": np.ascontiguousarray(np.asarray(inputs["L1_ffn_ew1"],
                                                dtype=np.float32).transpose(0, 2, 1)),
        "ew2T": np.ascontiguousarray(np.asarray(inputs["L1_ffn_ew2"],
                                                dtype=np.float32).transpose(0, 2, 1)),
        "ew3T": np.ascontiguousarray(np.asarray(inputs["L1_ffn_ew3"],
                                                dtype=np.float32).transpose(0, 2, 1)),
        "fc1T": np.ascontiguousarray(np.asarray(inputs["L1_ffn_fc1"], dtype=np.float32).T),
        "fc2T": np.ascontiguousarray(np.asarray(inputs["L1_ffn_fc2"], dtype=np.float32).T),
        "fc3T": np.ascontiguousarray(np.asarray(inputs["L1_ffn_fc3"], dtype=np.float32).T),
        "ident": np.eye(128, dtype=BF),
        "identf": np.eye(128, dtype=np.float32),
        "mask1": np.where(np.arange(128)[None, :] <= np.arange(128)[:, None],
                          0.0, NEG).astype(np.float32),
        "mask2": np.concatenate([
            np.where(np.arange(128)[:, None] <= np.arange(128)[None, :],
                     0.0, NEG).astype(np.float32),
            np.zeros((128, 384), dtype=np.float32)], axis=1),
        "iota6": np.ascontiguousarray(np.broadcast_to(
            np.arange(6, dtype=np.float32), (128, 6))),
    }
    half = 32
    theta = 1.0 / (10000.0 ** (np.arange(half, dtype=np.float64) / half))
    tpos = np.arange(S, dtype=np.float64)
    ang = tpos[:, None] * theta[None, :]
    cosv = np.cos(ang).astype(np.float32).astype(BF)
    sinv = np.sin(ang).astype(np.float32).astype(BF)
    base["cos_t"] = np.ascontiguousarray(
        cosv.reshape(NTF, 128, half).transpose(1, 0, 2).reshape(128, NTF * half))
    base["sin_t"] = np.ascontiguousarray(
        sinv.reshape(NTF, 128, half).transpose(1, 0, 2).reshape(128, NTF * half))

    W_qkv = np.asarray(inputs["L1_attn_W_qkv"], dtype=np.float32)
    in_maps = []
    for c in range(8):
        b, h = c // 2, c % 2
        m = dict(base)
        m["x_own"] = np.ascontiguousarray(x[b, h * TOK:(h + 1) * TOK])
        m["xT_full"] = np.ascontiguousarray(x[b].T)
        m["xT_own"] = np.ascontiguousarray(x[b, h * TOK:(h + 1) * TOK].T)
        m["x_t0"] = np.ascontiguousarray(x[b, 0:128])
        m["wqkv"] = np.ascontiguousarray(np.concatenate(
            [W_qkv[:, 512 * h:512 * (h + 1)],
             W_qkv[:, D + 512 * h:D + 512 * (h + 1)],
             W_qkv[:, 2 * D + 512 * h:2 * D + 512 * (h + 1)]], axis=1))
        m["own_mask"] = np.full((128, 1), 1.0 - h, dtype=np.float32)
        in_maps.append(m)
    return in_maps


def _bcast(v):
    v = np.asarray(v, dtype=np.float32).reshape(-1)
    return np.ascontiguousarray(np.broadcast_to(v[None, :], (128, v.size)))


def kernel(**inputs):
    in_maps = _prep(inputs)
    nc = build()
    res = run_bass_kernel_spmd(nc, in_maps, core_ids=list(range(8)))
    out = np.empty((4, S, D), dtype=np.float32)
    for c in range(8):
        b, h = c // 2, c % 2
        out[b, h * TOK:(h + 1) * TOK] = res.results[c]["y_out"]
    kernel.last_results = res
    return out
